# version 38
# baseline (speedup 1.0000x reference)
import numpy as np
import concourse.bass as bass
import concourse.mybir as mybir
from concourse.bass_utils import run_bass_kernel_spmd

F32 = mybir.dt.float32
BF16 = mybir.dt.bfloat16
I32 = mybir.dt.int32
AF = mybir.ActivationFunctionType
ALU = mybir.AluOpType
AX = mybir.AxisListType
ts = bass.ts

MAGIC = 12582912.0
LN_EPS = 1e-5
P = 128


def build_program(cfg):
    T, D, N, INNER = cfg["T"], cfg["D"], cfg["N"], cfg["INNER"]
    TCH = cfg["TCH"]
    DBG = cfg.get("debug", False)
    DBLK = D // P
    NSUP = T // TCH
    NT = TCH // P
    MINE = TCH // 2
    NMT = MINE // P
    IBLK = INNER // P
    WQC = min(INNER, 1024)

    nc = bass.Bass("TRN2", target_bir_lowering=False)

    def din(name, shape, dtype=F32):
        return nc.dram_tensor(name, shape, dtype, kind="ExternalInput").ap()

    x_h = din("x", [T, D])
    xm_h = din("xm", [T // 2, D])
    parf_h = din("parf", [1, 1])
    wdtT_h = din("wdtT", [D, D])
    wbT_h = din("wbT", [D, N])
    wcT_h = din("wcT", [D, N])
    gam_h = din("gam_pp", [P, DBLK])
    bet_h = din("bet_pp", [P, DBLK])
    bdt_h = din("bdt_row", [1, D])
    apar_h = din("a_pp", [P, DBLK * N])
    dpar_h = din("dparam_row", [1, D])
    wgT_h = din("wgT", [INNER // P * D, P])
    wuT_h = din("wuT", [INNER // P * D, P])
    wdT_h = din("wdT", [D // P * INNER, P])

    out_h = nc.dram_tensor("out", [NSUP * MINE, D], F32, kind="ExternalOutput").ap()

    def dsc(name, shape, dtype):
        return nc.dram_tensor(name, shape, dtype, kind="Internal").ap()

    bT_d = dsc("bT_d", [N, TCH], BF16)
    cT_d = dsc("cT_d", [N, TCH], BF16)
    wgq_d = dsc("wgq_d", [INNER // P * D, P], BF16)
    wuq_d = dsc("wuq_d", [INNER // P * D, P], BF16)
    wdq_d = dsc("wdq_d", [D // P * INNER, P], BF16)
    brow_d = dsc("brow_d", [1, D + 2 * N], F32)
    sq_d = dsc("sq_d", [2, MINE], BF16)
    wqs_d = dsc("wqs_d", [3, 2], F32)

    dbg = {}
    if DBG:
        for nm, shape in [("d_znT", [D, TCH]), ("d_dtT", [D, TCH]),
                          ("d_bT", [N, TCH]), ("d_cT", [N, TCH]),
                          ("d_wT", [D, TCH]), ("d_y", [TCH, D]),
                          ("d_u", [MINE, D]), ("d_z2q", [D, MINE]),
                          ("d_q", [INNER, MINE]), ("d_wgq", [D, INNER]),
                          ("d_f", [MINE, D]),
                          ("d_h", [P, TCH])]:
            dbg[nm] = nc.dram_tensor(nm, shape, F32, kind="ExternalOutput").ap()

    ctxs = []

    def sb(name, shape, dtype):
        cm = nc.sbuf_tensor(name, shape, dtype)
        t = cm.__enter__()
        ctxs.append(cm)
        return t

    def psb(name, shape, dtype=F32):
        cm = nc.psum_tensor(name, shape, dtype)
        t = cm.__enter__()
        ctxs.append(cm)
        return t

    ident = sb("ident", [P, P], BF16)
    onescol = sb("onescol", [P, 1], BF16)
    onesrow = sb("onesrow", [1, P], BF16)
    sqstg = sb("sqstg", [P, 8], BF16)
    sel_all = sb("sel_all", [N, N * P], BF16)
    bcT_all = sb("bcT_all", [N, 2 * TCH], BF16)
    sqTrow = sb("sqTrow", [1, 8 * P], BF16)
    gam = sb("gam", [P, DBLK], F32)
    bet = sb("bet", [P, DBLK], F32)
    apar = sb("apar", [P, DBLK * N], F32)
    drep = sb("drep", [P, D], BF16)
    bdtc = sb("bdtc", [P, DBLK], F32)
    bbc = sb("bbc", [N, 2], F32)
    parc = sb("parc", [P, 2], F32)
    consts = sb("consts", [P, 14], F32)
    wdtg = sb("wdtg", [P, DBLK * D], BF16)
    wbg = sb("wbg", [P, DBLK * N], BF16)
    wcg = sb("wcg", [P, DBLK * N], BF16)
    stats = sb("stats", [P, NT * 2], F32)
    sc1 = sb("sc1", [P, NT], F32)
    r1v = sb("r1v", [P, NT], F32)
    mnM2 = [sb("mnM0", [P, NMT], F32), sb("mnM1", [P, NMT], F32)]
    r1M2 = [sb("r1M0", [P, NMT], F32), sb("r1M1", [P, NMT], F32)]
    sc2 = sb("sc2", [P, NT], F32)
    statw = sb("statw", [P, 16], F32)
    statw2 = sb("statw2", [P, 16], F32)
    ln2s = sb("ln2s", [P, 2 * NMT], F32)
    ln3s = sb("ln3s", [P, 2 * NMT], F32)
    ln4s = sb("ln4s", [P, 2], F32)
    sxv = sb("sxv", [P, NMT], F32)
    sqv = sb("sqv", [P, NMT], F32)
    sqvr = sb("sqvr", [P, NMT], F32)
    accs = sb("accs", [P, DBLK * TCH], BF16)
    state = sb("state", [P, DBLK * N], F32)
    xts = sb("xts", [P, NT * D], F32)
    xt2 = [xts[:, 0:D], xts[:, D:2 * D]]
    xt0 = xt2[0]
    znt2 = [sb("znt", [P, D], BF16), sb("zntb", [P, D], BF16)]
    znt = znt2[0]
    bns = sb("bns", [P, 16], F32)

    a_b1 = sb("a_b1", [P, 2 * DBLK * TCH + IBLK * MINE], BF16)
    dtT = a_b1[:, 0:DBLK * TCH]
    wpT = a_b1[:, DBLK * TCH:2 * DBLK * TCH]
    qs = a_b1[:, 2 * DBLK * TCH:2 * DBLK * TCH + IBLK * MINE]
    a_b2 = sb("a_b2", [P, DBLK * TCH + 2 * NMT * D + MINE + DBLK * MINE], BF16)
    znT = a_b2[:, 0:DBLK * TCH]
    _o2 = DBLK * TCH
    yam = a_b2[:, _o2:_o2 + NMT * D]
    z1m = a_b2[:, _o2 + NMT * D:_o2 + 2 * NMT * D]
    ytd = a_b2[:, _o2 + 2 * NMT * D:_o2 + 2 * NMT * D + MINE]
    z2qT = a_b2[:, _o2 + 2 * NMT * D + MINE:_o2 + 2 * NMT * D + MINE + DBLK * MINE]
    SCRW = max(4 * TCH, 2 * D + 2 * N)
    FCOLS = max(DBLK * TCH, SCRW, 2 * NMT * D + 3 * D + 1024, NMT * D + 6 * WQC)
    a_f = sb("a_f", [P, FCOLS], F32)
    scr = a_f[:, 0:SCRW]

    wsc = a_f[:, NMT * D:NMT * D + WQC]
    um = a_f[:, 0:NMT * D]
    fTM = a_f[:, NMT * D:2 * NMT * D]
    fscr = a_f[:, 2 * NMT * D:2 * NMT * D + D]
    zscr = a_f[:, 2 * NMT * D + D:2 * NMT * D + 2 * D]
    oscr = a_f[:, 2 * NMT * D + 2 * D:2 * NMT * D + 3 * D]
    _bs = 2 * NMT * D + 3 * D
    bscr2 = [a_f[:, _bs:_bs + 512], a_f[:, _bs + 512:_bs + 1024]]
    a_b3 = sb("a_b3", [P, 21 * TCH + 4 * DBLK * P + 33 * MINE + 2 * IBLK * P + NMT * D], BF16)
    brep2 = [a_b3[:, j * TCH:(j + 1) * TCH] for j in range(5)]
    crep2 = [a_b3[:, (5 + j) * TCH:(6 + j) * TCH] for j in range(5)]
    abar2 = [a_b3[:, (10 + j) * TCH:(11 + j) * TCH] for j in range(5)]
    bxt2 = [a_b3[:, 15 * TCH:16 * TCH], a_b3[:, 16 * TCH:17 * TCH]]
    ht2 = [a_b3[:, 17 * TCH:18 * TCH], a_b3[:, 18 * TCH:19 * TCH]]
    cht2 = [a_b3[:, 19 * TCH:20 * TCH], a_b3[:, 20 * TCH:21 * TCH]]
    betac = sb("betac", [P, DBLK], BF16)
    _fb = 21 * TCH
    fw2 = [a_b3[:, _fb:_fb + DBLK * P], a_b3[:, _fb + DBLK * P:_fb + 2 * DBLK * P]]
    fu2 = [a_b3[:, _fb + 2 * DBLK * P:_fb + 3 * DBLK * P],
           a_b3[:, _fb + 3 * DBLK * P:_fb + 4 * DBLK * P]]
    _o = _fb + 4 * DBLK * P
    gt2 = [a_b3[:, _o + j * MINE:_o + (j + 1) * MINE] for j in range(4)]
    gt4 = a_b3[:, _o:_o + 4 * MINE]
    pt2 = [a_b3[:, _o + (4 + j) * MINE:_o + (5 + j) * MINE] for j in range(4)]
    pt4 = a_b3[:, _o + 4 * MINE:_o + 8 * MINE]
    mrow4 = a_b3[:, _o + 8 * MINE:_o + 12 * MINE]
    mrow24 = a_b3[:, _o + 12 * MINE:_o + 16 * MINE]
    mrow = a_b3[:, _o + 16 * MINE:_o + 17 * MINE]
    sqrep4 = a_b3[:, _o + 17 * MINE:_o + 21 * MINE]
    sqrrep4 = a_b3[:, _o + 21 * MINE:_o + 25 * MINE]
    qsc4 = a_b3[:, _o + 25 * MINE:_o + 29 * MINE]
    fcm2 = [a_b3[:, _o + (29 + j) * MINE:_o + (30 + j) * MINE] for j in range(4)]
    fc4 = a_b3[:, _o + 29 * MINE:_o + 33 * MINE]
    _w = _o + 33 * MINE
    wdsl2 = [a_b3[:, _w:_w + IBLK * P], a_b3[:, _w + IBLK * P:_w + 2 * IBLK * P]]
    zqb = a_b3[:, _w + 2 * IBLK * P:_w + 2 * IBLK * P + NMT * D]
    AB3 = _w + 2 * IBLK * P + NMT * D
    
    pm0 = psb("pm0", [P, 512])
    pm1 = psb("pm1", [P, 512])
    pm2 = psb("pm2", [P, 512])
    pm3 = psb("pm3", [P, 512])
    ptrp = psb("ptrp", [P, 2 * P], BF16)
    ptr = ptrp[:, 0:P]
    ptr2 = ptrp[:, P:2 * P]
    ptrw = [psb("ptrw0", [P, DBLK * P], BF16), psb("ptrw1", [P, DBLK * P], BF16)]
    pmq = psb("pmq", [P, 512])

    class Sem:
        def __init__(self, name):
            self.h = nc.alloc_semaphore(name=name)
            self.n = 0

        def inc(self, instr, k=1):
            instr.then_inc(self.h, k)
            self.n += k
            return instr

    s_ld, s_g, s_v, s_s, s_t, s_st, s_bar, s_ge = (Sem(x) for x in
        ("ld", "g", "v", "s", "t", "st", "bar", "ge"))
    s_rp = [Sem("rp0"), Sem("rp1"), Sem("rp2")]
    s_x = [Sem("x0"), Sem("x1"), Sem("x2")]
    s_fw = [Sem("fw0"), Sem("fw1")]
    s_dw = [Sem("dw0"), Sem("dw1")]
    s_gx = [Sem(f"gx{i}") for i in range(6)]
    s_gq3 = [Sem(f"gq{i}") for i in range(3)]
    s_gl3 = [Sem(f"gl{i}") for i in range(3)]

    V, S, TE, G, SY = nc.vector, nc.scalar, nc.tensor, nc.gpsimd, nc.sync

    def vf(instr=None):
        if instr is None:
            instr = V.engine_nop()
        s_v.inc(instr)
        V.wait_ge(s_v.h, s_v.n)
        return instr

    def sf(instr=None):
        if instr is None:
            instr = S.copy(statw[0:1, 13:14], consts[0:1, 7:8])
        s_s.inc(instr)
        S.wait_ge(s_s.h, s_s.n)
        return instr
    ALL_E = ("sync", "vector", "scalar", "tensor", "gpsimd")

    def barrier():
        SY.wait_ge(s_ld.h, s_ld.n)
        SY.wait_ge(s_st.h, s_st.n)
        G.wait_ge(s_g.h, s_g.n)
        G.wait_ge(s_ge.h, s_ge.n)
        for e in ALL_E:
            eng = getattr(nc, e)
            if e != "sync":
                eng.drain()
            eng.sem_inc(s_bar.h, 1)
        s_bar.n += len(ALL_E)
        for e in ALL_E:
            getattr(nc, e).wait_ge(s_bar.h, s_bar.n)

    NG = (D + 511) // 512

    def ln_stats(dst2, src):
        V.wait_ge(s_v.h, s_v.n)
        for gi in range(NG):
            lo, hi = gi * 512, min(D, (gi + 1) * 512)
            V.bn_stats(bns[:, 6 * gi:6 * gi + 6], src[:, lo:hi])
            vf()
        return V.bn_aggr(dst2, bns[:, 0:6 * NG])

    def rsqrt_cols(src_ap, dst_ap, tmp_ap):
        S.activation(tmp_ap, src_ap, AF.Ln, bias=LN_EPS)
        sf()
        return S.activation(dst_ap, tmp_ap, AF.Exp, scale=-0.5)

    ptrw_g = [0, 0]
    ptrw_gs = [0, 0]
    gp_v = [0, 0, 0, 0]
    q4v = [0]
    fcm_t = [0, 0, 0, 0]
    pm_g = {i: {"s": 0, "v": 0} for i in range(4)}
    xg = {"a_scal": 0, "zn_s": [0, 0], "znT_rd_t": 0, "bc_st": 0,
          "rp_fin": [0, 0], "acc_rd_t": 0, "yam_rd_s": 0, "um_rd_v": 0,
          "z2q_v": 0, "qs_v": 0, "o_st": [0, 0], "zn_v": 0,
          "pmq_s": 0, "sqT_t": 0, "bct_s": 0, "rep_t": 0, "zn_s2": 0,
          "z2q_s": 0, "ftm_s": 0}

    def pm_wait(bank):
        TE.wait_ge(s_s.h, pm_g[bank]["s"])
        TE.wait_ge(s_v.h, pm_g[bank]["v"])

    def finish():
        SY.wait_ge(s_ld.h, s_ld.n)
        SY.wait_ge(s_st.h, s_st.n)
        for cm in reversed(ctxs):
            cm.__exit__(None, None, None)
        return nc

    STOP = cfg.get("stop")

    V.memset(consts[:, 6:7], -0.5)
    vf()
    V.memset(consts[:, 7:8], 1.0)
    vf()
    V.memset(consts[:, 8:9], 0.5)
    vf()
    V.memset(consts[:, 9:10], 0.0)
    vf()
    V.memset(consts[:, 10:11], LN_EPS)
    vf()
    s_v.inc(V.memset(consts[:, 11:12], -0.5))
    for val, col in ((-0.5, 6), (1.0, 7), (0.5, 8), (0.0, 9), (LN_EPS, 10)):
        nc.const_aps.register(F32, val, consts[:, col:col + 1])
    s_ld.inc(SY.dma_start(gam[:], gam_h[:]), 16)
    s_ld.inc(SY.dma_start(bet[:], bet_h[:]), 16)
    s_ld.inc(SY.dma_start(apar[:], apar_h[:]), 16)
    s_g.inc(G.dma_start(drep[:], dpar_h.partition_broadcast(P)), 16)
    s_ld.inc(SY.dma_start(parc[:, 0:1], parf_h.partition_broadcast(P)), 16)
    s_ld.inc(SY.dma_start(scr[0:1, D + 2 * N:2 * D + 2 * N], bdt_h[0:1, 0:D]), 16)
    s_g.inc(G.dma_start(wdtg.ap().rearrange("p (c o) -> p c o", c=DBLK),
                        wdtT_h.rearrange("(c p) o -> p c o", p=P)), 16)
    s_g.inc(G.dma_start(wbg.ap().rearrange("p (c n) -> p c n", c=DBLK),
                        wbT_h.rearrange("(c p) n -> p c n", p=P)), 16)
    s_g.inc(G.dma_start(wcg.ap().rearrange("p (c n) -> p c n", c=DBLK),
                        wcT_h.rearrange("(c p) n -> p c n", p=P)), 16)

    G.wait_ge(s_g.h, s_g.n)
    s_ge.inc(G.memset(onesrow[:], 1.0))
    s_ge.inc(G.memset(sel_all[:], 0.0))
    G.wait_ge(s_ge.h, s_ge.n)
    for n_ in range(N):
        s_ge.inc(G.affine_select(out=sel_all[0:N, n_ * P:(n_ + 1) * P],
                                 in_=sel_all[0:N, n_ * P:(n_ + 1) * P],
                                 compare_op=ALU.not_equal, fill=1.0, base=-n_,
                                 pattern=[[0, P]], channel_multiplier=1))
    s_ge.inc(G.memset(ident[:], 0.0))
    G.wait_ge(s_ge.h, s_ge.n)
    s_ge.inc(G.affine_select(out=ident[:], in_=ident[:], compare_op=ALU.not_equal,
                             fill=1.0, base=0, pattern=[[-1, P]], channel_multiplier=1))

    S.wait_ge(s_ld.h, s_ld.n)
    S.wait_ge(s_v.h, s_v.n)
    S.activation(apar[:], apar[:], AF.Exp)
    sf()
    s_s.inc(S.mul(apar[:], apar[:], -1.0))

    V.wait_ge(s_ld.h, s_ld.n)
    V.wait_ge(s_g.h, s_g.n)
    V.tensor_scalar(parc[:, 1:2], parc[:, 0:1], -1.0, 1.0, ALU.mult, ALU.add)
    vf()
    V.tensor_copy(onescol[:], consts[:, 7:8])
    vf()
    s_v.inc(V.tensor_copy(betac[:, 0:DBLK], bet[:]))
    TE.wait_ge(s_v.h, s_v.n)
    TE.wait_ge(s_g.h, s_g.n)

    for oc in range(0, D, 512):
        w = min(512, D - oc)
        for c in range(DBLK):
            mm = TE.matmul(pm0[0:1, 0:w], betac[:, c:c + 1],
                           wdtg[:, c * D + oc:c * D + oc + w],
                           start=(c == 0), stop=(c == DBLK - 1))
            if c == DBLK - 1:
                s_t.inc(mm)
        V.wait_ge(s_t.h, s_t.n)
        s_v.inc(V.tensor_copy(scr[0:1, oc:oc + w], pm0[0:1, 0:w]))
        TE.wait_ge(s_v.h, s_v.n)
        pm_g[0]["v"] = s_v.n
    for wi, wsrc in enumerate((wbg, wcg)):
        for c in range(DBLK):
            mm = TE.matmul(pm1[0:1, 0:N], betac[:, c:c + 1], wsrc[:, c * N:(c + 1) * N],
                           start=(c == 0), stop=(c == DBLK - 1))
            if c == DBLK - 1:
                s_t.inc(mm)
        V.wait_ge(s_t.h, s_t.n)
        s_v.inc(V.tensor_copy(scr[0:1, D + wi * N:D + (wi + 1) * N], pm1[0:1, 0:N]))
        TE.wait_ge(s_v.h, s_v.n)
        pm_g[1]["v"] = s_v.n
    V.wait_ge(s_ld.h, s_ld.n)
    s_v.inc(V.tensor_tensor(scr[0:1, 0:D], scr[0:1, 0:D], scr[0:1, D + 2 * N:2 * D + 2 * N], ALU.add))
    SY.wait_ge(s_v.h, s_v.n)
    s_st.inc(SY.dma_start(brow_d[0:1, :], scr[0:1, 0:D + 2 * N]), 16)
    SY.wait_ge(s_st.h, s_st.n)
    s_ld.inc(SY.dma_start(bdtc.ap().rearrange("p (one c) -> p one c", one=1),
                          brow_d[0:1, 0:D].rearrange("one (c p) -> p one c", p=P), allow_slow_non_contiguous=True), 16)
    s_ld.inc(SY.dma_start(bbc[0:N, 0:2].rearrange("n (one two) -> n one two", one=1),
                          brow_d[0:1, D:D + 2 * N].rearrange("one (two n) -> n one two", n=N), allow_slow_non_contiguous=True), 16)

    V.wait_ge(s_s.h, s_s.n)
    V.wait_ge(s_ld.h, s_ld.n)
    for c in range(DBLK):
        V.tensor_scalar(wdtg[:, c * D:(c + 1) * D], wdtg[:, c * D:(c + 1) * D],
                        gam[:, c:c + 1], None, ALU.mult)
        V.tensor_scalar(wbg[:, c * N:(c + 1) * N], wbg[:, c * N:(c + 1) * N],
                        gam[:, c:c + 1], None, ALU.mult)
        V.tensor_scalar(wcg[:, c * N:(c + 1) * N], wcg[:, c * N:(c + 1) * N],
                        gam[:, c:c + 1], None, ALU.mult)
    barrier()


    if STOP == "prologue":
        return finish()
    wsc2 = [a_f[:, NMT * D + i * WQC:NMT * D + (i + 1) * WQC] for i in range(6)]
    wqa = [a_b3[:, i * WQC:(i + 1) * WQC] for i in range(6)]
    wqb = [a_b3[:, (6 + i) * WQC:(7 + i) * WQC] for i in range(6)]


    def gen_quant(w_h, wq_d, nrows, slot):
        CH_ROWS = WQC
        chunks = list(range(0, nrows, CH_ROWS))
        bufs = (2 * slot, 2 * slot + 1)
        ldsem = (s_gx[2 * slot], s_gx[2 * slot + 1])
        stsem = (s_fw[0], s_fw[1], s_dw[0])[slot]
        pmx = (pm0, pm1, pm2)[slot]
        sw = 4 * slot
        nmm = 0
        for r0 in chunks:
            rr = min(CH_ROWS, nrows - r0)
            nmm += (rr + 511) // 512
        k = 0
        s_done_abs = [0, 0]
        t_done_abs = [0, 0]
        ld_w = [0, 0]
        for ci, r0 in enumerate(chunks):
            b = ci % 2
            p = bufs[b]
            rr = min(CH_ROWS, nrows - r0)
            w = rr
            G.wait_ge(s_s.h, s_done_abs[b])
            ldsem[b].inc(G.dma_start(wsc2[p][:, 0:w].rearrange("p (a q) -> p a q", q=P),
                                     w_h[r0:r0 + rr, :].rearrange("(p a) q -> p a q", p=P)), 16)
            ld_w[b] = ldsem[b].n
            S.wait_ge(ldsem[b].h, ld_w[b])
            S.wait_ge(s_t.h, t_done_abs[b])
            s_s.inc(S.activation(wqa[p][:, 0:w], wsc2[p][:, 0:w], AF.Abs))
            s_done_abs[b] = s_s.n
            TE.wait_ge(s_s.h, s_done_abs[b])
            if ci == 0:
                pm_wait(slot)
            noff = (w + 511) // 512
            for oi, off in enumerate(range(0, w, 512)):
                ww = min(512, w - off)
                mm = TE.matmul(pmx[0:1, 0:ww], onescol[:], wqa[p][:, off:off + ww],
                               start=(k == 0), stop=(k == nmm - 1))
                if oi == noff - 1:
                    s_t.inc(mm)
                k += 1
            t_done_abs[b] = s_t.n
            yield
        V.wait_ge(s_t.h, s_t.n)
        s_v.inc(V.tensor_reduce(statw[0:1, sw:sw + 1], pmx[0:1, 0:512], AX.X, ALU.add))
        pm_g[slot]["v"] = s_v.n
        V.wait_ge(s_v.h, s_v.n)
        V.tensor_scalar(statw[0:1, sw + 1:sw + 2], statw[0:1, sw:sw + 1],
                        1.0 / float(nrows * P), 1e-5, ALU.mult, ALU.max)
        vf()
        V.reciprocal(statw[0:1, sw + 2:sw + 3], statw[0:1, sw + 1:sw + 2])
        vf()
        s_v.inc(V.tensor_scalar(statw[0:1, sw + 3:sw + 4], statw[0:1, sw + 1:sw + 2],
                                0.5, None, ALU.mult))
        G.wait_ge(s_v.h, s_v.n)
        s_gq3[slot].inc(G.dma_start(wqs_d[slot:slot + 1, 0:2], statw[0:1, sw + 2:sw + 4]), 16)
        G.wait_ge(s_gq3[slot].h, s_gq3[slot].n)
        s_gl3[slot].inc(G.dma_start(consts[:, slot:slot + 1],
                                    wqs_d[slot:slot + 1, 0:1].partition_broadcast(P)), 16)
        s_gl3[slot].inc(G.dma_start(consts[:, 3 + slot:4 + slot],
                                    wqs_d[slot:slot + 1, 1:2].partition_broadcast(P)), 16)
        gl_done = s_gl3[slot].n
        yield
        S.wait_ge(s_gl3[slot].h, gl_done)
        s_done_sg = list(s_done_abs)
        v_done_sg = [0, 0]
        st_all = stsem.n
        for ci, r0 in enumerate(chunks):
            b = ci % 2
            p = bufs[b]
            rr = min(CH_ROWS, nrows - r0)
            w = rr
            G.wait_ge(s_s.h, s_done_sg[b])
            ldsem[b].inc(G.dma_start(wsc2[p][:, 0:w].rearrange("p (a q) -> p a q", q=P),
                                     w_h[r0:r0 + rr, :].rearrange("(p a) q -> p a q", p=P)), 16)
            ld_w[b] = ldsem[b].n
            S.wait_ge(ldsem[b].h, ld_w[b])
            S.wait_ge(s_v.h, v_done_sg[b])
            S.wait_ge(stsem.h, st_all)
            S.activation(wqa[p][:, 0:w], wsc2[p][:, 0:w], AF.Sign,
                         scale=consts[:, slot:slot + 1], bias=consts[:, 6:7])
            s_s.inc(S.activation(wqb[p][:, 0:w], wsc2[p][:, 0:w], AF.Sign,
                                 scale=consts[:, slot:slot + 1], bias=consts[:, 8:9]))
            s_done_sg[b] = s_s.n
            V.wait_ge(s_s.h, s_done_sg[b])
            s_v.inc(V.tensor_tensor(wqa[p][:, 0:w], wqa[p][:, 0:w], wqb[p][:, 0:w],
                                    ALU.add))
            v_done_sg[b] = s_v.n
            SY.wait_ge(s_v.h, v_done_sg[b])
            stsem.inc(SY.dma_start(wq_d[r0:r0 + rr, :].rearrange("(p a) q -> p a q", p=P),
                                   wqa[p][:, 0:w].rearrange("p (a q) -> p a q", q=P)), 16)
            if ci >= 1:
                st_all = stsem.n - 16
            yield
        SY.wait_ge(stsem.h, stsem.n)

    qgens = [gen_quant(wgT_h, wgq_d, INNER // P * D, 0),
             gen_quant(wuT_h, wuq_d, INNER // P * D, 1),
             gen_quant(wdT_h, wdq_d, D // P * INNER, 2)]
    live = list(qgens)
    while live:
        for g in list(live):
            try:
                next(g)
            except StopIteration:
                live.remove(g)
    V.wait_ge(s_gl3[0].h, s_gl3[0].n)
    s_v.inc(V.tensor_scalar(consts[:, 12:13], consts[:, 3:4], 0.5, None, ALU.mult))
    barrier()

    if STOP == "quant":
        return finish()
    s_v.inc(V.memset(state[:], 0.0))
    vf()

    def gen_A(sidx):
        t0 = sidx * TCH
        kt0 = t0 // P
        v_done_x = [0, 0]
        ld_x = [0, 0]
        V.wait_ge(s_s.h, xg["a_scal"])
        for k in range(NT):
            p = k % 2
            if k < 2:
                SY.wait_ge(s_s.h, xg["zn_s"][0])
                SY.wait_ge(s_s.h, xg["zn_s"][1])
            s_x[p].inc(SY.dma_start(xts[:, k * D:(k + 1) * D],
                                    x_h[t0 + k * P:t0 + (k + 1) * P, :]), 16)
            ld_x[p] = s_x[p].n
            V.wait_ge(s_x[p].h, ld_x[p])
            s_v.inc(ln_stats(stats[:, 2 * k:2 * k + 2], xts[:, k * D:(k + 1) * D]))
            v_done_x[p] = s_v.n
            yield
        mview = stats.ap().rearrange("p (k two) -> p two k", two=2)
        mean_v = mview[:, 0, :]
        var_v = mview[:, 1, :]
        S.wait_ge(s_v.h, s_v.n)
        s_s.inc(rsqrt_cols(var_v, sc1[:, 0:NT], statw[:, 0:NT]))
        yield
        V.wait_ge(s_s.h, s_s.n)
        V.tensor_copy(r1v[:, 0:NT], sc1[:, 0:NT])
        vf()
        V.tensor_tensor(statw[:, 0:NT], sc1[:, 0:NT], sc1[:, 0:NT], ALU.mult)
        vf()
        s_v.inc(V.tensor_tensor(statw[:, 0:NT], statw[:, 0:NT], var_v, ALU.mult))
        S.wait_ge(s_v.h, s_v.n)
        s_s.inc(rsqrt_cols(statw[:, 0:NT], statw[:, NT:2 * NT], statw[:, 0:NT]))
        yield
        V.wait_ge(s_s.h, s_s.n)
        V.tensor_tensor(sc1[:, 0:NT], sc1[:, 0:NT], statw[:, NT:2 * NT], ALU.mult)
        vf()
        V.tensor_tensor(sc2[:, 0:NT], mean_v, sc1[:, 0:NT], ALU.mult)
        vf()
        vf()
        V.tensor_scalar(sc2[:, 0:NT], sc2[:, 0:NT], -1.0, None, ALU.mult)
        vf()
        mev = stats.ap().rearrange("p (k par mv) -> p par mv k", par=2, mv=2)
        for (dst, ev, od) in ((mnM2[sidx % 2], mev[:, 0, 0, :], mev[:, 1, 0, :]),
                              (r1M2[sidx % 2], r1v.ap().rearrange("p (k two) -> p two k", two=2)[:, 0, :],
                               r1v.ap().rearrange("p (k two) -> p two k", two=2)[:, 1, :])):
            V.tensor_scalar(statw[:, 0:NMT], od, parc[:, 0:1], None, ALU.mult)
            vf()
            V.scalar_tensor_tensor(dst[:, 0:NMT], ev, parc[:, 1:2],
                                   statw[:, 0:NMT], ALU.mult, ALU.add)
            vf()
        s_v.inc(V.engine_nop())
        S.wait_ge(s_v.h, s_v.n)
        yield
        s_done_zn = [0, 0]
        t_done_zn = [0, 0]
        S.wait_ge(s_t.h, xg["znT_rd_t"])
        znT_v = znT.rearrange("p (c t) -> p c t", c=DBLK)
        for k in range(NT):
            p = k % 2
            S.wait_ge(s_t.h, t_done_zn[p])
            s_s.inc(S.activation(znt2[p][:], xts[:, k * D:(k + 1) * D], AF.Identity,
                                 scale=sc1[:, k:k + 1], bias=sc2[:, k:k + 1]))
            s_done_zn[p] = s_s.n
            TE.wait_ge(s_s.h, s_done_zn[p])
            TE.wait_ge(s_v.h, ptrw_g[p])
            TE.wait_ge(s_s.h, ptrw_gs[p])
            for c in range(DBLK):
                mm = TE.transpose(ptrw[p][:, c * P:(c + 1) * P],
                                  znt2[p][:, c * P:(c + 1) * P], ident[:])
                if c == DBLK - 1:
                    s_t.inc(mm)
            t_done_zn[p] = s_t.n
            S.wait_ge(s_t.h, t_done_zn[p])
            s_s.inc(S.activation(znT_v[:, :, k * P:(k + 1) * P],
                                 ptrw[p][:, 0:DBLK * P].rearrange("p (c q) -> p c q", c=DBLK),
                                 AF.Identity, scale=consts[:, 7:8], bias=consts[:, 9:10]))
            ptrw_gs[p] = s_s.n
            yield
        xg["a_scal"] = s_s.n
        xg["zn_s"] = list(s_done_zn)
        xg["zn_v"] = s_v.n
        xg["zn_s2"] = s_s.n

    def gen_B(sidx):
        t0 = sidx * TCH
        kt0 = t0 // P
        t_done_dt = [0, 0]
        s_done_dt = [0, 0]
        NTC = TCH // 512
        fscr2 = bscr2
        TE.wait_ge(s_v.h, xg["zn_v"])
        TE.wait_ge(s_s.h, xg["zn_s2"])
        for c in range(DBLK):
            for ti, tc in enumerate(range(0, TCH, 512)):
                w = min(512, TCH - tc)
                g = (c * NTC + ti) % 2
                pmx = pm0 if g == 0 else pm1
                pm_wait(g)
                for kk in range(DBLK):
                    mm = TE.matmul(pmx[:, 0:w],
                                   wdtg[:, kk * D + c * P:kk * D + (c + 1) * P],
                                   znT[:, kk * TCH + tc:kk * TCH + tc + w],
                                   start=(kk == 0), stop=(kk == DBLK - 1))
                    if kk == DBLK - 1:
                        s_t.inc(mm)
                t_done_dt[g] = s_t.n
                S.wait_ge(s_t.h, t_done_dt[g])
                S.activation(fscr2[g][:, 0:w], pmx[:, 0:w], AF.Exp,
                             bias=bdtc[:, c:c + 1], scale=consts[:, 7:8])
                sf()
                s_s.inc(S.activation(dtT[:, c * TCH + tc:c * TCH + tc + w],
                                     fscr2[g][:, 0:w], AF.Ln, bias=1.0))
                s_done_dt[g] = s_s.n
                pm_g[g]["s"] = s_s.n
                yield
        s_done_bc = 0
        for wi, wsrc in enumerate((wbg, wcg)):
            for ti, tc in enumerate(range(0, TCH, 512)):
                w = min(512, TCH - tc)
                pm_wait(2)
                for kk in range(DBLK):
                    mm = TE.matmul(pm2[0:N, 0:w], wsrc[:, kk * N:(kk + 1) * N],
                                   znT[:, kk * TCH + tc:kk * TCH + tc + w],
                                   start=(kk == 0), stop=(kk == DBLK - 1))
                    if kk == DBLK - 1:
                        s_t.inc(mm)
                S.wait_ge(s_t.h, s_t.n)
                S.wait_ge(s_t.h, xg["rep_t"])
                s_s.inc(S.activation(bcT_all[0:N, wi * TCH + tc:wi * TCH + tc + w],
                                     pm2[0:N, 0:w], AF.Identity,
                                     bias=bbc[0:N, wi:wi + 1], scale=consts[0:N, 7:8]))
                s_done_bc = s_s.n
                pm_g[2]["s"] = s_s.n
                yield
        xg["bct_s"] = s_done_bc
        V.wait_ge(s_s.h, s_s.n)
        for c in range(DBLK):
            V.tensor_scalar(wpT[:, c * TCH:(c + 1) * TCH], znT[:, c * TCH:(c + 1) * TCH],
                            gam[:, c:c + 1], bet[:, c:c + 1], ALU.mult, ALU.add)
            vf()
            s_v.inc(V.tensor_tensor(wpT[:, c * TCH:(c + 1) * TCH],
                                    wpT[:, c * TCH:(c + 1) * TCH],
                                    dtT[:, c * TCH:(c + 1) * TCH], ALU.mult))
            yield
        xg["znT_rd_t"] = s_t.n
        xg["bc_st"] = s_st.n

    def gen_C(sidx):
        t0 = sidx * TCH
        kt0 = t0 // P
        v_done_rep = [0] * 5
        v_done_ab = [0] * 5
        s_done_ab = [0] * 5
        v_done_ht = [0, 0]
        g_done_ht = [0, 0]
        s_done_rep = [0] * 5
        V.wait_ge(s_t.h, xg["acc_rd_t"])
        V.wait_ge(s_v.h, s_v.n)
        for n in range(N + 1):
            p = n % 5
            if n < N:
                TE.wait_ge(s_s.h, xg["pmq_s"])
                TE.wait_ge(s_s.h, xg["bct_s"])
                s_t.inc(TE.matmul(pmq[:, 0:TCH], sel_all[0:N, n * P:(n + 1) * P],
                                  bcT_all[0:N, 0:TCH], start=True, stop=True))
                S.wait_ge(s_t.h, s_t.n)
                S.wait_ge(s_v.h, v_done_rep[p])
                s_s.inc(S.activation(brep2[p][:, 0:TCH], pmq[:, 0:TCH], AF.Identity,
                                     scale=consts[:, 7:8], bias=consts[:, 9:10]))
                xg["pmq_s"] = s_s.n
                TE.wait_ge(s_s.h, xg["pmq_s"])
                s_t.inc(TE.matmul(pmq[:, 0:TCH], sel_all[0:N, n * P:(n + 1) * P],
                                  bcT_all[0:N, TCH:2 * TCH], start=True, stop=True))
                xg["rep_t"] = s_t.n
                S.wait_ge(s_t.h, s_t.n)
                s_s.inc(S.activation(crep2[p][:, 0:TCH], pmq[:, 0:TCH], AF.Identity,
                                     scale=consts[:, 7:8], bias=consts[:, 9:10]))
                xg["pmq_s"] = s_s.n
                s_done_rep[p] = s_s.n
            if n == 0:
                continue
            nn = n - 1
            pp = nn % 5
            for c0 in range(0, DBLK, 2):
                cpair = (c0, c0 + 1)
                for c in cpair:
                    ac = c % 5
                    S.wait_ge(s_v.h, v_done_ab[ac])
                    s_s.inc(S.activation(abar2[ac][:, 0:TCH],
                                         dtT[:, c * TCH:(c + 1) * TCH], AF.Exp,
                                         scale=apar[:, c * N + nn:c * N + nn + 1],
                                         bias=consts[:, 9:10]))
                    s_done_ab[ac] = s_s.n
                V.wait_ge(s_s.h, s_done_rep[pp])
                V.wait_ge(s_s.h, s_done_ab[cpair[1] % 5])
                V.wait_ge(s_ge.h, g_done_ht[0])
                V.wait_ge(s_ge.h, g_done_ht[1])
                V.wait_ge(s_v.h, v_done_ht[0])
                V.wait_ge(s_v.h, v_done_ht[1])
                bx_n = [0, 0]
                sc_n = [0, 0]
                ch_n = [0, 0]
                for c in cpair:
                    pc = c % 2
                    s_v.inc(V.tensor_tensor(bxt2[pc][:, 0:TCH],
                                            wpT[:, c * TCH:(c + 1) * TCH],
                                            brep2[pp][:, 0:TCH], ALU.mult))
                    bx_n[pc] = s_v.n
                for c in cpair:
                    pc = c % 2
                    ac = c % 5
                    V.wait_ge(s_v.h, bx_n[pc])
                    s_v.inc(V.tensor_tensor_scan(ht2[pc][:, 0:TCH],
                                                 abar2[ac][:, 0:TCH],
                                                 bxt2[pc][:, 0:TCH],
                                                 state[:, c * N + nn:c * N + nn + 1],
                                                 ALU.mult, ALU.add))
                    sc_n[pc] = s_v.n
                    v_done_ab[ac] = s_v.n
                for c in cpair:
                    pc = c % 2
                    V.wait_ge(s_v.h, sc_n[pc])
                    s_v.inc(V.tensor_tensor(cht2[pc][:, 0:TCH], ht2[pc][:, 0:TCH],
                                            crep2[pp][:, 0:TCH], ALU.mult))
                    ch_n[pc] = s_v.n
                for c in cpair:
                    pc = c % 2
                    V.wait_ge(s_v.h, ch_n[pc])
                    if nn == 0:
                        s_v.inc(V.tensor_tensor(accs[:, c * TCH:(c + 1) * TCH],
                                                ht2[pc][:, 0:TCH], crep2[pp][:, 0:TCH],
                                                ALU.mult))
                    else:
                        s_v.inc(V.tensor_tensor(accs[:, c * TCH:(c + 1) * TCH],
                                                accs[:, c * TCH:(c + 1) * TCH],
                                                cht2[pc][:, 0:TCH], ALU.add))
                    v_done_ht[pc] = s_v.n
                    G.wait_ge(s_v.h, v_done_ht[pc])
                    s_ge.inc(G.tensor_copy(state[:, c * N + nn:c * N + nn + 1],
                                           ht2[pc][:, TCH - 1:TCH]))
                    g_done_ht[pc] = s_ge.n
                yield
            v_done_rep[pp] = s_v.n
        V.wait_ge(s_ge.h, s_ge.n)
        V.wait_ge(s_s.h, xg["yam_rd_s"])
        for c in range(DBLK):
            p = c % 2
            TE.wait_ge(s_v.h, max(ptrw_g[p], s_v.n if c < 2 else 0))
            TE.wait_ge(s_s.h, ptrw_gs[p])
            for k in range(NT):
                mm = TE.transpose(ptrw[p][:, k * P:(k + 1) * P],
                                  accs[:, c * TCH + k * P:c * TCH + (k + 1) * P],
                                  ident[:])
                if k == NT - 1:
                    s_t.inc(mm)
            V.wait_ge(s_t.h, s_t.n)
            pv = ptrw[p][:, 0:NT * P].rearrange("p (k two q) -> p k two q", two=2, q=P)
            vf(V.tensor_scalar(ytd[:, 0:MINE].rearrange("p (k q) -> p k q", q=P),
                               pv[:, :, 1, :], parc[:, 0:1], None, ALU.mult))
            yv = yam.rearrange("p (k d) -> p k d", k=NMT)
            s_v.inc(V.scalar_tensor_tensor(yv[:, :, c * P:(c + 1) * P],
                                           pv[:, :, 0, :], parc[:, 1:2],
                                           ytd[:, 0:MINE].rearrange("p (k q) -> p k q", q=P),
                                           ALU.mult, ALU.add))
            ptrw_g[p] = s_v.n
            yield
        xg["acc_rd_t"] = s_t.n

    def gen_C2(sidx):
        t0 = sidx * TCH
        kt0 = t0 // P
        V.wait_ge(s_s.h, s_s.n)
        SY.wait_ge(s_v.h, xg["um_rd_v"])
        for k in range(NMT):
            s_ld.inc(SY.dma_start(um[:, k * D:(k + 1) * D],
                                  xm_h[sidx * MINE + k * P:sidx * MINE + (k + 1) * P, :]), 16)
        yield
        V.wait_ge(s_t.h, s_t.n)
        mnM = mnM2[sidx % 2]
        r1M = r1M2[sidx % 2]
        vf(V.tensor_tensor(statw2[:, 0:NMT], mnM[:, 0:NMT], r1M[:, 0:NMT], ALU.mult))
        s_v.inc(V.tensor_scalar(statw2[:, 0:NMT], statw2[:, 0:NMT], -1.0, None, ALU.mult))
        S.wait_ge(s_v.h, s_v.n)
        S.wait_ge(s_ld.h, s_ld.n)
        for k in range(NMT):
            mm = S.activation(z1m[:, k * D:(k + 1) * D], um[:, k * D:(k + 1) * D],
                              AF.Identity, scale=r1M[:, k:k + 1], bias=statw2[:, k:k + 1])
            if k == NMT - 1:
                s_s.inc(mm)
        yield
        V.wait_ge(s_s.h, s_s.n)
        for k in range(NMT):
            V.tensor_tensor(z1m[:, k * D:(k + 1) * D], z1m[:, k * D:(k + 1) * D],
                            drep[:], ALU.mult)
        vf()
        for k in range(NMT):
            V.tensor_tensor(yam[:, k * D:(k + 1) * D], yam[:, k * D:(k + 1) * D],
                            z1m[:, k * D:(k + 1) * D], ALU.add)
        vf()
        for k in range(NMT):
            s_v.inc(ln_stats(ln3s[:, 2 * k:2 * k + 2], yam[:, k * D:(k + 1) * D]))
            yield
        S.wait_ge(s_v.h, s_v.n)
        l3v = ln3s.ap().rearrange("p (k two) -> p two k", two=2)
        s_s.inc(rsqrt_cols(l3v[:, 1, :], sxv[:, 0:NMT], statw2[:, 0:NMT]))
        V.wait_ge(s_s.h, s_s.n)
        vf(V.tensor_tensor(statw2[:, 0:NMT], l3v[:, 0, :], sxv[:, 0:NMT], ALU.mult))
        s_v.inc(V.tensor_scalar(statw2[:, 0:NMT], statw2[:, 0:NMT], -1.0, None, ALU.mult))
        S.wait_ge(s_v.h, s_v.n)
        for k in range(NMT):
            mm = S.activation(fTM[:, k * D:(k + 1) * D], yam[:, k * D:(k + 1) * D],
                              AF.Identity, scale=sxv[:, k:k + 1], bias=statw2[:, k:k + 1])
            if k == NMT - 1:
                s_s.inc(mm)
        xg["yam_rd_s"] = s_s.n
        yield
        V.wait_ge(s_s.h, s_s.n)
        for k in range(NMT):
            mm = V.tensor_tensor(um[:, k * D:(k + 1) * D], um[:, k * D:(k + 1) * D],
                                 fTM[:, k * D:(k + 1) * D], ALU.add)
            if k == NMT - 1:
                s_v.inc(mm)
        S.wait_ge(s_v.h, s_v.n)
        SY.wait_ge(s_v.h, s_v.n)

    def gen_D_head(sidx):
        t0 = sidx * TCH
        kt0 = t0 // P
        zq = zqb
        V.wait_ge(s_s.h, s_s.n)
        for k in range(NMT):
            s_v.inc(ln_stats(ln2s[:, 2 * k:2 * k + 2], um[:, k * D:(k + 1) * D]))
            yield
        l2v = ln2s.ap().rearrange("p (k two) -> p two k", two=2)
        S.wait_ge(s_v.h, s_v.n)
        s_s.inc(rsqrt_cols(l2v[:, 1, :], sxv[:, 0:NMT], statw2[:, 0:NMT]))
        V.wait_ge(s_s.h, s_s.n)
        vf(V.tensor_tensor(statw2[:, 0:NMT], l2v[:, 0, :], sxv[:, 0:NMT], ALU.mult))
        s_v.inc(V.tensor_scalar(statw2[:, 0:NMT], statw2[:, 0:NMT], -1.0, None, ALU.mult))
        S.wait_ge(s_v.h, s_v.n)
        for k in range(NMT):
            mm = S.activation(fTM[:, k * D:(k + 1) * D], um[:, k * D:(k + 1) * D],
                              AF.Identity, scale=sxv[:, k:k + 1], bias=statw2[:, k:k + 1])
            if k == NMT - 1:
                s_s.inc(mm)
        yield
        V.wait_ge(s_s.h, s_s.n)
        for k in range(NMT):
            V.tensor_reduce(statw2[:, k:k + 1], fTM[:, k * D:(k + 1) * D], AX.X,
                            ALU.max, apply_absolute_value=True)
        vf()
        yield
        V.tensor_scalar(statw2[:, 8:8 + NMT], statw2[:, 0:NMT], 1e-5, 1.0 / 127.0,
                        ALU.max, ALU.mult)
        vf()
        V.reciprocal(statw2[:, 0:NMT], statw2[:, 8:8 + NMT])
        vf()
        for k in range(NMT):
            V.tensor_scalar(fTM[:, k * D:(k + 1) * D], fTM[:, k * D:(k + 1) * D],
                            statw2[:, k:k + 1], MAGIC, ALU.mult, ALU.add)
        vf()
        for k in range(NMT):
            mm = V.tensor_scalar(zq[:, k * D:(k + 1) * D], fTM[:, k * D:(k + 1) * D],
                                 -MAGIC, statw2[:, 8 + k:9 + k], ALU.add, ALU.mult)
            if k == NMT - 1:
                s_v.inc(mm)
        TE.wait_ge(s_v.h, s_v.n)
        z2qT_v = z2qT.rearrange("p (c t) -> p c t", c=DBLK)
        for k in range(NMT):
            p = k % 2
            TE.wait_ge(s_v.h, ptrw_g[p])
            TE.wait_ge(s_s.h, ptrw_gs[p])
            for c in range(DBLK):
                mm = TE.transpose(ptrw[p][:, c * P:(c + 1) * P],
                                  zq[:, k * D + c * P:k * D + (c + 1) * P], ident[:])
                if c == DBLK - 1:
                    s_t.inc(mm)
            S.wait_ge(s_t.h, s_t.n)
            s_s.inc(S.activation(z2qT_v[:, :, k * P:(k + 1) * P],
                                 ptrw[p][:, 0:DBLK * P].rearrange("p (c q) -> p c q", c=DBLK),
                                 AF.Identity, scale=consts[:, 7:8], bias=consts[:, 9:10]))
            ptrw_gs[p] = s_s.n
            yield
        xg["z2q_s"] = s_s.n
        TE.wait_ge(s_s.h, xg["z2q_s"])

    def gen_D(sidx):
        zq = zqb
        t_done_fw = [0, 0]
        s_done_gu = [0, 0, 0, 0]
        v_done_gp = gp_v
        ld_fw = [0, 0]

        def gu_load(ic):
            p = ic % 2
            SY.wait_ge(s_t.h, t_done_fw[p])
            s_fw[p].inc(SY.dma_start(fw2[p][:, 0:DBLK * P].rearrange("p (kk o) -> p kk o", kk=DBLK),
                                     wgq_d[ic * D:(ic + 1) * D, :].rearrange(
                                         "(p kk) o -> p kk o", p=P)), 16)
            s_fw[p].inc(SY.dma_start(fu2[p][:, 0:DBLK * P].rearrange("p (kk o) -> p kk o", kk=DBLK),
                                     wuq_d[ic * D:(ic + 1) * D, :].rearrange(
                                         "(p kk) o -> p kk o", p=P)), 16)
            ld_fw[p] = s_fw[p].n

        gu_load(0)
        for ic in range(IBLK):
            p = ic % 2
            q = ic % 4
            if ic + 1 < IBLK:
                gu_load(ic + 1)
            TE.wait_ge(s_fw[p].h, ld_fw[p])
            pmg = pm0 if p == 0 else pm2
            pmu = pm1 if p == 0 else pm3
            for tc in range(0, MINE, 512):
                w = min(512, MINE - tc)
                pm_wait(0 if p == 0 else 2)
                pm_wait(1 if p == 0 else 3)
                for kk in range(DBLK):
                    mm = TE.matmul(pmg[:, tc % 512:tc % 512 + w], fw2[p][:, kk * P:(kk + 1) * P],
                                   z2qT[:, kk * MINE + tc:kk * MINE + tc + w],
                                   start=(kk == 0), stop=(kk == DBLK - 1))
                    if kk == DBLK - 1:
                        s_t.inc(mm)
                for kk in range(DBLK):
                    mm = TE.matmul(pmu[:, tc % 512:tc % 512 + w], fu2[p][:, kk * P:(kk + 1) * P],
                                   z2qT[:, kk * MINE + tc:kk * MINE + tc + w],
                                   start=(kk == 0), stop=(kk == DBLK - 1))
                    if kk == DBLK - 1:
                        s_t.inc(mm)
                t_done_fw[p] = s_t.n
                S.wait_ge(s_t.h, t_done_fw[p])
                S.wait_ge(s_v.h, v_done_gp[q])
                s_s.inc(S.activation(gt2[q][:, tc:tc + w], pmg[:, tc % 512:tc % 512 + w],
                                     AF.Identity, scale=consts[:, 12:13],
                                     bias=consts[:, 9:10]))
                s_s.inc(S.activation(pt2[q][:, tc:tc + w], pmu[:, tc % 512:tc % 512 + w],
                                     AF.Identity, scale=consts[:, 4:5], bias=consts[:, 9:10]))
                s_done_gu[q] = s_s.n
                pm_g[0 if p == 0 else 2]["s"] = s_s.n
                pm_g[1 if p == 0 else 3]["s"] = s_s.n
            yield
            if q == 3:
                S.wait_ge(s_v.h, q4v[0])
                S.wait_ge(s_s.h, s_done_gu[q])
                s_s.inc(S.activation(sqrep4[:], gt4[:], AF.Tanh))
                V.wait_ge(s_s.h, s_s.n)
                vf(V.scalar_tensor_tensor(gt4[:], sqrep4[:], 1.0, gt4[:],
                                          ALU.add, ALU.mult))
                q4v[0] = s_v.n
                vf(V.tensor_tensor(qs[:, (ic - 3) * MINE:(ic + 1) * MINE], gt4[:],
                                   pt4[:], ALU.mult))
                for j in range(4):
                    v_done_gp[j] = s_v.n
                if ic == 3:
                    V.tensor_copy(mrow4[:], qs[:, 0:4 * MINE])
                    s_v.inc(V.tensor_copy(mrow24[:], qs[:, 0:4 * MINE]))
                else:
                    V.tensor_tensor(mrow4[:], mrow4[:],
                                    qs[:, (ic - 3) * MINE:(ic + 1) * MINE], ALU.max)
                    s_v.inc(V.tensor_tensor(mrow24[:], mrow24[:],
                                            qs[:, (ic - 3) * MINE:(ic + 1) * MINE],
                                            ALU.min))
                vf()
            yield
        vf(V.scalar_tensor_tensor(mrow4[:], mrow24[:], -1.0, mrow4[:],
                                  ALU.mult, ALU.max))
        vf(V.tensor_tensor(mrow4[:, 0:2 * MINE], mrow4[:, 0:2 * MINE],
                           mrow4[:, 2 * MINE:4 * MINE], ALU.max))
        vf(V.tensor_tensor(mrow[:, 0:MINE], mrow4[:, 0:MINE],
                           mrow4[:, MINE:2 * MINE], ALU.max))
        TE.wait_ge(s_v.h, s_v.n)
        for k in range(NMT):
            s_t.inc(TE.transpose(ptr[:, 0:P], mrow[:, k * P:(k + 1) * P], ident[:]))
            V.wait_ge(s_t.h, s_t.n)
            V.tensor_reduce(sqv[:, k:k + 1], ptr[:, 0:P], AX.X, ALU.max,
                            apply_absolute_value=True)
            vf()
            s_v.inc(V.tensor_copy(statw2[:, 15:16], sqv[:, k:k + 1]))
            TE.wait_ge(s_v.h, s_v.n)
            yield
        V.tensor_scalar(sqvr[:, 0:NMT], sqv[:, 0:NMT], 1e-5, 1.0 / 127.0,
                        ALU.max, ALU.mult)
        vf()
        V.reciprocal(sqv[:, 0:NMT], sqvr[:, 0:NMT])
        vf()
        V.wait_ge(s_t.h, xg["sqT_t"])
        V.tensor_copy(sqstg[:, 0:NMT], sqv[:, 0:NMT])
        vf()
        s_v.inc(V.tensor_copy(sqstg[:, NMT:2 * NMT], sqvr[:, 0:NMT]))
        TE.wait_ge(s_v.h, s_v.n)
        TE.wait_ge(s_v.h, ptrw_g[0])
        TE.wait_ge(s_s.h, ptrw_gs[0])
        for j in range(2 * NMT):
            mm = TE.transpose(ptrw[0][0:1, j * P:(j + 1) * P], sqstg[:, j:j + 1], ident[:])
            if j == 2 * NMT - 1:
                s_t.inc(mm)
        xg["sqT_t"] = s_t.n
        S.wait_ge(s_t.h, s_t.n)
        s_s.inc(S.activation(sqTrow[0:1, 0:2 * NMT * P], ptrw[0][0:1, 0:2 * NMT * P],
                             AF.Identity, scale=consts[0:1, 7:8], bias=consts[0:1, 9:10]))
        V.wait_ge(s_s.h, s_s.n)
        s_v.inc(V.engine_nop())
        ptrw_g[0] = s_v.n
        TE.wait_ge(s_s.h, s_s.n)
        TE.wait_ge(s_s.h, xg["pmq_s"])
        for k in range(NMT):
            s_t.inc(TE.matmul(pmq[:, k * P:(k + 1) * P], onesrow[0:1, 0:P],
                              sqTrow[0:1, k * P:(k + 1) * P], start=True, stop=True))
            s_t.inc(TE.matmul(pmq[:, MINE + k * P:MINE + (k + 1) * P], onesrow[0:1, 0:P],
                              sqTrow[0:1, (NMT + k) * P:(NMT + k + 1) * P],
                              start=True, stop=True))
        S.wait_ge(s_t.h, s_t.n)
        S.wait_ge(s_v.h, xg["qs_v"])
        S.wait_ge(s_v.h, q4v[0])
        RB = 4 if 4 * MINE <= NMT * D else 2
        for rep in range(RB):
            s_s.inc(S.activation(sqrep4[:, rep * MINE:(rep + 1) * MINE],
                                 pmq[:, 0:MINE], AF.Identity,
                                 scale=consts[:, 7:8], bias=consts[:, 9:10]))
            s_s.inc(S.activation(sqrrep4[:, rep * MINE:(rep + 1) * MINE],
                                 pmq[:, MINE:2 * MINE], AF.Identity,
                                 scale=consts[:, 7:8], bias=consts[:, 9:10]))
        xg["pmq_s"] = s_s.n
        yield
        V.wait_ge(s_s.h, xg["pmq_s"])
        RW = RB * MINE
        for jj in range(IBLK // RB):
            sl = qs[:, jj * RW:(jj + 1) * RW]
            V.tensor_tensor(fTM[:, 0:RW], sl, sqrep4[:, 0:RW], ALU.mult)
            vf()
            V.tensor_scalar(fTM[:, 0:RW], fTM[:, 0:RW], MAGIC, -MAGIC,
                            ALU.add, ALU.add)
            vf()
            V.tensor_copy(qsc4[:, 0:RW], fTM[:, 0:RW])
            vf()
            s_v.inc(V.tensor_tensor(sl, qsc4[:, 0:RW], sqrrep4[:, 0:RW], ALU.mult))
            yield
        xg["qs_v"] = s_v.n
        TE.wait_ge(s_v.h, xg["qs_v"])
        t_done_wd = [0, 0]
        s_done_dn = [0, 0]
        v_done_fc = [0, 0]
        ld_wd = [0, 0]

        def dn_load(oc):
            p = oc % 2
            SY.wait_ge(s_t.h, t_done_wd[p])
            s_dw[p].inc(SY.dma_start(wdsl2[p][:, 0:IBLK * P].rearrange("p (kk o) -> p kk o", kk=IBLK),
                                     wdq_d[oc * INNER:(oc + 1) * INNER, :].rearrange(
                                         "(p kk) o -> p kk o", p=P)), 16)
            ld_wd[p] = s_dw[p].n

        dn_load(0)
        for oc in range(DBLK):
            p = oc % 2
            fq = oc % 4
            if oc + 1 < DBLK:
                dn_load(oc + 1)
            TE.wait_ge(s_dw[p].h, ld_wd[p])
            pmx = pm0 if p == 0 else pm1
            for tc in range(0, MINE, 512):
                w = min(512, MINE - tc)
                pm_wait(p)
                for kk in range(IBLK):
                    mm = TE.matmul(pmx[:, 0:w], wdsl2[p][:, kk * P:(kk + 1) * P],
                                   qs[:, kk * MINE + tc:kk * MINE + tc + w],
                                   start=(kk == 0), stop=(kk == IBLK - 1))
                    if kk == IBLK - 1:
                        s_t.inc(mm)
                t_done_wd[p] = s_t.n
                S.wait_ge(s_t.h, t_done_wd[p])
                S.wait_ge(s_v.h, gp_v[fq])
                S.wait_ge(s_t.h, fcm_t[fq])
                s_s.inc(S.activation(fcm2[fq][:, tc:tc + w], pmx[:, 0:w], AF.Identity,
                                     scale=consts[:, 5:6], bias=consts[:, 9:10]))
                s_done_dn[p] = s_s.n
                pm_g[p]["s"] = s_s.n
            yield
            TE.wait_ge(s_s.h, s_done_dn[p])
            TE.wait_ge(s_v.h, ptrw_g[p])
            TE.wait_ge(s_s.h, ptrw_gs[p])
            ptw = ptrw[p]
            for k in range(NMT):
                mm = TE.transpose(ptw[:, k * P:(k + 1) * P], fcm2[fq][:, k * P:(k + 1) * P],
                                  ident[:])
                if k == NMT - 1:
                    s_t.inc(mm)
            t_done_wd[p] = s_t.n
            fcm_t[fq] = s_t.n
            S.wait_ge(s_t.h, t_done_wd[p])
            S.wait_ge(s_v.h, xg["qs_v"])
            fTM_v = fTM.rearrange("p (k d) -> p k d", k=NMT)
            s_s.inc(S.activation(fTM_v[:, :, oc * P:(oc + 1) * P],
                                 ptw[:, 0:NMT * P].rearrange("p (k q) -> p k q", k=NMT),
                                 AF.Identity, scale=consts[:, 7:8], bias=consts[:, 9:10]))
            ptrw_gs[p] = s_s.n
            xg["ftm_s"] = s_s.n
            yield
        vf()
        V.wait_ge(s_s.h, xg["ftm_s"])
        for k in range(NMT):
            s_v.inc(ln_stats(ln2s[:, 2 * k:2 * k + 2], fTM[:, k * D:(k + 1) * D]))
        S.wait_ge(s_v.h, s_v.n)
        l4v = ln2s.ap().rearrange("p (k two) -> p two k", two=2)
        s_s.inc(rsqrt_cols(l4v[:, 1, :], sxv[:, 0:NMT], statw2[:, 0:NMT]))
        V.wait_ge(s_s.h, s_s.n)
        vf(V.tensor_tensor(statw2[:, 0:NMT], l4v[:, 0, :], sxv[:, 0:NMT], ALU.mult))
        s_v.inc(V.tensor_scalar(statw2[:, 0:NMT], statw2[:, 0:NMT], -1.0, None, ALU.mult))
        S.wait_ge(s_v.h, s_v.n)
        for k in range(NMT):
            mm = S.activation(zq[:, k * D:(k + 1) * D], fTM[:, k * D:(k + 1) * D],
                              AF.Identity, scale=sxv[:, k:k + 1], bias=statw2[:, k:k + 1])
            if k == NMT - 1:
                s_s.inc(mm)
        V.wait_ge(s_s.h, s_s.n)
        st_done_o = list(xg["o_st"])
        osc2 = [oscr, fscr]
        for k in range(NMT):
            p = k % 2
            V.wait_ge(s_st.h, st_done_o[p])
            s_v.inc(V.tensor_tensor(osc2[p][:, 0:D], um[:, k * D:(k + 1) * D],
                                    zq[:, k * D:(k + 1) * D], ALU.add))
            SY.wait_ge(s_v.h, s_v.n)
            s_st.inc(SY.dma_start(out_h[sidx * MINE + k * P:sidx * MINE + (k + 1) * P, :],
                                  osc2[p][:, 0:D]), 16)
            st_done_o[p] = s_st.n
            yield
        xg["o_st"] = list(st_done_o)
        xg["um_rd_v"] = s_v.n


    def gen_MIX(sidx):
        for _ in gen_A(sidx):
            yield
        for _ in gen_B(sidx):
            yield
        for _ in gen_C(sidx):
            yield

    def gen_DFull(sidx):
        for _ in gen_C2(sidx):
            yield
        for _ in gen_D_head(sidx):
            yield
        for _ in gen_D(sidx):
            yield

    def drive(mix, dside, rm=3, rd=4):
        di = 0
        while True:
            for _ in range(rm):
                try:
                    next(mix)
                except StopIteration:
                    return dside[di:]
            for _ in range(rd):
                if di < len(dside):
                    try:
                        next(dside[di])
                    except StopIteration:
                        di += 1

    dside = []
    for s in range(NSUP):
        if s >= 1:
            dside.append(gen_DFull(s - 1))
        dside = drive(gen_MIX(s), dside)
    dside.append(gen_DFull(NSUP - 1))
    for g in dside:
        for _ in g:
            pass
    barrier()

    SY.wait_ge(s_st.h, s_st.n)
    for cm in reversed(ctxs):
        cm.__exit__(None, None, None)
    return nc


def _prep_inputs(inputs, cfg):
    D, N = cfg["D"], cfg["N"]
    DBLK = D // P
    f = np.float32

    def arr(k):
        return np.asarray(inputs[k], f)

    gam = np.ascontiguousarray(arr("L0_attn_gamma").reshape(DBLK, P).T)
    bet = np.ascontiguousarray(arr("L0_attn_beta").reshape(DBLK, P).T)
    a_pp = np.ascontiguousarray(
        arr("L0_attn_A_log").reshape(DBLK, P, N).transpose(1, 0, 2).reshape(P, DBLK * N))
    return {
        "wdtT": np.ascontiguousarray(arr("L0_attn_W_dt").T),
        "wbT": np.ascontiguousarray(arr("L0_attn_W_B").T),
        "wcT": np.ascontiguousarray(arr("L0_attn_W_C").T),
        "gam_pp": gam, "bet_pp": bet, "a_pp": a_pp,
        "bdt_row": np.ascontiguousarray(arr("L0_attn_b_dt").reshape(1, D)),
        "dparam_row": np.ascontiguousarray(arr("L0_attn_D_param").reshape(1, D)),
        "wgT": np.ascontiguousarray(
            arr("L0_ffn_w_gate").T.reshape(D // P, P, -1, P)
            .transpose(2, 1, 0, 3).reshape(-1, P)),
        "wuT": np.ascontiguousarray(
            arr("L0_ffn_w_up").T.reshape(D // P, P, -1, P)
            .transpose(2, 1, 0, 3).reshape(-1, P)),
        "wdT": np.ascontiguousarray(
            arr("L0_ffn_w_down").T.reshape(-1, P, D // P, P)
            .transpose(2, 1, 0, 3).reshape(-1, P)),
    }


def kernel(**inputs):
    x = np.asarray(inputs["L0_attn_x"], np.float32)
    B, T, D = x.shape
    cfg = {"T": T, "D": D, "N": int(inputs["L0_attn_A_log"].shape[1]),
           "INNER": int(inputs["L0_ffn_w_gate"].shape[0]), "TCH": 512}
    nc = build_program(cfg)
    base = _prep_inputs(inputs, cfg)
    in_maps = []
    for c in range(8):
        b, j = c // 2, c % 2
        m = dict(base)
        m["x"] = np.ascontiguousarray(x[b])
        xb = x[b].reshape(T // P, P, D)
        m["xm"] = np.ascontiguousarray(xb[j::2].reshape(T // 2, D))
        m["parf"] = np.full((1, 1), j, np.float32)
        in_maps.append(m)
    res = run_bass_kernel_spmd(nc, in_maps, core_ids=list(range(8)))
    out = np.empty((B, T, D), np.float32)
    TCH = cfg["TCH"]
    MINE = TCH // 2
    NMT = MINE // P
    for c in range(8):
        b, j = c // 2, c % 2
        o = np.asarray(res.results[c]["out"])
        for s in range(T // TCH):
            for k in range(NMT):
                trow = s * TCH + (2 * k + j) * P
                out[b, trow:trow + P, :] = o[s * MINE + k * P:s * MINE + (k + 1) * P, :]
    return out


# version 39
# speedup vs baseline: 1.0180x; 1.0180x over previous
import numpy as np
import concourse.bass as bass
import concourse.mybir as mybir
from concourse.bass_utils import run_bass_kernel_spmd

F32 = mybir.dt.float32
BF16 = mybir.dt.bfloat16
I32 = mybir.dt.int32
AF = mybir.ActivationFunctionType
ALU = mybir.AluOpType
AX = mybir.AxisListType
ts = bass.ts

MAGIC = 12582912.0
LN_EPS = 1e-5
P = 128


def build_program(cfg):
    T, D, N, INNER = cfg["T"], cfg["D"], cfg["N"], cfg["INNER"]
    TCH = cfg["TCH"]
    DBG = cfg.get("debug", False)
    DBLK = D // P
    NSUP = T // TCH
    NT = TCH // P
    MINE = TCH // 2
    NMT = MINE // P
    IBLK = INNER // P
    WQC = min(INNER, 1024)

    nc = bass.Bass("TRN2", target_bir_lowering=False)

    def din(name, shape, dtype=F32):
        return nc.dram_tensor(name, shape, dtype, kind="ExternalInput").ap()

    x_h = din("x", [T, D])
    xm_h = din("xm", [T // 2, D])
    parf_h = din("parf", [1, 1])
    wdtT_h = din("wdtT", [D, D])
    wbT_h = din("wbT", [D, N])
    wcT_h = din("wcT", [D, N])
    gam_h = din("gam_pp", [P, DBLK])
    bet_h = din("bet_pp", [P, DBLK])
    bdt_h = din("bdt_row", [1, D])
    apar_h = din("a_pp", [P, DBLK * N])
    dpar_h = din("dparam_row", [1, D])
    wgT_h = din("wgT", [INNER // P * D, P])
    wuT_h = din("wuT", [INNER // P * D, P])
    wdT_h = din("wdT", [D // P * INNER, P])

    out_h = nc.dram_tensor("out", [NSUP * MINE, D], F32, kind="ExternalOutput").ap()

    def dsc(name, shape, dtype):
        return nc.dram_tensor(name, shape, dtype, kind="Internal").ap()

    bT_d = dsc("bT_d", [N, TCH], BF16)
    cT_d = dsc("cT_d", [N, TCH], BF16)
    wgq_d = dsc("wgq_d", [INNER // P * D, P], BF16)
    wuq_d = dsc("wuq_d", [INNER // P * D, P], BF16)
    wdq_d = dsc("wdq_d", [D // P * INNER, P], BF16)
    brow_d = dsc("brow_d", [1, D + 2 * N], F32)
    sq_d = dsc("sq_d", [2, MINE], BF16)
    wqs_d = dsc("wqs_d", [3, 2], F32)

    dbg = {}
    if DBG:
        for nm, shape in [("d_znT", [D, TCH]), ("d_dtT", [D, TCH]),
                          ("d_bT", [N, TCH]), ("d_cT", [N, TCH]),
                          ("d_wT", [D, TCH]), ("d_y", [TCH, D]),
                          ("d_u", [MINE, D]), ("d_z2q", [D, MINE]),
                          ("d_q", [INNER, MINE]), ("d_wgq", [D, INNER]),
                          ("d_f", [MINE, D]),
                          ("d_h", [P, TCH])]:
            dbg[nm] = nc.dram_tensor(nm, shape, F32, kind="ExternalOutput").ap()

    ctxs = []

    def sb(name, shape, dtype):
        cm = nc.sbuf_tensor(name, shape, dtype)
        t = cm.__enter__()
        ctxs.append(cm)
        return t

    def psb(name, shape, dtype=F32):
        cm = nc.psum_tensor(name, shape, dtype)
        t = cm.__enter__()
        ctxs.append(cm)
        return t

    ident = sb("ident", [P, P], BF16)
    onescol = sb("onescol", [P, 1], BF16)
    onesrow = sb("onesrow", [1, P], BF16)
    sqstg = sb("sqstg", [P, 8], BF16)
    sel_all = sb("sel_all", [N, N * P], BF16)
    bcT_all = sb("bcT_all", [N, 2 * TCH], BF16)
    sqTrow = sb("sqTrow", [1, 8 * P], BF16)
    gam = sb("gam", [P, DBLK], F32)
    bet = sb("bet", [P, DBLK], F32)
    apar = sb("apar", [P, DBLK * N], F32)
    drep = sb("drep", [P, D], BF16)
    bdtc = sb("bdtc", [P, DBLK], F32)
    bbc = sb("bbc", [N, 2], F32)
    parc = sb("parc", [P, 2], F32)
    consts = sb("consts", [P, 14], F32)
    wdtg = sb("wdtg", [P, DBLK * D], BF16)
    wbg = sb("wbg", [P, DBLK * N], BF16)
    wcg = sb("wcg", [P, DBLK * N], BF16)
    stats = sb("stats", [P, NT * 2], F32)
    sc1 = sb("sc1", [P, NT], F32)
    r1v = sb("r1v", [P, NT], F32)
    mnM2 = [sb("mnM0", [P, NMT], F32), sb("mnM1", [P, NMT], F32)]
    r1M2 = [sb("r1M0", [P, NMT], F32), sb("r1M1", [P, NMT], F32)]
    sc2 = sb("sc2", [P, NT], F32)
    statw = sb("statw", [P, 16], F32)
    statw2 = sb("statw2", [P, 16], F32)
    ln2s = sb("ln2s", [P, 2 * NMT], F32)
    ln3s = sb("ln3s", [P, 2 * NMT], F32)
    ln4s = sb("ln4s", [P, 2], F32)
    sxv = sb("sxv", [P, NMT], F32)
    sqv = sb("sqv", [P, NMT], F32)
    sqvr = sb("sqvr", [P, NMT], F32)
    accs = sb("accs", [P, DBLK * TCH], BF16)
    state = sb("state", [P, DBLK * N], F32)
    xts = sb("xts", [P, NT * D], F32)
    xt2 = [xts[:, 0:D], xts[:, D:2 * D]]
    xt0 = xt2[0]
    znt2 = [sb("znt", [P, D], BF16), sb("zntb", [P, D], BF16)]
    znt = znt2[0]
    bns = sb("bns", [P, 16], F32)

    a_b1 = sb("a_b1", [P, 2 * DBLK * TCH + IBLK * MINE], BF16)
    dtT = a_b1[:, 0:DBLK * TCH]
    wpT = a_b1[:, DBLK * TCH:2 * DBLK * TCH]
    qs = a_b1[:, 2 * DBLK * TCH:2 * DBLK * TCH + IBLK * MINE]
    a_b2 = sb("a_b2", [P, DBLK * TCH + 2 * NMT * D + MINE + DBLK * MINE], BF16)
    znT = a_b2[:, 0:DBLK * TCH]
    _o2 = DBLK * TCH
    yam = a_b2[:, _o2:_o2 + NMT * D]
    z1m = a_b2[:, _o2 + NMT * D:_o2 + 2 * NMT * D]
    ytd = a_b2[:, _o2 + 2 * NMT * D:_o2 + 2 * NMT * D + MINE]
    z2qT = a_b2[:, _o2 + 2 * NMT * D + MINE:_o2 + 2 * NMT * D + MINE + DBLK * MINE]
    SCRW = max(4 * TCH, 2 * D + 2 * N)
    FCOLS = max(DBLK * TCH, SCRW, 2 * NMT * D + 3 * D + 1024, NMT * D + 6 * WQC)
    a_f = sb("a_f", [P, FCOLS], F32)
    scr = a_f[:, 0:SCRW]

    wsc = a_f[:, NMT * D:NMT * D + WQC]
    um = a_f[:, 0:NMT * D]
    fTM = a_f[:, NMT * D:2 * NMT * D]
    fscr = a_f[:, 2 * NMT * D:2 * NMT * D + D]
    zscr = a_f[:, 2 * NMT * D + D:2 * NMT * D + 2 * D]
    oscr = a_f[:, 2 * NMT * D + 2 * D:2 * NMT * D + 3 * D]
    _bs = 2 * NMT * D + 3 * D
    bscr2 = [a_f[:, _bs:_bs + 512], a_f[:, _bs + 512:_bs + 1024]]
    a_b3 = sb("a_b3", [P, 21 * TCH + 4 * DBLK * P + 33 * MINE + 2 * IBLK * P + NMT * D], BF16)
    brep2 = [a_b3[:, j * TCH:(j + 1) * TCH] for j in range(5)]
    crep2 = [a_b3[:, (5 + j) * TCH:(6 + j) * TCH] for j in range(5)]
    abar2 = [a_b3[:, (10 + j) * TCH:(11 + j) * TCH] for j in range(5)]
    bxt2 = [a_b3[:, 15 * TCH:16 * TCH], a_b3[:, 16 * TCH:17 * TCH]]
    ht2 = [a_b3[:, 17 * TCH:18 * TCH], a_b3[:, 18 * TCH:19 * TCH]]
    cht2 = [a_b3[:, 19 * TCH:20 * TCH], a_b3[:, 20 * TCH:21 * TCH]]
    betac = sb("betac", [P, DBLK], BF16)
    _fb = 21 * TCH
    fw2 = [a_b3[:, _fb:_fb + DBLK * P], a_b3[:, _fb + DBLK * P:_fb + 2 * DBLK * P]]
    fu2 = [a_b3[:, _fb + 2 * DBLK * P:_fb + 3 * DBLK * P],
           a_b3[:, _fb + 3 * DBLK * P:_fb + 4 * DBLK * P]]
    _o = _fb + 4 * DBLK * P
    gt2 = [a_b3[:, _o + j * MINE:_o + (j + 1) * MINE] for j in range(4)]
    gt4 = a_b3[:, _o:_o + 4 * MINE]
    pt2 = [a_b3[:, _o + (4 + j) * MINE:_o + (5 + j) * MINE] for j in range(4)]
    pt4 = a_b3[:, _o + 4 * MINE:_o + 8 * MINE]
    mrow4 = a_b3[:, _o + 8 * MINE:_o + 12 * MINE]
    mrow24 = a_b3[:, _o + 12 * MINE:_o + 16 * MINE]
    mrow = a_b3[:, _o + 16 * MINE:_o + 17 * MINE]
    sqrep4 = a_b3[:, _o + 17 * MINE:_o + 21 * MINE]
    sqrrep4 = a_b3[:, _o + 21 * MINE:_o + 25 * MINE]
    qsc4 = a_b3[:, _o + 25 * MINE:_o + 29 * MINE]
    fcm2 = [a_b3[:, _o + (29 + j) * MINE:_o + (30 + j) * MINE] for j in range(4)]
    fc4 = a_b3[:, _o + 29 * MINE:_o + 33 * MINE]
    _w = _o + 33 * MINE
    wdsl2 = [a_b3[:, _w:_w + IBLK * P], a_b3[:, _w + IBLK * P:_w + 2 * IBLK * P]]
    zqb = a_b3[:, _w + 2 * IBLK * P:_w + 2 * IBLK * P + NMT * D]
    AB3 = _w + 2 * IBLK * P + NMT * D
    
    pm0 = psb("pm0", [P, 512])
    pm1 = psb("pm1", [P, 512])
    pm2 = psb("pm2", [P, 512])
    pm3 = psb("pm3", [P, 512])
    ptrp = psb("ptrp", [P, 2 * P], BF16)
    ptr = ptrp[:, 0:P]
    ptr2 = ptrp[:, P:2 * P]
    ptrw = [psb("ptrw0", [P, DBLK * P], BF16), psb("ptrw1", [P, DBLK * P], BF16)]
    pmq = psb("pmq", [P, 512])

    class Sem:
        def __init__(self, name):
            self.h = nc.alloc_semaphore(name=name)
            self.n = 0

        def inc(self, instr, k=1):
            instr.then_inc(self.h, k)
            self.n += k
            return instr

    s_ld, s_g, s_v, s_s, s_t, s_st, s_bar, s_ge = (Sem(x) for x in
        ("ld", "g", "v", "s", "t", "st", "bar", "ge"))
    s_rp = [Sem("rp0"), Sem("rp1"), Sem("rp2")]
    s_x = [Sem("x0"), Sem("x1"), Sem("x2")]
    s_fw = [Sem("fw0"), Sem("fw1")]
    s_dw = [Sem("dw0"), Sem("dw1")]
    s_gx = [Sem(f"gx{i}") for i in range(6)]
    s_gq3 = [Sem(f"gq{i}") for i in range(3)]
    s_gl3 = [Sem(f"gl{i}") for i in range(3)]

    V, S, TE, G, SY = nc.vector, nc.scalar, nc.tensor, nc.gpsimd, nc.sync

    def vf(instr=None):
        if instr is None:
            instr = V.engine_nop()
        s_v.inc(instr)
        V.wait_ge(s_v.h, s_v.n)
        return instr

    def sf(instr=None):
        if instr is None:
            instr = S.copy(statw[0:1, 13:14], consts[0:1, 7:8])
        s_s.inc(instr)
        S.wait_ge(s_s.h, s_s.n)
        return instr
    ALL_E = ("sync", "vector", "scalar", "tensor", "gpsimd")

    def barrier():
        SY.wait_ge(s_ld.h, s_ld.n)
        SY.wait_ge(s_st.h, s_st.n)
        G.wait_ge(s_g.h, s_g.n)
        G.wait_ge(s_ge.h, s_ge.n)
        for e in ALL_E:
            eng = getattr(nc, e)
            if e != "sync":
                eng.drain()
            eng.sem_inc(s_bar.h, 1)
        s_bar.n += len(ALL_E)
        for e in ALL_E:
            getattr(nc, e).wait_ge(s_bar.h, s_bar.n)

    NG = (D + 511) // 512

    def ln_stats(dst2, src):
        V.wait_ge(s_v.h, s_v.n)
        for gi in range(NG):
            lo, hi = gi * 512, min(D, (gi + 1) * 512)
            V.bn_stats(bns[:, 6 * gi:6 * gi + 6], src[:, lo:hi])
            vf()
        return V.bn_aggr(dst2, bns[:, 0:6 * NG])

    def rsqrt_cols(src_ap, dst_ap, tmp_ap):
        S.activation(tmp_ap, src_ap, AF.Ln, bias=LN_EPS)
        sf()
        return S.activation(dst_ap, tmp_ap, AF.Exp, scale=-0.5)

    ptrw_g = [0, 0]
    gp_v = [0, 0, 0, 0]
    q4v = [0]
    fcm_t = [0, 0, 0, 0]
    pm_g = {i: {"s": 0, "v": 0} for i in range(4)}
    xg = {"a_scal": 0, "zn_s": [0, 0], "znT_rd_t": 0, "bc_st": 0,
          "rp_fin": [0, 0], "acc_rd_t": 0, "yam_rd_s": 0, "um_rd_v": 0,
          "z2q_v": 0, "qs_v": 0, "o_st": [0, 0], "zn_v": 0,
          "pmq_s": 0, "sqT_t": 0, "bct_s": 0, "rep_t": 0, "ln4_s": 0}

    def pm_wait(bank):
        TE.wait_ge(s_s.h, pm_g[bank]["s"])
        TE.wait_ge(s_v.h, pm_g[bank]["v"])

    def finish():
        SY.wait_ge(s_ld.h, s_ld.n)
        SY.wait_ge(s_st.h, s_st.n)
        for cm in reversed(ctxs):
            cm.__exit__(None, None, None)
        return nc

    STOP = cfg.get("stop")

    V.memset(consts[:, 6:7], -0.5)
    vf()
    V.memset(consts[:, 7:8], 1.0)
    vf()
    V.memset(consts[:, 8:9], 0.5)
    vf()
    V.memset(consts[:, 9:10], 0.0)
    vf()
    V.memset(consts[:, 10:11], LN_EPS)
    vf()
    s_v.inc(V.memset(consts[:, 11:12], -0.5))
    for val, col in ((-0.5, 6), (1.0, 7), (0.5, 8), (0.0, 9), (LN_EPS, 10)):
        nc.const_aps.register(F32, val, consts[:, col:col + 1])
    s_ld.inc(SY.dma_start(gam[:], gam_h[:]), 16)
    s_ld.inc(SY.dma_start(bet[:], bet_h[:]), 16)
    s_ld.inc(SY.dma_start(apar[:], apar_h[:]), 16)
    s_g.inc(G.dma_start(drep[:], dpar_h.partition_broadcast(P)), 16)
    s_ld.inc(SY.dma_start(parc[:, 0:1], parf_h.partition_broadcast(P)), 16)
    s_ld.inc(SY.dma_start(scr[0:1, D + 2 * N:2 * D + 2 * N], bdt_h[0:1, 0:D]), 16)
    s_g.inc(G.dma_start(wdtg.ap().rearrange("p (c o) -> p c o", c=DBLK),
                        wdtT_h.rearrange("(c p) o -> p c o", p=P)), 16)
    s_g.inc(G.dma_start(wbg.ap().rearrange("p (c n) -> p c n", c=DBLK),
                        wbT_h.rearrange("(c p) n -> p c n", p=P)), 16)
    s_g.inc(G.dma_start(wcg.ap().rearrange("p (c n) -> p c n", c=DBLK),
                        wcT_h.rearrange("(c p) n -> p c n", p=P)), 16)

    G.wait_ge(s_g.h, s_g.n)
    s_ge.inc(G.memset(onesrow[:], 1.0))
    s_ge.inc(G.memset(sel_all[:], 0.0))
    G.wait_ge(s_ge.h, s_ge.n)
    for n_ in range(N):
        s_ge.inc(G.affine_select(out=sel_all[0:N, n_ * P:(n_ + 1) * P],
                                 in_=sel_all[0:N, n_ * P:(n_ + 1) * P],
                                 compare_op=ALU.not_equal, fill=1.0, base=-n_,
                                 pattern=[[0, P]], channel_multiplier=1))
    s_ge.inc(G.memset(ident[:], 0.0))
    G.wait_ge(s_ge.h, s_ge.n)
    s_ge.inc(G.affine_select(out=ident[:], in_=ident[:], compare_op=ALU.not_equal,
                             fill=1.0, base=0, pattern=[[-1, P]], channel_multiplier=1))

    S.wait_ge(s_ld.h, s_ld.n)
    S.wait_ge(s_v.h, s_v.n)
    S.activation(apar[:], apar[:], AF.Exp)
    sf()
    s_s.inc(S.mul(apar[:], apar[:], -1.0))

    V.wait_ge(s_ld.h, s_ld.n)
    V.wait_ge(s_g.h, s_g.n)
    V.tensor_scalar(parc[:, 1:2], parc[:, 0:1], -1.0, 1.0, ALU.mult, ALU.add)
    vf()
    V.tensor_copy(onescol[:], consts[:, 7:8])
    vf()
    s_v.inc(V.tensor_copy(betac[:, 0:DBLK], bet[:]))
    TE.wait_ge(s_v.h, s_v.n)
    TE.wait_ge(s_g.h, s_g.n)

    for oc in range(0, D, 512):
        w = min(512, D - oc)
        for c in range(DBLK):
            mm = TE.matmul(pm0[0:1, 0:w], betac[:, c:c + 1],
                           wdtg[:, c * D + oc:c * D + oc + w],
                           start=(c == 0), stop=(c == DBLK - 1))
            if c == DBLK - 1:
                s_t.inc(mm)
        V.wait_ge(s_t.h, s_t.n)
        s_v.inc(V.tensor_copy(scr[0:1, oc:oc + w], pm0[0:1, 0:w]))
        TE.wait_ge(s_v.h, s_v.n)
        pm_g[0]["v"] = s_v.n
    for wi, wsrc in enumerate((wbg, wcg)):
        for c in range(DBLK):
            mm = TE.matmul(pm1[0:1, 0:N], betac[:, c:c + 1], wsrc[:, c * N:(c + 1) * N],
                           start=(c == 0), stop=(c == DBLK - 1))
            if c == DBLK - 1:
                s_t.inc(mm)
        V.wait_ge(s_t.h, s_t.n)
        s_v.inc(V.tensor_copy(scr[0:1, D + wi * N:D + (wi + 1) * N], pm1[0:1, 0:N]))
        TE.wait_ge(s_v.h, s_v.n)
        pm_g[1]["v"] = s_v.n
    V.wait_ge(s_ld.h, s_ld.n)
    s_v.inc(V.tensor_tensor(scr[0:1, 0:D], scr[0:1, 0:D], scr[0:1, D + 2 * N:2 * D + 2 * N], ALU.add))
    SY.wait_ge(s_v.h, s_v.n)
    s_st.inc(SY.dma_start(brow_d[0:1, :], scr[0:1, 0:D + 2 * N]), 16)
    SY.wait_ge(s_st.h, s_st.n)
    s_ld.inc(SY.dma_start(bdtc.ap().rearrange("p (one c) -> p one c", one=1),
                          brow_d[0:1, 0:D].rearrange("one (c p) -> p one c", p=P), allow_slow_non_contiguous=True), 16)
    s_ld.inc(SY.dma_start(bbc[0:N, 0:2].rearrange("n (one two) -> n one two", one=1),
                          brow_d[0:1, D:D + 2 * N].rearrange("one (two n) -> n one two", n=N), allow_slow_non_contiguous=True), 16)

    V.wait_ge(s_s.h, s_s.n)
    V.wait_ge(s_ld.h, s_ld.n)
    for c in range(DBLK):
        V.tensor_scalar(wdtg[:, c * D:(c + 1) * D], wdtg[:, c * D:(c + 1) * D],
                        gam[:, c:c + 1], None, ALU.mult)
        V.tensor_scalar(wbg[:, c * N:(c + 1) * N], wbg[:, c * N:(c + 1) * N],
                        gam[:, c:c + 1], None, ALU.mult)
        V.tensor_scalar(wcg[:, c * N:(c + 1) * N], wcg[:, c * N:(c + 1) * N],
                        gam[:, c:c + 1], None, ALU.mult)
    barrier()


    if STOP == "prologue":
        return finish()
    wsc2 = [a_f[:, NMT * D + i * WQC:NMT * D + (i + 1) * WQC] for i in range(6)]
    wqa = [a_b3[:, i * WQC:(i + 1) * WQC] for i in range(6)]
    wqb = [a_b3[:, (6 + i) * WQC:(7 + i) * WQC] for i in range(6)]


    def gen_quant(w_h, wq_d, nrows, slot):
        CH_ROWS = WQC
        chunks = list(range(0, nrows, CH_ROWS))
        bufs = (2 * slot, 2 * slot + 1)
        ldsem = (s_gx[2 * slot], s_gx[2 * slot + 1])
        stsem = (s_fw[0], s_fw[1], s_dw[0])[slot]
        pmx = (pm0, pm1, pm2)[slot]
        sw = 4 * slot
        nmm = 0
        for r0 in chunks:
            rr = min(CH_ROWS, nrows - r0)
            nmm += (rr + 511) // 512
        k = 0
        s_done_abs = [0, 0]
        t_done_abs = [0, 0]
        ld_w = [0, 0]
        for ci, r0 in enumerate(chunks):
            b = ci % 2
            p = bufs[b]
            rr = min(CH_ROWS, nrows - r0)
            w = rr
            G.wait_ge(s_s.h, s_done_abs[b])
            ldsem[b].inc(G.dma_start(wsc2[p][:, 0:w].rearrange("p (a q) -> p a q", q=P),
                                     w_h[r0:r0 + rr, :].rearrange("(p a) q -> p a q", p=P)), 16)
            ld_w[b] = ldsem[b].n
            S.wait_ge(ldsem[b].h, ld_w[b])
            S.wait_ge(s_t.h, t_done_abs[b])
            s_s.inc(S.activation(wqa[p][:, 0:w], wsc2[p][:, 0:w], AF.Abs))
            s_done_abs[b] = s_s.n
            TE.wait_ge(s_s.h, s_done_abs[b])
            if ci == 0:
                pm_wait(slot)
            noff = (w + 511) // 512
            for oi, off in enumerate(range(0, w, 512)):
                ww = min(512, w - off)
                mm = TE.matmul(pmx[0:1, 0:ww], onescol[:], wqa[p][:, off:off + ww],
                               start=(k == 0), stop=(k == nmm - 1))
                if oi == noff - 1:
                    s_t.inc(mm)
                k += 1
            t_done_abs[b] = s_t.n
            yield
        V.wait_ge(s_t.h, s_t.n)
        s_v.inc(V.tensor_reduce(statw[0:1, sw:sw + 1], pmx[0:1, 0:512], AX.X, ALU.add))
        pm_g[slot]["v"] = s_v.n
        V.wait_ge(s_v.h, s_v.n)
        V.tensor_scalar(statw[0:1, sw + 1:sw + 2], statw[0:1, sw:sw + 1],
                        1.0 / float(nrows * P), 1e-5, ALU.mult, ALU.max)
        vf()
        V.reciprocal(statw[0:1, sw + 2:sw + 3], statw[0:1, sw + 1:sw + 2])
        vf()
        s_v.inc(V.tensor_scalar(statw[0:1, sw + 3:sw + 4], statw[0:1, sw + 1:sw + 2],
                                0.5, None, ALU.mult))
        G.wait_ge(s_v.h, s_v.n)
        s_gq3[slot].inc(G.dma_start(wqs_d[slot:slot + 1, 0:2], statw[0:1, sw + 2:sw + 4]), 16)
        G.wait_ge(s_gq3[slot].h, s_gq3[slot].n)
        s_gl3[slot].inc(G.dma_start(consts[:, slot:slot + 1],
                                    wqs_d[slot:slot + 1, 0:1].partition_broadcast(P)), 16)
        s_gl3[slot].inc(G.dma_start(consts[:, 3 + slot:4 + slot],
                                    wqs_d[slot:slot + 1, 1:2].partition_broadcast(P)), 16)
        gl_done = s_gl3[slot].n
        yield
        S.wait_ge(s_gl3[slot].h, gl_done)
        s_done_sg = list(s_done_abs)
        v_done_sg = [0, 0]
        st_all = stsem.n
        for ci, r0 in enumerate(chunks):
            b = ci % 2
            p = bufs[b]
            rr = min(CH_ROWS, nrows - r0)
            w = rr
            G.wait_ge(s_s.h, s_done_sg[b])
            ldsem[b].inc(G.dma_start(wsc2[p][:, 0:w].rearrange("p (a q) -> p a q", q=P),
                                     w_h[r0:r0 + rr, :].rearrange("(p a) q -> p a q", p=P)), 16)
            ld_w[b] = ldsem[b].n
            S.wait_ge(ldsem[b].h, ld_w[b])
            S.wait_ge(s_v.h, v_done_sg[b])
            S.wait_ge(stsem.h, st_all)
            S.activation(wqa[p][:, 0:w], wsc2[p][:, 0:w], AF.Sign,
                         scale=consts[:, slot:slot + 1], bias=consts[:, 6:7])
            s_s.inc(S.activation(wqb[p][:, 0:w], wsc2[p][:, 0:w], AF.Sign,
                                 scale=consts[:, slot:slot + 1], bias=consts[:, 8:9]))
            s_done_sg[b] = s_s.n
            V.wait_ge(s_s.h, s_done_sg[b])
            s_v.inc(V.tensor_tensor(wqa[p][:, 0:w], wqa[p][:, 0:w], wqb[p][:, 0:w],
                                    ALU.add))
            v_done_sg[b] = s_v.n
            SY.wait_ge(s_v.h, v_done_sg[b])
            stsem.inc(SY.dma_start(wq_d[r0:r0 + rr, :].rearrange("(p a) q -> p a q", p=P),
                                   wqa[p][:, 0:w].rearrange("p (a q) -> p a q", q=P)), 16)
            if ci >= 1:
                st_all = stsem.n - 16
            yield
        SY.wait_ge(stsem.h, stsem.n)

    qgens = [gen_quant(wgT_h, wgq_d, INNER // P * D, 0),
             gen_quant(wuT_h, wuq_d, INNER // P * D, 1),
             gen_quant(wdT_h, wdq_d, D // P * INNER, 2)]
    live = list(qgens)
    while live:
        for g in list(live):
            try:
                next(g)
            except StopIteration:
                live.remove(g)
    V.wait_ge(s_gl3[0].h, s_gl3[0].n)
    s_v.inc(V.tensor_scalar(consts[:, 12:13], consts[:, 3:4], 0.5, None, ALU.mult))
    barrier()

    if STOP == "quant":
        return finish()
    s_v.inc(V.memset(state[:], 0.0))
    vf()

    def gen_A(sidx):
        t0 = sidx * TCH
        kt0 = t0 // P
        v_done_x = [0, 0]
        ld_x = [0, 0]
        V.wait_ge(s_s.h, xg["a_scal"])
        for k in range(NT):
            p = k % 2
            if k < 2:
                SY.wait_ge(s_s.h, xg["zn_s"][0])
                SY.wait_ge(s_s.h, xg["zn_s"][1])
            s_x[p].inc(SY.dma_start(xts[:, k * D:(k + 1) * D],
                                    x_h[t0 + k * P:t0 + (k + 1) * P, :]), 16)
            ld_x[p] = s_x[p].n
            V.wait_ge(s_x[p].h, ld_x[p])
            s_v.inc(ln_stats(stats[:, 2 * k:2 * k + 2], xts[:, k * D:(k + 1) * D]))
            v_done_x[p] = s_v.n
            yield
        mview = stats.ap().rearrange("p (k two) -> p two k", two=2)
        mean_v = mview[:, 0, :]
        var_v = mview[:, 1, :]
        S.wait_ge(s_v.h, s_v.n)
        s_s.inc(rsqrt_cols(var_v, sc1[:, 0:NT], statw[:, 0:NT]))
        yield
        V.wait_ge(s_s.h, s_s.n)
        V.tensor_copy(r1v[:, 0:NT], sc1[:, 0:NT])
        vf()
        V.tensor_tensor(statw[:, 0:NT], sc1[:, 0:NT], sc1[:, 0:NT], ALU.mult)
        vf()
        s_v.inc(V.tensor_tensor(statw[:, 0:NT], statw[:, 0:NT], var_v, ALU.mult))
        S.wait_ge(s_v.h, s_v.n)
        s_s.inc(rsqrt_cols(statw[:, 0:NT], statw[:, NT:2 * NT], statw[:, 0:NT]))
        yield
        V.wait_ge(s_s.h, s_s.n)
        V.tensor_tensor(sc1[:, 0:NT], sc1[:, 0:NT], statw[:, NT:2 * NT], ALU.mult)
        vf()
        V.tensor_tensor(sc2[:, 0:NT], mean_v, sc1[:, 0:NT], ALU.mult)
        vf()
        vf()
        V.tensor_scalar(sc2[:, 0:NT], sc2[:, 0:NT], -1.0, None, ALU.mult)
        vf()
        mev = stats.ap().rearrange("p (k par mv) -> p par mv k", par=2, mv=2)
        for (dst, ev, od) in ((mnM2[sidx % 2], mev[:, 0, 0, :], mev[:, 1, 0, :]),
                              (r1M2[sidx % 2], r1v.ap().rearrange("p (k two) -> p two k", two=2)[:, 0, :],
                               r1v.ap().rearrange("p (k two) -> p two k", two=2)[:, 1, :])):
            V.tensor_scalar(statw[:, 0:NMT], od, parc[:, 0:1], None, ALU.mult)
            vf()
            V.scalar_tensor_tensor(dst[:, 0:NMT], ev, parc[:, 1:2],
                                   statw[:, 0:NMT], ALU.mult, ALU.add)
            vf()
        s_v.inc(V.engine_nop())
        S.wait_ge(s_v.h, s_v.n)
        yield
        s_done_zn = [0, 0]
        t_done_zn = [0, 0]
        V.wait_ge(s_t.h, xg["znT_rd_t"])
        znT_v = znT.rearrange("p (c t) -> p c t", c=DBLK)
        for k in range(NT):
            p = k % 2
            S.wait_ge(s_t.h, t_done_zn[p])
            s_s.inc(S.activation(znt2[p][:], xts[:, k * D:(k + 1) * D], AF.Identity,
                                 scale=sc1[:, k:k + 1], bias=sc2[:, k:k + 1]))
            s_done_zn[p] = s_s.n
            TE.wait_ge(s_s.h, s_done_zn[p])
            TE.wait_ge(s_v.h, ptrw_g[p])
            for c in range(DBLK):
                mm = TE.transpose(ptrw[p][:, c * P:(c + 1) * P],
                                  znt2[p][:, c * P:(c + 1) * P], ident[:])
                if c == DBLK - 1:
                    s_t.inc(mm)
            t_done_zn[p] = s_t.n
            V.wait_ge(s_t.h, t_done_zn[p])
            s_v.inc(V.tensor_copy(znT_v[:, :, k * P:(k + 1) * P],
                                  ptrw[p][:, 0:DBLK * P].rearrange("p (c q) -> p c q", c=DBLK)))
            ptrw_g[p] = s_v.n
            yield
        xg["a_scal"] = s_s.n
        xg["zn_s"] = list(s_done_zn)
        xg["zn_v"] = s_v.n

    def gen_B(sidx):
        t0 = sidx * TCH
        kt0 = t0 // P
        t_done_dt = [0, 0]
        s_done_dt = [0, 0]
        NTC = TCH // 512
        fscr2 = bscr2
        TE.wait_ge(s_v.h, xg["zn_v"])
        for c in range(DBLK):
            for ti, tc in enumerate(range(0, TCH, 512)):
                w = min(512, TCH - tc)
                g = (c * NTC + ti) % 2
                pmx = pm0 if g == 0 else pm1
                pm_wait(g)
                for kk in range(DBLK):
                    mm = TE.matmul(pmx[:, 0:w],
                                   wdtg[:, kk * D + c * P:kk * D + (c + 1) * P],
                                   znT[:, kk * TCH + tc:kk * TCH + tc + w],
                                   start=(kk == 0), stop=(kk == DBLK - 1))
                    if kk == DBLK - 1:
                        s_t.inc(mm)
                t_done_dt[g] = s_t.n
                S.wait_ge(s_t.h, t_done_dt[g])
                S.activation(fscr2[g][:, 0:w], pmx[:, 0:w], AF.Exp,
                             bias=bdtc[:, c:c + 1], scale=consts[:, 7:8])
                sf()
                s_s.inc(S.activation(dtT[:, c * TCH + tc:c * TCH + tc + w],
                                     fscr2[g][:, 0:w], AF.Ln, bias=1.0))
                s_done_dt[g] = s_s.n
                pm_g[g]["s"] = s_s.n
                yield
        s_done_bc = 0
        for wi, wsrc in enumerate((wbg, wcg)):
            for ti, tc in enumerate(range(0, TCH, 512)):
                w = min(512, TCH - tc)
                pm_wait(2)
                for kk in range(DBLK):
                    mm = TE.matmul(pm2[0:N, 0:w], wsrc[:, kk * N:(kk + 1) * N],
                                   znT[:, kk * TCH + tc:kk * TCH + tc + w],
                                   start=(kk == 0), stop=(kk == DBLK - 1))
                    if kk == DBLK - 1:
                        s_t.inc(mm)
                S.wait_ge(s_t.h, s_t.n)
                S.wait_ge(s_t.h, xg["rep_t"])
                s_s.inc(S.activation(bcT_all[0:N, wi * TCH + tc:wi * TCH + tc + w],
                                     pm2[0:N, 0:w], AF.Identity,
                                     bias=bbc[0:N, wi:wi + 1], scale=consts[0:N, 7:8]))
                s_done_bc = s_s.n
                pm_g[2]["s"] = s_s.n
                yield
        xg["bct_s"] = s_done_bc
        V.wait_ge(s_s.h, s_s.n)
        for c in range(DBLK):
            V.tensor_scalar(wpT[:, c * TCH:(c + 1) * TCH], znT[:, c * TCH:(c + 1) * TCH],
                            gam[:, c:c + 1], bet[:, c:c + 1], ALU.mult, ALU.add)
            vf()
            s_v.inc(V.tensor_tensor(wpT[:, c * TCH:(c + 1) * TCH],
                                    wpT[:, c * TCH:(c + 1) * TCH],
                                    dtT[:, c * TCH:(c + 1) * TCH], ALU.mult))
            yield
        xg["znT_rd_t"] = s_t.n
        xg["bc_st"] = s_st.n

    def gen_C(sidx):
        t0 = sidx * TCH
        kt0 = t0 // P
        v_done_rep = [0] * 5
        v_done_ab = [0] * 5
        s_done_ab = [0] * 5
        v_done_ht = [0, 0]
        g_done_ht = [0, 0]
        s_done_rep = [0] * 5
        V.wait_ge(s_t.h, xg["acc_rd_t"])
        V.wait_ge(s_v.h, s_v.n)
        for n in range(N + 1):
            p = n % 5
            if n < N:
                TE.wait_ge(s_s.h, xg["pmq_s"])
                TE.wait_ge(s_s.h, xg["bct_s"])
                s_t.inc(TE.matmul(pmq[:, 0:TCH], sel_all[0:N, n * P:(n + 1) * P],
                                  bcT_all[0:N, 0:TCH], start=True, stop=True))
                S.wait_ge(s_t.h, s_t.n)
                S.wait_ge(s_v.h, v_done_rep[p])
                s_s.inc(S.activation(brep2[p][:, 0:TCH], pmq[:, 0:TCH], AF.Identity,
                                     scale=consts[:, 7:8], bias=consts[:, 9:10]))
                xg["pmq_s"] = s_s.n
                TE.wait_ge(s_s.h, xg["pmq_s"])
                s_t.inc(TE.matmul(pmq[:, 0:TCH], sel_all[0:N, n * P:(n + 1) * P],
                                  bcT_all[0:N, TCH:2 * TCH], start=True, stop=True))
                xg["rep_t"] = s_t.n
                S.wait_ge(s_t.h, s_t.n)
                s_s.inc(S.activation(crep2[p][:, 0:TCH], pmq[:, 0:TCH], AF.Identity,
                                     scale=consts[:, 7:8], bias=consts[:, 9:10]))
                xg["pmq_s"] = s_s.n
                s_done_rep[p] = s_s.n
            if n == 0:
                continue
            nn = n - 1
            pp = nn % 5
            for c0 in range(0, DBLK, 2):
                cpair = (c0, c0 + 1)
                for c in cpair:
                    ac = c % 5
                    S.wait_ge(s_v.h, v_done_ab[ac])
                    s_s.inc(S.activation(abar2[ac][:, 0:TCH],
                                         dtT[:, c * TCH:(c + 1) * TCH], AF.Exp,
                                         scale=apar[:, c * N + nn:c * N + nn + 1],
                                         bias=consts[:, 9:10]))
                    s_done_ab[ac] = s_s.n
                V.wait_ge(s_s.h, s_done_rep[pp])
                V.wait_ge(s_s.h, s_done_ab[cpair[1] % 5])
                V.wait_ge(s_ge.h, g_done_ht[0])
                V.wait_ge(s_ge.h, g_done_ht[1])
                V.wait_ge(s_v.h, v_done_ht[0])
                V.wait_ge(s_v.h, v_done_ht[1])
                bx_n = [0, 0]
                sc_n = [0, 0]
                ch_n = [0, 0]
                for c in cpair:
                    pc = c % 2
                    s_v.inc(V.tensor_tensor(bxt2[pc][:, 0:TCH],
                                            wpT[:, c * TCH:(c + 1) * TCH],
                                            brep2[pp][:, 0:TCH], ALU.mult))
                    bx_n[pc] = s_v.n
                for c in cpair:
                    pc = c % 2
                    ac = c % 5
                    V.wait_ge(s_v.h, bx_n[pc])
                    s_v.inc(V.tensor_tensor_scan(ht2[pc][:, 0:TCH],
                                                 abar2[ac][:, 0:TCH],
                                                 bxt2[pc][:, 0:TCH],
                                                 state[:, c * N + nn:c * N + nn + 1],
                                                 ALU.mult, ALU.add))
                    sc_n[pc] = s_v.n
                    v_done_ab[ac] = s_v.n
                for c in cpair:
                    pc = c % 2
                    V.wait_ge(s_v.h, sc_n[pc])
                    s_v.inc(V.tensor_tensor(cht2[pc][:, 0:TCH], ht2[pc][:, 0:TCH],
                                            crep2[pp][:, 0:TCH], ALU.mult))
                    ch_n[pc] = s_v.n
                for c in cpair:
                    pc = c % 2
                    V.wait_ge(s_v.h, ch_n[pc])
                    if nn == 0:
                        s_v.inc(V.tensor_tensor(accs[:, c * TCH:(c + 1) * TCH],
                                                ht2[pc][:, 0:TCH], crep2[pp][:, 0:TCH],
                                                ALU.mult))
                    else:
                        s_v.inc(V.tensor_tensor(accs[:, c * TCH:(c + 1) * TCH],
                                                accs[:, c * TCH:(c + 1) * TCH],
                                                cht2[pc][:, 0:TCH], ALU.add))
                    v_done_ht[pc] = s_v.n
                    G.wait_ge(s_v.h, v_done_ht[pc])
                    s_ge.inc(G.tensor_copy(state[:, c * N + nn:c * N + nn + 1],
                                           ht2[pc][:, TCH - 1:TCH]))
                    g_done_ht[pc] = s_ge.n
                yield
            v_done_rep[pp] = s_v.n
        V.wait_ge(s_ge.h, s_ge.n)
        V.wait_ge(s_s.h, xg["yam_rd_s"])
        for c in range(DBLK):
            p = c % 2
            TE.wait_ge(s_v.h, max(ptrw_g[p], s_v.n if c < 2 else 0))
            for k in range(NT):
                mm = TE.transpose(ptrw[p][:, k * P:(k + 1) * P],
                                  accs[:, c * TCH + k * P:c * TCH + (k + 1) * P],
                                  ident[:])
                if k == NT - 1:
                    s_t.inc(mm)
            V.wait_ge(s_t.h, s_t.n)
            pv = ptrw[p][:, 0:NT * P].rearrange("p (k two q) -> p k two q", two=2, q=P)
            vf(V.tensor_scalar(ytd[:, 0:MINE].rearrange("p (k q) -> p k q", q=P),
                               pv[:, :, 1, :], parc[:, 0:1], None, ALU.mult))
            yv = yam.rearrange("p (k d) -> p k d", k=NMT)
            s_v.inc(V.scalar_tensor_tensor(yv[:, :, c * P:(c + 1) * P],
                                           pv[:, :, 0, :], parc[:, 1:2],
                                           ytd[:, 0:MINE].rearrange("p (k q) -> p k q", q=P),
                                           ALU.mult, ALU.add))
            ptrw_g[p] = s_v.n
            yield
        xg["acc_rd_t"] = s_t.n

    def gen_C2(sidx):
        t0 = sidx * TCH
        kt0 = t0 // P
        V.wait_ge(s_s.h, xg["ln4_s"])
        SY.wait_ge(s_v.h, xg["um_rd_v"])
        for k in range(NMT):
            s_ld.inc(SY.dma_start(um[:, k * D:(k + 1) * D],
                                  xm_h[sidx * MINE + k * P:sidx * MINE + (k + 1) * P, :]), 16)
        yield
        V.wait_ge(s_t.h, s_t.n)
        mnM = mnM2[sidx % 2]
        r1M = r1M2[sidx % 2]
        vf(V.tensor_tensor(statw2[:, 0:NMT], mnM[:, 0:NMT], r1M[:, 0:NMT], ALU.mult))
        s_v.inc(V.tensor_scalar(statw2[:, 0:NMT], statw2[:, 0:NMT], -1.0, None, ALU.mult))
        S.wait_ge(s_v.h, s_v.n)
        S.wait_ge(s_ld.h, s_ld.n)
        for k in range(NMT):
            mm = S.activation(z1m[:, k * D:(k + 1) * D], um[:, k * D:(k + 1) * D],
                              AF.Identity, scale=r1M[:, k:k + 1], bias=statw2[:, k:k + 1])
            if k == NMT - 1:
                s_s.inc(mm)
        yield
        V.wait_ge(s_s.h, s_s.n)
        for k in range(NMT):
            V.tensor_tensor(z1m[:, k * D:(k + 1) * D], z1m[:, k * D:(k + 1) * D],
                            drep[:], ALU.mult)
        vf()
        for k in range(NMT):
            V.tensor_tensor(yam[:, k * D:(k + 1) * D], yam[:, k * D:(k + 1) * D],
                            z1m[:, k * D:(k + 1) * D], ALU.add)
        vf()
        for k in range(NMT):
            s_v.inc(ln_stats(ln3s[:, 2 * k:2 * k + 2], yam[:, k * D:(k + 1) * D]))
            yield
        S.wait_ge(s_v.h, s_v.n)
        l3v = ln3s.ap().rearrange("p (k two) -> p two k", two=2)
        s_s.inc(rsqrt_cols(l3v[:, 1, :], sxv[:, 0:NMT], statw2[:, 0:NMT]))
        V.wait_ge(s_s.h, s_s.n)
        vf(V.tensor_tensor(statw2[:, 0:NMT], l3v[:, 0, :], sxv[:, 0:NMT], ALU.mult))
        s_v.inc(V.tensor_scalar(statw2[:, 0:NMT], statw2[:, 0:NMT], -1.0, None, ALU.mult))
        S.wait_ge(s_v.h, s_v.n)
        for k in range(NMT):
            mm = S.activation(fTM[:, k * D:(k + 1) * D], yam[:, k * D:(k + 1) * D],
                              AF.Identity, scale=sxv[:, k:k + 1], bias=statw2[:, k:k + 1])
            if k == NMT - 1:
                s_s.inc(mm)
        xg["yam_rd_s"] = s_s.n
        yield
        V.wait_ge(s_s.h, s_s.n)
        for k in range(NMT):
            mm = V.tensor_tensor(um[:, k * D:(k + 1) * D], um[:, k * D:(k + 1) * D],
                                 fTM[:, k * D:(k + 1) * D], ALU.add)
            if k == NMT - 1:
                s_v.inc(mm)
        S.wait_ge(s_v.h, s_v.n)
        SY.wait_ge(s_v.h, s_v.n)

    def gen_D_head(sidx):
        t0 = sidx * TCH
        kt0 = t0 // P
        zq = zqb
        V.wait_ge(s_s.h, xg["ln4_s"])
        for k in range(NMT):
            s_v.inc(ln_stats(ln2s[:, 2 * k:2 * k + 2], um[:, k * D:(k + 1) * D]))
            yield
        l2v = ln2s.ap().rearrange("p (k two) -> p two k", two=2)
        S.wait_ge(s_v.h, s_v.n)
        s_s.inc(rsqrt_cols(l2v[:, 1, :], sxv[:, 0:NMT], statw2[:, 0:NMT]))
        V.wait_ge(s_s.h, s_s.n)
        vf(V.tensor_tensor(statw2[:, 0:NMT], l2v[:, 0, :], sxv[:, 0:NMT], ALU.mult))
        s_v.inc(V.tensor_scalar(statw2[:, 0:NMT], statw2[:, 0:NMT], -1.0, None, ALU.mult))
        S.wait_ge(s_v.h, s_v.n)
        for k in range(NMT):
            mm = S.activation(fTM[:, k * D:(k + 1) * D], um[:, k * D:(k + 1) * D],
                              AF.Identity, scale=sxv[:, k:k + 1], bias=statw2[:, k:k + 1])
            if k == NMT - 1:
                s_s.inc(mm)
        yield
        V.wait_ge(s_s.h, s_s.n)
        for k in range(NMT):
            V.tensor_reduce(statw2[:, k:k + 1], fTM[:, k * D:(k + 1) * D], AX.X,
                            ALU.max, apply_absolute_value=True)
        vf()
        yield
        V.tensor_scalar(statw2[:, 8:8 + NMT], statw2[:, 0:NMT], 1e-5, 1.0 / 127.0,
                        ALU.max, ALU.mult)
        vf()
        V.reciprocal(statw2[:, 0:NMT], statw2[:, 8:8 + NMT])
        vf()
        for k in range(NMT):
            V.tensor_scalar(fTM[:, k * D:(k + 1) * D], fTM[:, k * D:(k + 1) * D],
                            statw2[:, k:k + 1], MAGIC, ALU.mult, ALU.add)
        vf()
        for k in range(NMT):
            mm = V.tensor_scalar(zq[:, k * D:(k + 1) * D], fTM[:, k * D:(k + 1) * D],
                                 -MAGIC, statw2[:, 8 + k:9 + k], ALU.add, ALU.mult)
            if k == NMT - 1:
                s_v.inc(mm)
        TE.wait_ge(s_v.h, s_v.n)
        z2qT_v = z2qT.rearrange("p (c t) -> p c t", c=DBLK)
        for k in range(NMT):
            p = k % 2
            TE.wait_ge(s_v.h, ptrw_g[p])
            for c in range(DBLK):
                mm = TE.transpose(ptrw[p][:, c * P:(c + 1) * P],
                                  zq[:, k * D + c * P:k * D + (c + 1) * P], ident[:])
                if c == DBLK - 1:
                    s_t.inc(mm)
            V.wait_ge(s_t.h, s_t.n)
            s_v.inc(V.tensor_copy(z2qT_v[:, :, k * P:(k + 1) * P],
                                  ptrw[p][:, 0:DBLK * P].rearrange("p (c q) -> p c q", c=DBLK)))
            ptrw_g[p] = s_v.n
            yield
        xg["z2q_v"] = s_v.n
        TE.wait_ge(s_v.h, xg["z2q_v"])

    def gen_D(sidx):
        zq = zqb
        t_done_fw = [0, 0]
        s_done_gu = [0, 0, 0, 0]
        v_done_gp = gp_v
        ld_fw = [0, 0]

        def gu_load(ic):
            p = ic % 2
            SY.wait_ge(s_t.h, t_done_fw[p])
            s_fw[p].inc(SY.dma_start(fw2[p][:, 0:DBLK * P].rearrange("p (kk o) -> p kk o", kk=DBLK),
                                     wgq_d[ic * D:(ic + 1) * D, :].rearrange(
                                         "(p kk) o -> p kk o", p=P)), 16)
            s_fw[p].inc(SY.dma_start(fu2[p][:, 0:DBLK * P].rearrange("p (kk o) -> p kk o", kk=DBLK),
                                     wuq_d[ic * D:(ic + 1) * D, :].rearrange(
                                         "(p kk) o -> p kk o", p=P)), 16)
            ld_fw[p] = s_fw[p].n

        gu_load(0)
        for ic in range(IBLK):
            p = ic % 2
            q = ic % 4
            if ic + 1 < IBLK:
                gu_load(ic + 1)
            TE.wait_ge(s_fw[p].h, ld_fw[p])
            pmg = pm0 if p == 0 else pm2
            pmu = pm1 if p == 0 else pm3
            for tc in range(0, MINE, 512):
                w = min(512, MINE - tc)
                pm_wait(0 if p == 0 else 2)
                pm_wait(1 if p == 0 else 3)
                for kk in range(DBLK):
                    mm = TE.matmul(pmg[:, tc % 512:tc % 512 + w], fw2[p][:, kk * P:(kk + 1) * P],
                                   z2qT[:, kk * MINE + tc:kk * MINE + tc + w],
                                   start=(kk == 0), stop=(kk == DBLK - 1))
                    if kk == DBLK - 1:
                        s_t.inc(mm)
                for kk in range(DBLK):
                    mm = TE.matmul(pmu[:, tc % 512:tc % 512 + w], fu2[p][:, kk * P:(kk + 1) * P],
                                   z2qT[:, kk * MINE + tc:kk * MINE + tc + w],
                                   start=(kk == 0), stop=(kk == DBLK - 1))
                    if kk == DBLK - 1:
                        s_t.inc(mm)
                t_done_fw[p] = s_t.n
                S.wait_ge(s_t.h, t_done_fw[p])
                S.wait_ge(s_v.h, v_done_gp[q])
                s_s.inc(S.activation(gt2[q][:, tc:tc + w], pmg[:, tc % 512:tc % 512 + w],
                                     AF.Identity, scale=consts[:, 12:13],
                                     bias=consts[:, 9:10]))
                s_s.inc(S.activation(pt2[q][:, tc:tc + w], pmu[:, tc % 512:tc % 512 + w],
                                     AF.Identity, scale=consts[:, 4:5], bias=consts[:, 9:10]))
                s_done_gu[q] = s_s.n
                pm_g[0 if p == 0 else 2]["s"] = s_s.n
                pm_g[1 if p == 0 else 3]["s"] = s_s.n
            yield
            if q == 3:
                S.wait_ge(s_v.h, q4v[0])
                S.wait_ge(s_s.h, s_done_gu[q])
                s_s.inc(S.activation(sqrep4[:], gt4[:], AF.Tanh))
                V.wait_ge(s_s.h, s_s.n)
                vf(V.scalar_tensor_tensor(gt4[:], sqrep4[:], 1.0, gt4[:],
                                          ALU.add, ALU.mult))
                q4v[0] = s_v.n
                vf(V.tensor_tensor(qs[:, (ic - 3) * MINE:(ic + 1) * MINE], gt4[:],
                                   pt4[:], ALU.mult))
                for j in range(4):
                    v_done_gp[j] = s_v.n
                if ic == 3:
                    V.tensor_copy(mrow4[:], qs[:, 0:4 * MINE])
                    s_v.inc(V.tensor_copy(mrow24[:], qs[:, 0:4 * MINE]))
                else:
                    V.tensor_tensor(mrow4[:], mrow4[:],
                                    qs[:, (ic - 3) * MINE:(ic + 1) * MINE], ALU.max)
                    s_v.inc(V.tensor_tensor(mrow24[:], mrow24[:],
                                            qs[:, (ic - 3) * MINE:(ic + 1) * MINE],
                                            ALU.min))
                vf()
            yield
        vf(V.scalar_tensor_tensor(mrow4[:], mrow24[:], -1.0, mrow4[:],
                                  ALU.mult, ALU.max))
        vf(V.tensor_tensor(mrow4[:, 0:2 * MINE], mrow4[:, 0:2 * MINE],
                           mrow4[:, 2 * MINE:4 * MINE], ALU.max))
        vf(V.tensor_tensor(mrow[:, 0:MINE], mrow4[:, 0:MINE],
                           mrow4[:, MINE:2 * MINE], ALU.max))
        TE.wait_ge(s_v.h, s_v.n)
        for k in range(NMT):
            s_t.inc(TE.transpose(ptr[:, 0:P], mrow[:, k * P:(k + 1) * P], ident[:]))
            V.wait_ge(s_t.h, s_t.n)
            V.tensor_reduce(sqv[:, k:k + 1], ptr[:, 0:P], AX.X, ALU.max,
                            apply_absolute_value=True)
            vf()
            s_v.inc(V.tensor_copy(statw2[:, 15:16], sqv[:, k:k + 1]))
            TE.wait_ge(s_v.h, s_v.n)
            yield
        V.tensor_scalar(sqvr[:, 0:NMT], sqv[:, 0:NMT], 1e-5, 1.0 / 127.0,
                        ALU.max, ALU.mult)
        vf()
        V.reciprocal(sqv[:, 0:NMT], sqvr[:, 0:NMT])
        vf()
        V.wait_ge(s_t.h, xg["sqT_t"])
        V.tensor_copy(sqstg[:, 0:NMT], sqv[:, 0:NMT])
        vf()
        s_v.inc(V.tensor_copy(sqstg[:, NMT:2 * NMT], sqvr[:, 0:NMT]))
        TE.wait_ge(s_v.h, s_v.n)
        TE.wait_ge(s_v.h, ptrw_g[0])
        for j in range(2 * NMT):
            mm = TE.transpose(ptrw[0][0:1, j * P:(j + 1) * P], sqstg[:, j:j + 1], ident[:])
            if j == 2 * NMT - 1:
                s_t.inc(mm)
        xg["sqT_t"] = s_t.n
        S.wait_ge(s_t.h, s_t.n)
        s_s.inc(S.activation(sqTrow[0:1, 0:2 * NMT * P], ptrw[0][0:1, 0:2 * NMT * P],
                             AF.Identity, scale=consts[0:1, 7:8], bias=consts[0:1, 9:10]))
        V.wait_ge(s_s.h, s_s.n)
        s_v.inc(V.engine_nop())
        ptrw_g[0] = s_v.n
        TE.wait_ge(s_s.h, s_s.n)
        TE.wait_ge(s_s.h, xg["pmq_s"])
        for k in range(NMT):
            s_t.inc(TE.matmul(pmq[:, k * P:(k + 1) * P], onesrow[0:1, 0:P],
                              sqTrow[0:1, k * P:(k + 1) * P], start=True, stop=True))
            s_t.inc(TE.matmul(pmq[:, MINE + k * P:MINE + (k + 1) * P], onesrow[0:1, 0:P],
                              sqTrow[0:1, (NMT + k) * P:(NMT + k + 1) * P],
                              start=True, stop=True))
        S.wait_ge(s_t.h, s_t.n)
        S.wait_ge(s_v.h, xg["qs_v"])
        S.wait_ge(s_v.h, q4v[0])
        RB = 4 if 4 * MINE <= NMT * D else 2
        for rep in range(RB):
            s_s.inc(S.activation(sqrep4[:, rep * MINE:(rep + 1) * MINE],
                                 pmq[:, 0:MINE], AF.Identity,
                                 scale=consts[:, 7:8], bias=consts[:, 9:10]))
            s_s.inc(S.activation(sqrrep4[:, rep * MINE:(rep + 1) * MINE],
                                 pmq[:, MINE:2 * MINE], AF.Identity,
                                 scale=consts[:, 7:8], bias=consts[:, 9:10]))
        xg["pmq_s"] = s_s.n
        yield
        V.wait_ge(s_s.h, xg["pmq_s"])
        RW = RB * MINE
        for jj in range(IBLK // RB):
            sl = qs[:, jj * RW:(jj + 1) * RW]
            V.tensor_tensor(fTM[:, 0:RW], sl, sqrep4[:, 0:RW], ALU.mult)
            vf()
            V.tensor_scalar(fTM[:, 0:RW], fTM[:, 0:RW], MAGIC, -MAGIC,
                            ALU.add, ALU.add)
            vf()
            V.tensor_copy(qsc4[:, 0:RW], fTM[:, 0:RW])
            vf()
            s_v.inc(V.tensor_tensor(sl, qsc4[:, 0:RW], sqrrep4[:, 0:RW], ALU.mult))
            yield
        xg["qs_v"] = s_v.n
        TE.wait_ge(s_v.h, xg["qs_v"])
        t_done_wd = [0, 0]
        s_done_dn = [0, 0]
        v_done_fc = [0, 0]
        ld_wd = [0, 0]

        def dn_load(oc):
            p = oc % 2
            SY.wait_ge(s_t.h, t_done_wd[p])
            s_dw[p].inc(SY.dma_start(wdsl2[p][:, 0:IBLK * P].rearrange("p (kk o) -> p kk o", kk=IBLK),
                                     wdq_d[oc * INNER:(oc + 1) * INNER, :].rearrange(
                                         "(p kk) o -> p kk o", p=P)), 16)
            ld_wd[p] = s_dw[p].n

        dn_load(0)
        for oc in range(DBLK):
            p = oc % 2
            fq = oc % 4
            if oc + 1 < DBLK:
                dn_load(oc + 1)
            TE.wait_ge(s_dw[p].h, ld_wd[p])
            pmx = pm0 if p == 0 else pm1
            for tc in range(0, MINE, 512):
                w = min(512, MINE - tc)
                pm_wait(p)
                for kk in range(IBLK):
                    mm = TE.matmul(pmx[:, 0:w], wdsl2[p][:, kk * P:(kk + 1) * P],
                                   qs[:, kk * MINE + tc:kk * MINE + tc + w],
                                   start=(kk == 0), stop=(kk == IBLK - 1))
                    if kk == IBLK - 1:
                        s_t.inc(mm)
                t_done_wd[p] = s_t.n
                S.wait_ge(s_t.h, t_done_wd[p])
                S.wait_ge(s_v.h, gp_v[fq])
                S.wait_ge(s_t.h, fcm_t[fq])
                s_s.inc(S.activation(fcm2[fq][:, tc:tc + w], pmx[:, 0:w], AF.Identity,
                                     scale=consts[:, 5:6], bias=consts[:, 9:10]))
                s_done_dn[p] = s_s.n
                pm_g[p]["s"] = s_s.n
            yield
            TE.wait_ge(s_s.h, s_done_dn[p])
            TE.wait_ge(s_v.h, ptrw_g[p])
            ptw = ptrw[p]
            for k in range(NMT):
                mm = TE.transpose(ptw[:, k * P:(k + 1) * P], fcm2[fq][:, k * P:(k + 1) * P],
                                  ident[:])
                if k == NMT - 1:
                    s_t.inc(mm)
            t_done_wd[p] = s_t.n
            fcm_t[fq] = s_t.n
            V.wait_ge(s_t.h, t_done_wd[p])
            fTM_v = fTM.rearrange("p (k d) -> p k d", k=NMT)
            s_v.inc(V.tensor_copy(fTM_v[:, :, oc * P:(oc + 1) * P],
                                  ptw[:, 0:NMT * P].rearrange("p (k q) -> p k q", k=NMT)))
            ptrw_g[p] = s_v.n
            yield
        vf()
        for k in range(NMT):
            s_v.inc(ln_stats(ln2s[:, 2 * k:2 * k + 2], fTM[:, k * D:(k + 1) * D]))
        S.wait_ge(s_v.h, s_v.n)
        l4v = ln2s.ap().rearrange("p (k two) -> p two k", two=2)
        s_s.inc(rsqrt_cols(l4v[:, 1, :], sxv[:, 0:NMT], statw2[:, 0:NMT]))
        V.wait_ge(s_s.h, s_s.n)
        vf(V.tensor_tensor(statw2[:, 0:NMT], l4v[:, 0, :], sxv[:, 0:NMT], ALU.mult))
        s_v.inc(V.tensor_scalar(statw2[:, 0:NMT], statw2[:, 0:NMT], -1.0, None, ALU.mult))
        S.wait_ge(s_v.h, s_v.n)
        for k in range(NMT):
            mm = S.activation(zq[:, k * D:(k + 1) * D], fTM[:, k * D:(k + 1) * D],
                              AF.Identity, scale=sxv[:, k:k + 1], bias=statw2[:, k:k + 1])
            if k == NMT - 1:
                s_s.inc(mm)
        xg["ln4_s"] = s_s.n
        V.wait_ge(s_s.h, s_s.n)
        st_done_o = list(xg["o_st"])
        osc2 = [oscr, fscr]
        for k in range(NMT):
            p = k % 2
            V.wait_ge(s_st.h, st_done_o[p])
            s_v.inc(V.tensor_tensor(osc2[p][:, 0:D], um[:, k * D:(k + 1) * D],
                                    zq[:, k * D:(k + 1) * D], ALU.add))
            SY.wait_ge(s_v.h, s_v.n)
            s_st.inc(SY.dma_start(out_h[sidx * MINE + k * P:sidx * MINE + (k + 1) * P, :],
                                  osc2[p][:, 0:D]), 16)
            st_done_o[p] = s_st.n
            yield
        xg["o_st"] = list(st_done_o)
        xg["um_rd_v"] = s_v.n


    def gen_MIX(sidx):
        for _ in gen_A(sidx):
            yield
        for _ in gen_B(sidx):
            yield
        for _ in gen_C(sidx):
            yield

    def gen_DFull(sidx):
        for _ in gen_C2(sidx):
            yield
        for _ in gen_D_head(sidx):
            yield
        for _ in gen_D(sidx):
            yield

    def drive(mix, dside, rm=3, rd=4):
        di = 0
        while True:
            for _ in range(rm):
                try:
                    next(mix)
                except StopIteration:
                    return dside[di:]
            for _ in range(rd):
                if di < len(dside):
                    try:
                        next(dside[di])
                    except StopIteration:
                        di += 1

    dside = []
    for s in range(NSUP):
        if s >= 1:
            dside.append(gen_DFull(s - 1))
        dside = drive(gen_MIX(s), dside)
    dside.append(gen_DFull(NSUP - 1))
    for g in dside:
        for _ in g:
            pass
    barrier()

    SY.wait_ge(s_st.h, s_st.n)
    for cm in reversed(ctxs):
        cm.__exit__(None, None, None)
    return nc


def _prep_inputs(inputs, cfg):
    D, N = cfg["D"], cfg["N"]
    DBLK = D // P
    f = np.float32

    def arr(k):
        return np.asarray(inputs[k], f)

    gam = np.ascontiguousarray(arr("L0_attn_gamma").reshape(DBLK, P).T)
    bet = np.ascontiguousarray(arr("L0_attn_beta").reshape(DBLK, P).T)
    a_pp = np.ascontiguousarray(
        arr("L0_attn_A_log").reshape(DBLK, P, N).transpose(1, 0, 2).reshape(P, DBLK * N))
    return {
        "wdtT": np.ascontiguousarray(arr("L0_attn_W_dt").T),
        "wbT": np.ascontiguousarray(arr("L0_attn_W_B").T),
        "wcT": np.ascontiguousarray(arr("L0_attn_W_C").T),
        "gam_pp": gam, "bet_pp": bet, "a_pp": a_pp,
        "bdt_row": np.ascontiguousarray(arr("L0_attn_b_dt").reshape(1, D)),
        "dparam_row": np.ascontiguousarray(arr("L0_attn_D_param").reshape(1, D)),
        "wgT": np.ascontiguousarray(
            arr("L0_ffn_w_gate").T.reshape(D // P, P, -1, P)
            .transpose(2, 1, 0, 3).reshape(-1, P)),
        "wuT": np.ascontiguousarray(
            arr("L0_ffn_w_up").T.reshape(D // P, P, -1, P)
            .transpose(2, 1, 0, 3).reshape(-1, P)),
        "wdT": np.ascontiguousarray(
            arr("L0_ffn_w_down").T.reshape(-1, P, D // P, P)
            .transpose(2, 1, 0, 3).reshape(-1, P)),
    }


def kernel(**inputs):
    x = np.asarray(inputs["L0_attn_x"], np.float32)
    B, T, D = x.shape
    cfg = {"T": T, "D": D, "N": int(inputs["L0_attn_A_log"].shape[1]),
           "INNER": int(inputs["L0_ffn_w_gate"].shape[0]), "TCH": 512}
    nc = build_program(cfg)
    base = _prep_inputs(inputs, cfg)
    in_maps = []
    for c in range(8):
        b, j = c // 2, c % 2
        m = dict(base)
        m["x"] = np.ascontiguousarray(x[b])
        xb = x[b].reshape(T // P, P, D)
        m["xm"] = np.ascontiguousarray(xb[j::2].reshape(T // 2, D))
        m["parf"] = np.full((1, 1), j, np.float32)
        in_maps.append(m)
    res = run_bass_kernel_spmd(nc, in_maps, core_ids=list(range(8)))
    out = np.empty((B, T, D), np.float32)
    TCH = cfg["TCH"]
    MINE = TCH // 2
    NMT = MINE // P
    for c in range(8):
        b, j = c // 2, c % 2
        o = np.asarray(res.results[c]["out"])
        for s in range(T // TCH):
            for k in range(NMT):
                trow = s * TCH + (2 * k + j) * P
                out[b, trow:trow + P, :] = o[s * MINE + k * P:s * MINE + (k + 1) * P, :]
    return out


# version 40
# speedup vs baseline: 1.0190x; 1.0010x over previous
import numpy as np
import concourse.bass as bass
import concourse.mybir as mybir
from concourse.bass_utils import run_bass_kernel_spmd

F32 = mybir.dt.float32
BF16 = mybir.dt.bfloat16
I32 = mybir.dt.int32
AF = mybir.ActivationFunctionType
ALU = mybir.AluOpType
AX = mybir.AxisListType
ts = bass.ts

MAGIC = 12582912.0
LN_EPS = 1e-5
P = 128


def build_program(cfg):
    T, D, N, INNER = cfg["T"], cfg["D"], cfg["N"], cfg["INNER"]
    TCH = cfg["TCH"]
    DBG = cfg.get("debug", False)
    DBLK = D // P
    NSUP = T // TCH
    NT = TCH // P
    MINE = TCH // 2
    NMT = MINE // P
    IBLK = INNER // P
    WQC = min(INNER, 1024)

    nc = bass.Bass("TRN2", target_bir_lowering=False)

    def din(name, shape, dtype=F32):
        return nc.dram_tensor(name, shape, dtype, kind="ExternalInput").ap()

    x_h = din("x", [T, D])
    xm_h = din("xm", [T // 2, D])
    parf_h = din("parf", [1, 1])
    wdtT_h = din("wdtT", [D, D])
    wbT_h = din("wbT", [D, N])
    wcT_h = din("wcT", [D, N])
    gam_h = din("gam_pp", [P, DBLK])
    bet_h = din("bet_pp", [P, DBLK])
    bdt_h = din("bdt_row", [1, D])
    apar_h = din("a_pp", [P, DBLK * N])
    dpar_h = din("dparam_row", [1, D])
    wgT_h = din("wgT", [INNER // P * D, P])
    wuT_h = din("wuT", [INNER // P * D, P])
    wdT_h = din("wdT", [D // P * INNER, P])

    out_h = nc.dram_tensor("out", [NSUP * MINE, D], F32, kind="ExternalOutput").ap()

    def dsc(name, shape, dtype):
        return nc.dram_tensor(name, shape, dtype, kind="Internal").ap()

    bT_d = dsc("bT_d", [N, TCH], BF16)
    cT_d = dsc("cT_d", [N, TCH], BF16)
    wgq_d = dsc("wgq_d", [INNER // P * D, P], BF16)
    wuq_d = dsc("wuq_d", [INNER // P * D, P], BF16)
    wdq_d = dsc("wdq_d", [D // P * INNER, P], BF16)
    brow_d = dsc("brow_d", [1, D + 2 * N], F32)
    sq_d = dsc("sq_d", [2, MINE], BF16)
    wqs_d = dsc("wqs_d", [3, 2], F32)

    dbg = {}
    if DBG:
        for nm, shape in [("d_znT", [D, TCH]), ("d_dtT", [D, TCH]),
                          ("d_bT", [N, TCH]), ("d_cT", [N, TCH]),
                          ("d_wT", [D, TCH]), ("d_y", [TCH, D]),
                          ("d_u", [MINE, D]), ("d_z2q", [D, MINE]),
                          ("d_q", [INNER, MINE]), ("d_wgq", [D, INNER]),
                          ("d_f", [MINE, D]),
                          ("d_h", [P, TCH])]:
            dbg[nm] = nc.dram_tensor(nm, shape, F32, kind="ExternalOutput").ap()

    ctxs = []

    def sb(name, shape, dtype):
        cm = nc.sbuf_tensor(name, shape, dtype)
        t = cm.__enter__()
        ctxs.append(cm)
        return t

    def psb(name, shape, dtype=F32):
        cm = nc.psum_tensor(name, shape, dtype)
        t = cm.__enter__()
        ctxs.append(cm)
        return t

    ident = sb("ident", [P, P], BF16)
    onescol = sb("onescol", [P, 1], BF16)
    onesrow = sb("onesrow", [1, P], BF16)
    sqstg = sb("sqstg", [P, 8], BF16)
    sel_all = sb("sel_all", [N, N * P], BF16)
    bcT_all = sb("bcT_all", [N, 2 * TCH], BF16)
    sqTrow = sb("sqTrow", [1, 8 * P], BF16)
    gam = sb("gam", [P, DBLK], F32)
    bet = sb("bet", [P, DBLK], F32)
    apar = sb("apar", [P, DBLK * N], F32)
    drep = sb("drep", [P, D], BF16)
    bdtc = sb("bdtc", [P, DBLK], F32)
    bbc = sb("bbc", [N, 2], F32)
    parc = sb("parc", [P, 2], F32)
    consts = sb("consts", [P, 14], F32)
    wdtg = sb("wdtg", [P, DBLK * D], BF16)
    wbg = sb("wbg", [P, DBLK * N], BF16)
    wcg = sb("wcg", [P, DBLK * N], BF16)
    stats = sb("stats", [P, NT * 2], F32)
    sc1 = sb("sc1", [P, NT], F32)
    r1v = sb("r1v", [P, NT], F32)
    mnM2 = [sb("mnM0", [P, NMT], F32), sb("mnM1", [P, NMT], F32)]
    r1M2 = [sb("r1M0", [P, NMT], F32), sb("r1M1", [P, NMT], F32)]
    sc2 = sb("sc2", [P, NT], F32)
    statw = sb("statw", [P, 16], F32)
    statw2 = sb("statw2", [P, 16], F32)
    ln2s = sb("ln2s", [P, 2 * NMT], F32)
    ln3s = sb("ln3s", [P, 2 * NMT], F32)
    ln4s = sb("ln4s", [P, 2], F32)
    sxv = sb("sxv", [P, NMT], F32)
    sqv = sb("sqv", [P, NMT], F32)
    sqvr = sb("sqvr", [P, NMT], F32)
    accs = sb("accs", [P, DBLK * TCH], BF16)
    state = sb("state", [P, DBLK * N], F32)
    xts = sb("xts", [P, NT * D], F32)
    xt2 = [xts[:, 0:D], xts[:, D:2 * D]]
    xt0 = xt2[0]
    znt2 = [sb("znt", [P, D], BF16), sb("zntb", [P, D], BF16)]
    znt = znt2[0]
    bns = sb("bns", [P, 16], F32)

    a_b1 = sb("a_b1", [P, 2 * DBLK * TCH + IBLK * MINE], BF16)
    dtT = a_b1[:, 0:DBLK * TCH]
    wpT = a_b1[:, DBLK * TCH:2 * DBLK * TCH]
    qs = a_b1[:, 2 * DBLK * TCH:2 * DBLK * TCH + IBLK * MINE]
    a_b2 = sb("a_b2", [P, DBLK * TCH + 2 * NMT * D + MINE + DBLK * MINE], BF16)
    znT = a_b2[:, 0:DBLK * TCH]
    _o2 = DBLK * TCH
    yam = a_b2[:, _o2:_o2 + NMT * D]
    z1m = a_b2[:, _o2 + NMT * D:_o2 + 2 * NMT * D]
    ytd = a_b2[:, _o2 + 2 * NMT * D:_o2 + 2 * NMT * D + MINE]
    z2qT = a_b2[:, _o2 + 2 * NMT * D + MINE:_o2 + 2 * NMT * D + MINE + DBLK * MINE]
    SCRW = max(4 * TCH, 2 * D + 2 * N)
    FCOLS = max(DBLK * TCH, SCRW, 2 * NMT * D + 3 * D + 1024, NMT * D + 6 * WQC)
    a_f = sb("a_f", [P, FCOLS], F32)
    scr = a_f[:, 0:SCRW]

    wsc = a_f[:, NMT * D:NMT * D + WQC]
    um = a_f[:, 0:NMT * D]
    fTM = a_f[:, NMT * D:2 * NMT * D]
    fscr = a_f[:, 2 * NMT * D:2 * NMT * D + D]
    zscr = a_f[:, 2 * NMT * D + D:2 * NMT * D + 2 * D]
    oscr = a_f[:, 2 * NMT * D + 2 * D:2 * NMT * D + 3 * D]
    _bs = 2 * NMT * D + 3 * D
    bscr2 = [a_f[:, _bs:_bs + 512], a_f[:, _bs + 512:_bs + 1024]]
    a_b3 = sb("a_b3", [P, 21 * TCH + 4 * DBLK * P + 33 * MINE + 2 * IBLK * P + NMT * D], BF16)
    brep2 = [a_b3[:, j * TCH:(j + 1) * TCH] for j in range(5)]
    crep2 = [a_b3[:, (5 + j) * TCH:(6 + j) * TCH] for j in range(5)]
    abar2 = [a_b3[:, (10 + j) * TCH:(11 + j) * TCH] for j in range(5)]
    bxt2 = [a_b3[:, 15 * TCH:16 * TCH], a_b3[:, 16 * TCH:17 * TCH]]
    ht2 = [a_b3[:, 17 * TCH:18 * TCH], a_b3[:, 18 * TCH:19 * TCH]]
    cht2 = [a_b3[:, 19 * TCH:20 * TCH], a_b3[:, 20 * TCH:21 * TCH]]
    betac = sb("betac", [P, DBLK], BF16)
    _fb = 21 * TCH
    fw2 = [a_b3[:, _fb:_fb + DBLK * P], a_b3[:, _fb + DBLK * P:_fb + 2 * DBLK * P]]
    fu2 = [a_b3[:, _fb + 2 * DBLK * P:_fb + 3 * DBLK * P],
           a_b3[:, _fb + 3 * DBLK * P:_fb + 4 * DBLK * P]]
    _o = _fb + 4 * DBLK * P
    gt2 = [a_b3[:, _o + j * MINE:_o + (j + 1) * MINE] for j in range(4)]
    gt4 = a_b3[:, _o:_o + 4 * MINE]
    pt2 = [a_b3[:, _o + (4 + j) * MINE:_o + (5 + j) * MINE] for j in range(4)]
    pt4 = a_b3[:, _o + 4 * MINE:_o + 8 * MINE]
    mrow4 = a_b3[:, _o + 8 * MINE:_o + 12 * MINE]
    mrow24 = a_b3[:, _o + 12 * MINE:_o + 16 * MINE]
    mrow = a_b3[:, _o + 16 * MINE:_o + 17 * MINE]
    sqrep4 = a_b3[:, _o + 17 * MINE:_o + 21 * MINE]
    sqrrep4 = a_b3[:, _o + 21 * MINE:_o + 25 * MINE]
    qsc4 = a_b3[:, _o + 25 * MINE:_o + 29 * MINE]
    fcm2 = [a_b3[:, _o + (29 + j) * MINE:_o + (30 + j) * MINE] for j in range(4)]
    fc4 = a_b3[:, _o + 29 * MINE:_o + 33 * MINE]
    _w = _o + 33 * MINE
    wdsl2 = [a_b3[:, _w:_w + IBLK * P], a_b3[:, _w + IBLK * P:_w + 2 * IBLK * P]]
    zqb = a_b3[:, _w + 2 * IBLK * P:_w + 2 * IBLK * P + NMT * D]
    AB3 = _w + 2 * IBLK * P + NMT * D
    
    pm0 = psb("pm0", [P, 512])
    pm1 = psb("pm1", [P, 512])
    pm2 = psb("pm2", [P, 512])
    pm3 = psb("pm3", [P, 512])
    ptrp = psb("ptrp", [P, 2 * P], BF16)
    ptr = ptrp[:, 0:P]
    ptr2 = ptrp[:, P:2 * P]
    ptrw = [psb("ptrw0", [P, DBLK * P], BF16), psb("ptrw1", [P, DBLK * P], BF16)]
    pmq = psb("pmq", [P, 512])

    class Sem:
        def __init__(self, name):
            self.h = nc.alloc_semaphore(name=name)
            self.n = 0

        def inc(self, instr, k=1):
            instr.then_inc(self.h, k)
            self.n += k
            return instr

    s_ld, s_g, s_v, s_s, s_t, s_st, s_bar, s_ge = (Sem(x) for x in
        ("ld", "g", "v", "s", "t", "st", "bar", "ge"))
    s_rp = [Sem("rp0"), Sem("rp1"), Sem("rp2")]
    s_x = [Sem("x0"), Sem("x1"), Sem("x2")]
    s_fw = [Sem("fw0"), Sem("fw1")]
    s_dw = [Sem("dw0"), Sem("dw1")]
    s_gx = [Sem(f"gx{i}") for i in range(6)]
    s_gq3 = [Sem(f"gq{i}") for i in range(3)]
    s_gl3 = [Sem(f"gl{i}") for i in range(3)]

    V, S, TE, G, SY = nc.vector, nc.scalar, nc.tensor, nc.gpsimd, nc.sync

    def vf(instr=None):
        if instr is None:
            instr = V.engine_nop()
        s_v.inc(instr)
        V.wait_ge(s_v.h, s_v.n)
        return instr

    def sf(instr=None):
        if instr is None:
            instr = S.copy(statw[0:1, 13:14], consts[0:1, 7:8])
        s_s.inc(instr)
        S.wait_ge(s_s.h, s_s.n)
        return instr
    ALL_E = ("sync", "vector", "scalar", "tensor", "gpsimd")

    def barrier():
        SY.wait_ge(s_ld.h, s_ld.n)
        SY.wait_ge(s_st.h, s_st.n)
        G.wait_ge(s_g.h, s_g.n)
        G.wait_ge(s_ge.h, s_ge.n)
        for e in ALL_E:
            eng = getattr(nc, e)
            if e != "sync":
                eng.drain()
            eng.sem_inc(s_bar.h, 1)
        s_bar.n += len(ALL_E)
        for e in ALL_E:
            getattr(nc, e).wait_ge(s_bar.h, s_bar.n)

    NG = (D + 511) // 512

    def ln_stats(dst2, src):
        V.wait_ge(s_v.h, s_v.n)
        for gi in range(NG):
            lo, hi = gi * 512, min(D, (gi + 1) * 512)
            V.bn_stats(bns[:, 6 * gi:6 * gi + 6], src[:, lo:hi])
            vf()
        return V.bn_aggr(dst2, bns[:, 0:6 * NG])

    def rsqrt_cols(src_ap, dst_ap, tmp_ap):
        S.activation(tmp_ap, src_ap, AF.Ln, bias=LN_EPS)
        sf()
        return S.activation(dst_ap, tmp_ap, AF.Exp, scale=-0.5)

    ptrw_g = [0, 0]
    gp_v = [0, 0, 0, 0]
    q4v = [0]
    fcm_t = [0, 0, 0, 0]
    pm_g = {i: {"s": 0, "v": 0} for i in range(4)}
    xg = {"a_scal": 0, "zn_s": [0, 0], "znT_rd_t": 0, "bc_st": 0,
          "rp_fin": [0, 0], "acc_rd_t": 0, "yam_rd_s": 0, "um_rd_v": 0,
          "z2q_v": 0, "qs_v": 0, "o_st": [0, 0], "zn_v": 0,
          "pmq_s": 0, "sqT_t": 0, "bct_s": 0, "rep_t": 0, "ln4_s": 0}

    def pm_wait(bank):
        TE.wait_ge(s_s.h, pm_g[bank]["s"])
        TE.wait_ge(s_v.h, pm_g[bank]["v"])

    def finish():
        SY.wait_ge(s_ld.h, s_ld.n)
        SY.wait_ge(s_st.h, s_st.n)
        for cm in reversed(ctxs):
            cm.__exit__(None, None, None)
        return nc

    STOP = cfg.get("stop")

    V.memset(consts[:, 6:7], -0.5)
    vf()
    V.memset(consts[:, 7:8], 1.0)
    vf()
    V.memset(consts[:, 8:9], 0.5)
    vf()
    V.memset(consts[:, 9:10], 0.0)
    vf()
    V.memset(consts[:, 10:11], LN_EPS)
    vf()
    s_v.inc(V.memset(consts[:, 11:12], -0.5))
    for val, col in ((-0.5, 6), (1.0, 7), (0.5, 8), (0.0, 9), (LN_EPS, 10)):
        nc.const_aps.register(F32, val, consts[:, col:col + 1])
    s_ld.inc(SY.dma_start(gam[:], gam_h[:]), 16)
    s_ld.inc(SY.dma_start(bet[:], bet_h[:]), 16)
    s_ld.inc(SY.dma_start(apar[:], apar_h[:]), 16)
    s_g.inc(G.dma_start(drep[:], dpar_h.partition_broadcast(P)), 16)
    s_ld.inc(SY.dma_start(parc[:, 0:1], parf_h.partition_broadcast(P)), 16)
    s_ld.inc(SY.dma_start(scr[0:1, D + 2 * N:2 * D + 2 * N], bdt_h[0:1, 0:D]), 16)
    s_g.inc(G.dma_start(wdtg.ap().rearrange("p (c o) -> p c o", c=DBLK),
                        wdtT_h.rearrange("(c p) o -> p c o", p=P)), 16)
    s_g.inc(G.dma_start(wbg.ap().rearrange("p (c n) -> p c n", c=DBLK),
                        wbT_h.rearrange("(c p) n -> p c n", p=P)), 16)
    s_g.inc(G.dma_start(wcg.ap().rearrange("p (c n) -> p c n", c=DBLK),
                        wcT_h.rearrange("(c p) n -> p c n", p=P)), 16)

    G.wait_ge(s_g.h, s_g.n)
    s_ge.inc(G.memset(onesrow[:], 1.0))
    s_ge.inc(G.memset(sel_all[:], 0.0))
    G.wait_ge(s_ge.h, s_ge.n)
    for n_ in range(N):
        s_ge.inc(G.affine_select(out=sel_all[0:N, n_ * P:(n_ + 1) * P],
                                 in_=sel_all[0:N, n_ * P:(n_ + 1) * P],
                                 compare_op=ALU.not_equal, fill=1.0, base=-n_,
                                 pattern=[[0, P]], channel_multiplier=1))
    s_ge.inc(G.memset(ident[:], 0.0))
    G.wait_ge(s_ge.h, s_ge.n)
    s_ge.inc(G.affine_select(out=ident[:], in_=ident[:], compare_op=ALU.not_equal,
                             fill=1.0, base=0, pattern=[[-1, P]], channel_multiplier=1))

    S.wait_ge(s_ld.h, s_ld.n)
    S.wait_ge(s_v.h, s_v.n)
    S.activation(apar[:], apar[:], AF.Exp)
    sf()
    s_s.inc(S.mul(apar[:], apar[:], -1.0))

    V.wait_ge(s_ld.h, s_ld.n)
    V.wait_ge(s_g.h, s_g.n)
    V.tensor_scalar(parc[:, 1:2], parc[:, 0:1], -1.0, 1.0, ALU.mult, ALU.add)
    vf()
    V.tensor_copy(onescol[:], consts[:, 7:8])
    vf()
    s_v.inc(V.tensor_copy(betac[:, 0:DBLK], bet[:]))
    TE.wait_ge(s_v.h, s_v.n)
    TE.wait_ge(s_g.h, s_g.n)

    for oc in range(0, D, 512):
        w = min(512, D - oc)
        for c in range(DBLK):
            mm = TE.matmul(pm0[0:1, 0:w], betac[:, c:c + 1],
                           wdtg[:, c * D + oc:c * D + oc + w],
                           start=(c == 0), stop=(c == DBLK - 1))
            if c == DBLK - 1:
                s_t.inc(mm)
        V.wait_ge(s_t.h, s_t.n)
        s_v.inc(V.tensor_copy(scr[0:1, oc:oc + w], pm0[0:1, 0:w]))
        TE.wait_ge(s_v.h, s_v.n)
        pm_g[0]["v"] = s_v.n
    for wi, wsrc in enumerate((wbg, wcg)):
        for c in range(DBLK):
            mm = TE.matmul(pm1[0:1, 0:N], betac[:, c:c + 1], wsrc[:, c * N:(c + 1) * N],
                           start=(c == 0), stop=(c == DBLK - 1))
            if c == DBLK - 1:
                s_t.inc(mm)
        V.wait_ge(s_t.h, s_t.n)
        s_v.inc(V.tensor_copy(scr[0:1, D + wi * N:D + (wi + 1) * N], pm1[0:1, 0:N]))
        TE.wait_ge(s_v.h, s_v.n)
        pm_g[1]["v"] = s_v.n
    V.wait_ge(s_ld.h, s_ld.n)
    s_v.inc(V.tensor_tensor(scr[0:1, 0:D], scr[0:1, 0:D], scr[0:1, D + 2 * N:2 * D + 2 * N], ALU.add))
    SY.wait_ge(s_v.h, s_v.n)
    s_st.inc(SY.dma_start(brow_d[0:1, :], scr[0:1, 0:D + 2 * N]), 16)
    SY.wait_ge(s_st.h, s_st.n)
    s_ld.inc(SY.dma_start(bdtc.ap().rearrange("p (one c) -> p one c", one=1),
                          brow_d[0:1, 0:D].rearrange("one (c p) -> p one c", p=P), allow_slow_non_contiguous=True), 16)
    s_ld.inc(SY.dma_start(bbc[0:N, 0:2].rearrange("n (one two) -> n one two", one=1),
                          brow_d[0:1, D:D + 2 * N].rearrange("one (two n) -> n one two", n=N), allow_slow_non_contiguous=True), 16)

    V.wait_ge(s_s.h, s_s.n)
    V.wait_ge(s_ld.h, s_ld.n)
    for c in range(DBLK):
        V.tensor_scalar(wdtg[:, c * D:(c + 1) * D], wdtg[:, c * D:(c + 1) * D],
                        gam[:, c:c + 1], None, ALU.mult)
        V.tensor_scalar(wbg[:, c * N:(c + 1) * N], wbg[:, c * N:(c + 1) * N],
                        gam[:, c:c + 1], None, ALU.mult)
        V.tensor_scalar(wcg[:, c * N:(c + 1) * N], wcg[:, c * N:(c + 1) * N],
                        gam[:, c:c + 1], None, ALU.mult)
    barrier()


    if STOP == "prologue":
        return finish()
    wsc2 = [a_f[:, NMT * D + i * WQC:NMT * D + (i + 1) * WQC] for i in range(6)]
    wqa = [a_b3[:, i * WQC:(i + 1) * WQC] for i in range(6)]
    wqb = [a_b3[:, (6 + i) * WQC:(7 + i) * WQC] for i in range(6)]


    def gen_quant(w_h, wq_d, nrows, slot):
        CH_ROWS = WQC
        chunks = list(range(0, nrows, CH_ROWS))
        bufs = (2 * slot, 2 * slot + 1)
        ldsem = (s_gx[2 * slot], s_gx[2 * slot + 1])
        stsem = (s_fw[0], s_fw[1], s_dw[0])[slot]
        pmx = (pm0, pm1, pm2)[slot]
        sw = 4 * slot
        nmm = 0
        for r0 in chunks:
            rr = min(CH_ROWS, nrows - r0)
            nmm += (rr + 511) // 512
        k = 0
        s_done_abs = [0, 0]
        t_done_abs = [0, 0]
        ld_w = [0, 0]
        for ci, r0 in enumerate(chunks):
            b = ci % 2
            p = bufs[b]
            rr = min(CH_ROWS, nrows - r0)
            w = rr
            G.wait_ge(s_s.h, s_done_abs[b])
            ldsem[b].inc(G.dma_start(wsc2[p][:, 0:w].rearrange("p (a q) -> p a q", q=P),
                                     w_h[r0:r0 + rr, :].rearrange("(p a) q -> p a q", p=P)), 16)
            ld_w[b] = ldsem[b].n
            S.wait_ge(ldsem[b].h, ld_w[b])
            S.wait_ge(s_t.h, t_done_abs[b])
            s_s.inc(S.activation(wqa[p][:, 0:w], wsc2[p][:, 0:w], AF.Abs))
            s_done_abs[b] = s_s.n
            TE.wait_ge(s_s.h, s_done_abs[b])
            if ci == 0:
                pm_wait(slot)
            noff = (w + 511) // 512
            for oi, off in enumerate(range(0, w, 512)):
                ww = min(512, w - off)
                mm = TE.matmul(pmx[0:1, 0:ww], onescol[:], wqa[p][:, off:off + ww],
                               start=(k == 0), stop=(k == nmm - 1))
                if oi == noff - 1:
                    s_t.inc(mm)
                k += 1
            t_done_abs[b] = s_t.n
            yield
        V.wait_ge(s_t.h, s_t.n)
        s_v.inc(V.tensor_reduce(statw[0:1, sw:sw + 1], pmx[0:1, 0:512], AX.X, ALU.add))
        pm_g[slot]["v"] = s_v.n
        V.wait_ge(s_v.h, s_v.n)
        V.tensor_scalar(statw[0:1, sw + 1:sw + 2], statw[0:1, sw:sw + 1],
                        1.0 / float(nrows * P), 1e-5, ALU.mult, ALU.max)
        vf()
        V.reciprocal(statw[0:1, sw + 2:sw + 3], statw[0:1, sw + 1:sw + 2])
        vf()
        s_v.inc(V.tensor_scalar(statw[0:1, sw + 3:sw + 4], statw[0:1, sw + 1:sw + 2],
                                0.5, None, ALU.mult))
        G.wait_ge(s_v.h, s_v.n)
        s_gq3[slot].inc(G.dma_start(wqs_d[slot:slot + 1, 0:2], statw[0:1, sw + 2:sw + 4]), 16)
        G.wait_ge(s_gq3[slot].h, s_gq3[slot].n)
        s_gl3[slot].inc(G.dma_start(consts[:, slot:slot + 1],
                                    wqs_d[slot:slot + 1, 0:1].partition_broadcast(P)), 16)
        s_gl3[slot].inc(G.dma_start(consts[:, 3 + slot:4 + slot],
                                    wqs_d[slot:slot + 1, 1:2].partition_broadcast(P)), 16)
        gl_done = s_gl3[slot].n
        yield
        S.wait_ge(s_gl3[slot].h, gl_done)
        s_done_sg = list(s_done_abs)
        v_done_sg = [0, 0]
        st_all = stsem.n
        for ci, r0 in enumerate(chunks):
            b = ci % 2
            p = bufs[b]
            rr = min(CH_ROWS, nrows - r0)
            w = rr
            G.wait_ge(s_s.h, s_done_sg[b])
            ldsem[b].inc(G.dma_start(wsc2[p][:, 0:w].rearrange("p (a q) -> p a q", q=P),
                                     w_h[r0:r0 + rr, :].rearrange("(p a) q -> p a q", p=P)), 16)
            ld_w[b] = ldsem[b].n
            S.wait_ge(ldsem[b].h, ld_w[b])
            S.wait_ge(s_v.h, v_done_sg[b])
            S.wait_ge(stsem.h, st_all)
            S.activation(wqa[p][:, 0:w], wsc2[p][:, 0:w], AF.Sign,
                         scale=consts[:, slot:slot + 1], bias=consts[:, 6:7])
            s_s.inc(S.activation(wqb[p][:, 0:w], wsc2[p][:, 0:w], AF.Sign,
                                 scale=consts[:, slot:slot + 1], bias=consts[:, 8:9]))
            s_done_sg[b] = s_s.n
            V.wait_ge(s_s.h, s_done_sg[b])
            s_v.inc(V.tensor_tensor(wqa[p][:, 0:w], wqa[p][:, 0:w], wqb[p][:, 0:w],
                                    ALU.add))
            v_done_sg[b] = s_v.n
            SY.wait_ge(s_v.h, v_done_sg[b])
            stsem.inc(SY.dma_start(wq_d[r0:r0 + rr, :].rearrange("(p a) q -> p a q", p=P),
                                   wqa[p][:, 0:w].rearrange("p (a q) -> p a q", q=P)), 16)
            if ci >= 1:
                st_all = stsem.n - 16
            yield
        SY.wait_ge(stsem.h, stsem.n)

    qgens = [gen_quant(wgT_h, wgq_d, INNER // P * D, 0),
             gen_quant(wuT_h, wuq_d, INNER // P * D, 1),
             gen_quant(wdT_h, wdq_d, D // P * INNER, 2)]
    live = list(qgens)
    while live:
        for g in list(live):
            try:
                next(g)
            except StopIteration:
                live.remove(g)
    V.wait_ge(s_gl3[0].h, s_gl3[0].n)
    s_v.inc(V.tensor_scalar(consts[:, 12:13], consts[:, 3:4], 0.5, None, ALU.mult))
    barrier()

    if STOP == "quant":
        return finish()
    s_v.inc(V.memset(state[:], 0.0))
    vf()

    def gen_A(sidx):
        t0 = sidx * TCH
        kt0 = t0 // P
        v_done_x = [0, 0]
        ld_x = [0, 0]
        V.wait_ge(s_s.h, xg["a_scal"])
        for k in range(NT):
            p = k % 2
            if k < 2:
                SY.wait_ge(s_s.h, xg["zn_s"][0])
                SY.wait_ge(s_s.h, xg["zn_s"][1])
            s_x[p].inc(SY.dma_start(xts[:, k * D:(k + 1) * D],
                                    x_h[t0 + k * P:t0 + (k + 1) * P, :]), 16)
            ld_x[p] = s_x[p].n
            V.wait_ge(s_x[p].h, ld_x[p])
            s_v.inc(ln_stats(stats[:, 2 * k:2 * k + 2], xts[:, k * D:(k + 1) * D]))
            v_done_x[p] = s_v.n
            yield
        mview = stats.ap().rearrange("p (k two) -> p two k", two=2)
        mean_v = mview[:, 0, :]
        var_v = mview[:, 1, :]
        S.wait_ge(s_v.h, s_v.n)
        s_s.inc(rsqrt_cols(var_v, sc1[:, 0:NT], statw[:, 0:NT]))
        yield
        V.wait_ge(s_s.h, s_s.n)
        V.tensor_copy(r1v[:, 0:NT], sc1[:, 0:NT])
        vf()
        V.tensor_tensor(statw[:, 0:NT], sc1[:, 0:NT], sc1[:, 0:NT], ALU.mult)
        vf()
        s_v.inc(V.tensor_tensor(statw[:, 0:NT], statw[:, 0:NT], var_v, ALU.mult))
        S.wait_ge(s_v.h, s_v.n)
        s_s.inc(rsqrt_cols(statw[:, 0:NT], statw[:, NT:2 * NT], statw[:, 0:NT]))
        yield
        V.wait_ge(s_s.h, s_s.n)
        V.tensor_tensor(sc1[:, 0:NT], sc1[:, 0:NT], statw[:, NT:2 * NT], ALU.mult)
        vf()
        V.tensor_tensor(sc2[:, 0:NT], mean_v, sc1[:, 0:NT], ALU.mult)
        vf()
        vf()
        V.tensor_scalar(sc2[:, 0:NT], sc2[:, 0:NT], -1.0, None, ALU.mult)
        vf()
        mev = stats.ap().rearrange("p (k par mv) -> p par mv k", par=2, mv=2)
        for (dst, ev, od) in ((mnM2[sidx % 2], mev[:, 0, 0, :], mev[:, 1, 0, :]),
                              (r1M2[sidx % 2], r1v.ap().rearrange("p (k two) -> p two k", two=2)[:, 0, :],
                               r1v.ap().rearrange("p (k two) -> p two k", two=2)[:, 1, :])):
            V.tensor_scalar(statw[:, 0:NMT], od, parc[:, 0:1], None, ALU.mult)
            vf()
            V.scalar_tensor_tensor(dst[:, 0:NMT], ev, parc[:, 1:2],
                                   statw[:, 0:NMT], ALU.mult, ALU.add)
            vf()
        s_v.inc(V.engine_nop())
        S.wait_ge(s_v.h, s_v.n)
        yield
        s_done_zn = [0, 0]
        t_done_zn = [0, 0]
        V.wait_ge(s_t.h, xg["znT_rd_t"])
        znT_v = znT.rearrange("p (c t) -> p c t", c=DBLK)
        for k in range(NT):
            p = k % 2
            S.wait_ge(s_t.h, t_done_zn[p])
            s_s.inc(S.activation(znt2[p][:], xts[:, k * D:(k + 1) * D], AF.Identity,
                                 scale=sc1[:, k:k + 1], bias=sc2[:, k:k + 1]))
            s_done_zn[p] = s_s.n
            TE.wait_ge(s_s.h, s_done_zn[p])
            TE.wait_ge(s_v.h, ptrw_g[p])
            for c in range(DBLK):
                mm = TE.transpose(ptrw[p][:, c * P:(c + 1) * P],
                                  znt2[p][:, c * P:(c + 1) * P], ident[:])
                if c == DBLK - 1:
                    s_t.inc(mm)
            t_done_zn[p] = s_t.n
            V.wait_ge(s_t.h, t_done_zn[p])
            s_v.inc(V.tensor_copy(znT_v[:, :, k * P:(k + 1) * P],
                                  ptrw[p][:, 0:DBLK * P].rearrange("p (c q) -> p c q", c=DBLK)))
            ptrw_g[p] = s_v.n
            yield
        xg["a_scal"] = s_s.n
        xg["zn_s"] = list(s_done_zn)
        xg["zn_v"] = s_v.n

    def gen_B(sidx):
        t0 = sidx * TCH
        kt0 = t0 // P
        t_done_dt = [0, 0]
        s_done_dt = [0, 0]
        NTC = TCH // 512
        fscr2 = bscr2
        TE.wait_ge(s_v.h, xg["zn_v"])
        for c in range(DBLK):
            for ti, tc in enumerate(range(0, TCH, 512)):
                w = min(512, TCH - tc)
                g = (c * NTC + ti) % 2
                pmx = pm0 if g == 0 else pm1
                pm_wait(g)
                for kk in range(DBLK):
                    mm = TE.matmul(pmx[:, 0:w],
                                   wdtg[:, kk * D + c * P:kk * D + (c + 1) * P],
                                   znT[:, kk * TCH + tc:kk * TCH + tc + w],
                                   start=(kk == 0), stop=(kk == DBLK - 1))
                    if kk == DBLK - 1:
                        s_t.inc(mm)
                t_done_dt[g] = s_t.n
                S.wait_ge(s_t.h, t_done_dt[g])
                S.activation(fscr2[g][:, 0:w], pmx[:, 0:w], AF.Exp,
                             bias=bdtc[:, c:c + 1], scale=consts[:, 7:8])
                sf()
                s_s.inc(S.activation(dtT[:, c * TCH + tc:c * TCH + tc + w],
                                     fscr2[g][:, 0:w], AF.Ln, bias=1.0))
                s_done_dt[g] = s_s.n
                pm_g[g]["s"] = s_s.n
                yield
        s_done_bc = 0
        for wi, wsrc in enumerate((wbg, wcg)):
            for ti, tc in enumerate(range(0, TCH, 512)):
                w = min(512, TCH - tc)
                pm_wait(2)
                for kk in range(DBLK):
                    mm = TE.matmul(pm2[0:N, 0:w], wsrc[:, kk * N:(kk + 1) * N],
                                   znT[:, kk * TCH + tc:kk * TCH + tc + w],
                                   start=(kk == 0), stop=(kk == DBLK - 1))
                    if kk == DBLK - 1:
                        s_t.inc(mm)
                S.wait_ge(s_t.h, s_t.n)
                S.wait_ge(s_t.h, xg["rep_t"])
                s_s.inc(S.activation(bcT_all[0:N, wi * TCH + tc:wi * TCH + tc + w],
                                     pm2[0:N, 0:w], AF.Identity,
                                     bias=bbc[0:N, wi:wi + 1], scale=consts[0:N, 7:8]))
                s_done_bc = s_s.n
                pm_g[2]["s"] = s_s.n
                yield
        xg["bct_s"] = s_done_bc
        V.wait_ge(s_s.h, s_done_dt[0])
        V.wait_ge(s_s.h, s_done_dt[1])
        for c in range(DBLK):
            V.tensor_scalar(wpT[:, c * TCH:(c + 1) * TCH], znT[:, c * TCH:(c + 1) * TCH],
                            gam[:, c:c + 1], bet[:, c:c + 1], ALU.mult, ALU.add)
            vf()
            s_v.inc(V.tensor_tensor(wpT[:, c * TCH:(c + 1) * TCH],
                                    wpT[:, c * TCH:(c + 1) * TCH],
                                    dtT[:, c * TCH:(c + 1) * TCH], ALU.mult))
            yield
        xg["znT_rd_t"] = s_t.n
        xg["bc_st"] = s_st.n

    def gen_C(sidx):
        t0 = sidx * TCH
        kt0 = t0 // P
        v_done_rep = [0] * 5
        v_done_ab = [0] * 5
        s_done_ab = [0] * 5
        v_done_ht = [0, 0]
        g_done_ht = [0, 0]
        s_done_rep = [0] * 5
        V.wait_ge(s_t.h, xg["acc_rd_t"])
        V.wait_ge(s_v.h, s_v.n)
        for n in range(N + 1):
            p = n % 5
            if n < N:
                TE.wait_ge(s_s.h, xg["pmq_s"])
                TE.wait_ge(s_s.h, xg["bct_s"])
                s_t.inc(TE.matmul(pmq[:, 0:TCH], sel_all[0:N, n * P:(n + 1) * P],
                                  bcT_all[0:N, 0:TCH], start=True, stop=True))
                S.wait_ge(s_t.h, s_t.n)
                S.wait_ge(s_v.h, v_done_rep[p])
                s_s.inc(S.activation(brep2[p][:, 0:TCH], pmq[:, 0:TCH], AF.Identity,
                                     scale=consts[:, 7:8], bias=consts[:, 9:10]))
                xg["pmq_s"] = s_s.n
                TE.wait_ge(s_s.h, xg["pmq_s"])
                s_t.inc(TE.matmul(pmq[:, 0:TCH], sel_all[0:N, n * P:(n + 1) * P],
                                  bcT_all[0:N, TCH:2 * TCH], start=True, stop=True))
                xg["rep_t"] = s_t.n
                S.wait_ge(s_t.h, s_t.n)
                s_s.inc(S.activation(crep2[p][:, 0:TCH], pmq[:, 0:TCH], AF.Identity,
                                     scale=consts[:, 7:8], bias=consts[:, 9:10]))
                xg["pmq_s"] = s_s.n
                s_done_rep[p] = s_s.n
            if n == 0:
                continue
            nn = n - 1
            pp = nn % 5
            for c0 in range(0, DBLK, 2):
                cpair = (c0, c0 + 1)
                for c in cpair:
                    ac = c % 5
                    S.wait_ge(s_v.h, v_done_ab[ac])
                    s_s.inc(S.activation(abar2[ac][:, 0:TCH],
                                         dtT[:, c * TCH:(c + 1) * TCH], AF.Exp,
                                         scale=apar[:, c * N + nn:c * N + nn + 1],
                                         bias=consts[:, 9:10]))
                    s_done_ab[ac] = s_s.n
                V.wait_ge(s_s.h, s_done_rep[pp])
                V.wait_ge(s_s.h, s_done_ab[cpair[1] % 5])
                V.wait_ge(s_ge.h, g_done_ht[0])
                V.wait_ge(s_ge.h, g_done_ht[1])
                V.wait_ge(s_v.h, v_done_ht[0])
                V.wait_ge(s_v.h, v_done_ht[1])
                bx_n = [0, 0]
                sc_n = [0, 0]
                ch_n = [0, 0]
                for c in cpair:
                    pc = c % 2
                    s_v.inc(V.tensor_tensor(bxt2[pc][:, 0:TCH],
                                            wpT[:, c * TCH:(c + 1) * TCH],
                                            brep2[pp][:, 0:TCH], ALU.mult))
                    bx_n[pc] = s_v.n
                for c in cpair:
                    pc = c % 2
                    ac = c % 5
                    V.wait_ge(s_v.h, bx_n[pc])
                    s_v.inc(V.tensor_tensor_scan(ht2[pc][:, 0:TCH],
                                                 abar2[ac][:, 0:TCH],
                                                 bxt2[pc][:, 0:TCH],
                                                 state[:, c * N + nn:c * N + nn + 1],
                                                 ALU.mult, ALU.add))
                    sc_n[pc] = s_v.n
                    v_done_ab[ac] = s_v.n
                for c in cpair:
                    pc = c % 2
                    V.wait_ge(s_v.h, sc_n[pc])
                    s_v.inc(V.tensor_tensor(cht2[pc][:, 0:TCH], ht2[pc][:, 0:TCH],
                                            crep2[pp][:, 0:TCH], ALU.mult))
                    ch_n[pc] = s_v.n
                for c in cpair:
                    pc = c % 2
                    V.wait_ge(s_v.h, ch_n[pc])
                    if nn == 0:
                        s_v.inc(V.tensor_tensor(accs[:, c * TCH:(c + 1) * TCH],
                                                ht2[pc][:, 0:TCH], crep2[pp][:, 0:TCH],
                                                ALU.mult))
                    else:
                        s_v.inc(V.tensor_tensor(accs[:, c * TCH:(c + 1) * TCH],
                                                accs[:, c * TCH:(c + 1) * TCH],
                                                cht2[pc][:, 0:TCH], ALU.add))
                    v_done_ht[pc] = s_v.n
                    G.wait_ge(s_v.h, v_done_ht[pc])
                    s_ge.inc(G.tensor_copy(state[:, c * N + nn:c * N + nn + 1],
                                           ht2[pc][:, TCH - 1:TCH]))
                    g_done_ht[pc] = s_ge.n
                yield
            v_done_rep[pp] = s_v.n
        V.wait_ge(s_ge.h, s_ge.n)
        V.wait_ge(s_s.h, xg["yam_rd_s"])
        for c in range(DBLK):
            p = c % 2
            TE.wait_ge(s_v.h, max(ptrw_g[p], s_v.n if c < 2 else 0))
            for k in range(NT):
                mm = TE.transpose(ptrw[p][:, k * P:(k + 1) * P],
                                  accs[:, c * TCH + k * P:c * TCH + (k + 1) * P],
                                  ident[:])
                if k == NT - 1:
                    s_t.inc(mm)
            V.wait_ge(s_t.h, s_t.n)
            pv = ptrw[p][:, 0:NT * P].rearrange("p (k two q) -> p k two q", two=2, q=P)
            vf(V.tensor_scalar(ytd[:, 0:MINE].rearrange("p (k q) -> p k q", q=P),
                               pv[:, :, 1, :], parc[:, 0:1], None, ALU.mult))
            yv = yam.rearrange("p (k d) -> p k d", k=NMT)
            s_v.inc(V.scalar_tensor_tensor(yv[:, :, c * P:(c + 1) * P],
                                           pv[:, :, 0, :], parc[:, 1:2],
                                           ytd[:, 0:MINE].rearrange("p (k q) -> p k q", q=P),
                                           ALU.mult, ALU.add))
            ptrw_g[p] = s_v.n
            yield
        xg["acc_rd_t"] = s_t.n

    def gen_C2(sidx):
        t0 = sidx * TCH
        kt0 = t0 // P
        V.wait_ge(s_s.h, xg["ln4_s"])
        SY.wait_ge(s_v.h, xg["um_rd_v"])
        for k in range(NMT):
            s_ld.inc(SY.dma_start(um[:, k * D:(k + 1) * D],
                                  xm_h[sidx * MINE + k * P:sidx * MINE + (k + 1) * P, :]), 16)
        yield
        V.wait_ge(s_t.h, s_t.n)
        mnM = mnM2[sidx % 2]
        r1M = r1M2[sidx % 2]
        vf(V.tensor_tensor(statw2[:, 0:NMT], mnM[:, 0:NMT], r1M[:, 0:NMT], ALU.mult))
        s_v.inc(V.tensor_scalar(statw2[:, 0:NMT], statw2[:, 0:NMT], -1.0, None, ALU.mult))
        S.wait_ge(s_v.h, s_v.n)
        S.wait_ge(s_ld.h, s_ld.n)
        for k in range(NMT):
            mm = S.activation(z1m[:, k * D:(k + 1) * D], um[:, k * D:(k + 1) * D],
                              AF.Identity, scale=r1M[:, k:k + 1], bias=statw2[:, k:k + 1])
            if k == NMT - 1:
                s_s.inc(mm)
        c2_z1 = s_s.n
        yield
        V.wait_ge(s_s.h, c2_z1)
        for k in range(NMT):
            V.tensor_tensor(z1m[:, k * D:(k + 1) * D], z1m[:, k * D:(k + 1) * D],
                            drep[:], ALU.mult)
        vf()
        for k in range(NMT):
            V.tensor_tensor(yam[:, k * D:(k + 1) * D], yam[:, k * D:(k + 1) * D],
                            z1m[:, k * D:(k + 1) * D], ALU.add)
        vf()
        c2_st = 0
        for k in range(NMT):
            s_v.inc(ln_stats(ln3s[:, 2 * k:2 * k + 2], yam[:, k * D:(k + 1) * D]))
            c2_st = s_v.n
            yield
        S.wait_ge(s_v.h, c2_st)
        l3v = ln3s.ap().rearrange("p (k two) -> p two k", two=2)
        s_s.inc(rsqrt_cols(l3v[:, 1, :], sxv[:, 0:NMT], statw2[:, 0:NMT]))
        V.wait_ge(s_s.h, s_s.n)
        vf(V.tensor_tensor(statw2[:, 0:NMT], l3v[:, 0, :], sxv[:, 0:NMT], ALU.mult))
        s_v.inc(V.tensor_scalar(statw2[:, 0:NMT], statw2[:, 0:NMT], -1.0, None, ALU.mult))
        S.wait_ge(s_v.h, s_v.n)
        for k in range(NMT):
            mm = S.activation(fTM[:, k * D:(k + 1) * D], yam[:, k * D:(k + 1) * D],
                              AF.Identity, scale=sxv[:, k:k + 1], bias=statw2[:, k:k + 1])
            if k == NMT - 1:
                s_s.inc(mm)
        xg["yam_rd_s"] = s_s.n
        c2_f = s_s.n
        yield
        V.wait_ge(s_s.h, c2_f)
        for k in range(NMT):
            mm = V.tensor_tensor(um[:, k * D:(k + 1) * D], um[:, k * D:(k + 1) * D],
                                 fTM[:, k * D:(k + 1) * D], ALU.add)
            if k == NMT - 1:
                s_v.inc(mm)
        S.wait_ge(s_v.h, s_v.n)
        SY.wait_ge(s_v.h, s_v.n)

    def gen_D_head(sidx):
        t0 = sidx * TCH
        kt0 = t0 // P
        zq = zqb
        V.wait_ge(s_s.h, xg["ln4_s"])
        dh_st = 0
        for k in range(NMT):
            s_v.inc(ln_stats(ln2s[:, 2 * k:2 * k + 2], um[:, k * D:(k + 1) * D]))
            dh_st = s_v.n
            yield
        l2v = ln2s.ap().rearrange("p (k two) -> p two k", two=2)
        S.wait_ge(s_v.h, dh_st)
        s_s.inc(rsqrt_cols(l2v[:, 1, :], sxv[:, 0:NMT], statw2[:, 0:NMT]))
        V.wait_ge(s_s.h, s_s.n)
        vf(V.tensor_tensor(statw2[:, 0:NMT], l2v[:, 0, :], sxv[:, 0:NMT], ALU.mult))
        s_v.inc(V.tensor_scalar(statw2[:, 0:NMT], statw2[:, 0:NMT], -1.0, None, ALU.mult))
        S.wait_ge(s_v.h, s_v.n)
        for k in range(NMT):
            mm = S.activation(fTM[:, k * D:(k + 1) * D], um[:, k * D:(k + 1) * D],
                              AF.Identity, scale=sxv[:, k:k + 1], bias=statw2[:, k:k + 1])
            if k == NMT - 1:
                s_s.inc(mm)
        dh_z2 = s_s.n
        yield
        V.wait_ge(s_s.h, dh_z2)
        for k in range(NMT):
            V.tensor_reduce(statw2[:, k:k + 1], fTM[:, k * D:(k + 1) * D], AX.X,
                            ALU.max, apply_absolute_value=True)
        vf()
        yield
        V.tensor_scalar(statw2[:, 8:8 + NMT], statw2[:, 0:NMT], 1e-5, 1.0 / 127.0,
                        ALU.max, ALU.mult)
        vf()
        V.reciprocal(statw2[:, 0:NMT], statw2[:, 8:8 + NMT])
        vf()
        for k in range(NMT):
            V.tensor_scalar(fTM[:, k * D:(k + 1) * D], fTM[:, k * D:(k + 1) * D],
                            statw2[:, k:k + 1], MAGIC, ALU.mult, ALU.add)
        vf()
        for k in range(NMT):
            mm = V.tensor_scalar(zq[:, k * D:(k + 1) * D], fTM[:, k * D:(k + 1) * D],
                                 -MAGIC, statw2[:, 8 + k:9 + k], ALU.add, ALU.mult)
            if k == NMT - 1:
                s_v.inc(mm)
        TE.wait_ge(s_v.h, s_v.n)
        z2qT_v = z2qT.rearrange("p (c t) -> p c t", c=DBLK)
        for k in range(NMT):
            p = k % 2
            TE.wait_ge(s_v.h, ptrw_g[p])
            for c in range(DBLK):
                mm = TE.transpose(ptrw[p][:, c * P:(c + 1) * P],
                                  zq[:, k * D + c * P:k * D + (c + 1) * P], ident[:])
                if c == DBLK - 1:
                    s_t.inc(mm)
            V.wait_ge(s_t.h, s_t.n)
            s_v.inc(V.tensor_copy(z2qT_v[:, :, k * P:(k + 1) * P],
                                  ptrw[p][:, 0:DBLK * P].rearrange("p (c q) -> p c q", c=DBLK)))
            ptrw_g[p] = s_v.n
            yield
        xg["z2q_v"] = s_v.n
        TE.wait_ge(s_v.h, xg["z2q_v"])

    def gen_D(sidx):
        zq = zqb
        t_done_fw = [0, 0]
        s_done_gu = [0, 0, 0, 0]
        v_done_gp = gp_v
        ld_fw = [0, 0]

        def gu_load(ic):
            p = ic % 2
            SY.wait_ge(s_t.h, t_done_fw[p])
            s_fw[p].inc(SY.dma_start(fw2[p][:, 0:DBLK * P].rearrange("p (kk o) -> p kk o", kk=DBLK),
                                     wgq_d[ic * D:(ic + 1) * D, :].rearrange(
                                         "(p kk) o -> p kk o", p=P)), 16)
            s_fw[p].inc(SY.dma_start(fu2[p][:, 0:DBLK * P].rearrange("p (kk o) -> p kk o", kk=DBLK),
                                     wuq_d[ic * D:(ic + 1) * D, :].rearrange(
                                         "(p kk) o -> p kk o", p=P)), 16)
            ld_fw[p] = s_fw[p].n

        gu_load(0)
        for ic in range(IBLK):
            p = ic % 2
            q = ic % 4
            if ic + 1 < IBLK:
                gu_load(ic + 1)
            TE.wait_ge(s_fw[p].h, ld_fw[p])
            pmg = pm0 if p == 0 else pm2
            pmu = pm1 if p == 0 else pm3
            for tc in range(0, MINE, 512):
                w = min(512, MINE - tc)
                pm_wait(0 if p == 0 else 2)
                pm_wait(1 if p == 0 else 3)
                for kk in range(DBLK):
                    mm = TE.matmul(pmg[:, tc % 512:tc % 512 + w], fw2[p][:, kk * P:(kk + 1) * P],
                                   z2qT[:, kk * MINE + tc:kk * MINE + tc + w],
                                   start=(kk == 0), stop=(kk == DBLK - 1))
                    if kk == DBLK - 1:
                        s_t.inc(mm)
                for kk in range(DBLK):
                    mm = TE.matmul(pmu[:, tc % 512:tc % 512 + w], fu2[p][:, kk * P:(kk + 1) * P],
                                   z2qT[:, kk * MINE + tc:kk * MINE + tc + w],
                                   start=(kk == 0), stop=(kk == DBLK - 1))
                    if kk == DBLK - 1:
                        s_t.inc(mm)
                t_done_fw[p] = s_t.n
                S.wait_ge(s_t.h, t_done_fw[p])
                S.wait_ge(s_v.h, v_done_gp[q])
                s_s.inc(S.activation(gt2[q][:, tc:tc + w], pmg[:, tc % 512:tc % 512 + w],
                                     AF.Identity, scale=consts[:, 12:13],
                                     bias=consts[:, 9:10]))
                s_s.inc(S.activation(pt2[q][:, tc:tc + w], pmu[:, tc % 512:tc % 512 + w],
                                     AF.Identity, scale=consts[:, 4:5], bias=consts[:, 9:10]))
                s_done_gu[q] = s_s.n
                pm_g[0 if p == 0 else 2]["s"] = s_s.n
                pm_g[1 if p == 0 else 3]["s"] = s_s.n
            yield
            if q == 3:
                S.wait_ge(s_v.h, q4v[0])
                S.wait_ge(s_s.h, s_done_gu[q])
                s_s.inc(S.activation(sqrep4[:], gt4[:], AF.Tanh))
                V.wait_ge(s_s.h, s_s.n)
                vf(V.scalar_tensor_tensor(gt4[:], sqrep4[:], 1.0, gt4[:],
                                          ALU.add, ALU.mult))
                q4v[0] = s_v.n
                vf(V.tensor_tensor(qs[:, (ic - 3) * MINE:(ic + 1) * MINE], gt4[:],
                                   pt4[:], ALU.mult))
                for j in range(4):
                    v_done_gp[j] = s_v.n
                if ic == 3:
                    V.tensor_copy(mrow4[:], qs[:, 0:4 * MINE])
                    s_v.inc(V.tensor_copy(mrow24[:], qs[:, 0:4 * MINE]))
                else:
                    V.tensor_tensor(mrow4[:], mrow4[:],
                                    qs[:, (ic - 3) * MINE:(ic + 1) * MINE], ALU.max)
                    s_v.inc(V.tensor_tensor(mrow24[:], mrow24[:],
                                            qs[:, (ic - 3) * MINE:(ic + 1) * MINE],
                                            ALU.min))
                vf()
            yield
        vf(V.scalar_tensor_tensor(mrow4[:], mrow24[:], -1.0, mrow4[:],
                                  ALU.mult, ALU.max))
        vf(V.tensor_tensor(mrow4[:, 0:2 * MINE], mrow4[:, 0:2 * MINE],
                           mrow4[:, 2 * MINE:4 * MINE], ALU.max))
        vf(V.tensor_tensor(mrow[:, 0:MINE], mrow4[:, 0:MINE],
                           mrow4[:, MINE:2 * MINE], ALU.max))
        TE.wait_ge(s_v.h, s_v.n)
        for k in range(NMT):
            s_t.inc(TE.transpose(ptr[:, 0:P], mrow[:, k * P:(k + 1) * P], ident[:]))
            V.wait_ge(s_t.h, s_t.n)
            V.tensor_reduce(sqv[:, k:k + 1], ptr[:, 0:P], AX.X, ALU.max,
                            apply_absolute_value=True)
            vf()
            s_v.inc(V.tensor_copy(statw2[:, 15:16], sqv[:, k:k + 1]))
            TE.wait_ge(s_v.h, s_v.n)
            yield
        V.tensor_scalar(sqvr[:, 0:NMT], sqv[:, 0:NMT], 1e-5, 1.0 / 127.0,
                        ALU.max, ALU.mult)
        vf()
        V.reciprocal(sqv[:, 0:NMT], sqvr[:, 0:NMT])
        vf()
        V.wait_ge(s_t.h, xg["sqT_t"])
        V.tensor_copy(sqstg[:, 0:NMT], sqv[:, 0:NMT])
        vf()
        s_v.inc(V.tensor_copy(sqstg[:, NMT:2 * NMT], sqvr[:, 0:NMT]))
        TE.wait_ge(s_v.h, s_v.n)
        TE.wait_ge(s_v.h, ptrw_g[0])
        for j in range(2 * NMT):
            mm = TE.transpose(ptrw[0][0:1, j * P:(j + 1) * P], sqstg[:, j:j + 1], ident[:])
            if j == 2 * NMT - 1:
                s_t.inc(mm)
        xg["sqT_t"] = s_t.n
        S.wait_ge(s_t.h, s_t.n)
        s_s.inc(S.activation(sqTrow[0:1, 0:2 * NMT * P], ptrw[0][0:1, 0:2 * NMT * P],
                             AF.Identity, scale=consts[0:1, 7:8], bias=consts[0:1, 9:10]))
        V.wait_ge(s_s.h, s_s.n)
        s_v.inc(V.engine_nop())
        ptrw_g[0] = s_v.n
        TE.wait_ge(s_s.h, s_s.n)
        TE.wait_ge(s_s.h, xg["pmq_s"])
        for k in range(NMT):
            s_t.inc(TE.matmul(pmq[:, k * P:(k + 1) * P], onesrow[0:1, 0:P],
                              sqTrow[0:1, k * P:(k + 1) * P], start=True, stop=True))
            s_t.inc(TE.matmul(pmq[:, MINE + k * P:MINE + (k + 1) * P], onesrow[0:1, 0:P],
                              sqTrow[0:1, (NMT + k) * P:(NMT + k + 1) * P],
                              start=True, stop=True))
        S.wait_ge(s_t.h, s_t.n)
        S.wait_ge(s_v.h, xg["qs_v"])
        S.wait_ge(s_v.h, q4v[0])
        RB = 4 if 4 * MINE <= NMT * D else 2
        for rep in range(RB):
            s_s.inc(S.activation(sqrep4[:, rep * MINE:(rep + 1) * MINE],
                                 pmq[:, 0:MINE], AF.Identity,
                                 scale=consts[:, 7:8], bias=consts[:, 9:10]))
            s_s.inc(S.activation(sqrrep4[:, rep * MINE:(rep + 1) * MINE],
                                 pmq[:, MINE:2 * MINE], AF.Identity,
                                 scale=consts[:, 7:8], bias=consts[:, 9:10]))
        xg["pmq_s"] = s_s.n
        yield
        V.wait_ge(s_s.h, xg["pmq_s"])
        RW = RB * MINE
        for jj in range(IBLK // RB):
            sl = qs[:, jj * RW:(jj + 1) * RW]
            V.tensor_tensor(fTM[:, 0:RW], sl, sqrep4[:, 0:RW], ALU.mult)
            vf()
            V.tensor_scalar(fTM[:, 0:RW], fTM[:, 0:RW], MAGIC, -MAGIC,
                            ALU.add, ALU.add)
            vf()
            V.tensor_copy(qsc4[:, 0:RW], fTM[:, 0:RW])
            vf()
            s_v.inc(V.tensor_tensor(sl, qsc4[:, 0:RW], sqrrep4[:, 0:RW], ALU.mult))
            yield
        xg["qs_v"] = s_v.n
        TE.wait_ge(s_v.h, xg["qs_v"])
        t_done_wd = [0, 0]
        s_done_dn = [0, 0]
        v_done_fc = [0, 0]
        ld_wd = [0, 0]

        def dn_load(oc):
            p = oc % 2
            SY.wait_ge(s_t.h, t_done_wd[p])
            s_dw[p].inc(SY.dma_start(wdsl2[p][:, 0:IBLK * P].rearrange("p (kk o) -> p kk o", kk=IBLK),
                                     wdq_d[oc * INNER:(oc + 1) * INNER, :].rearrange(
                                         "(p kk) o -> p kk o", p=P)), 16)
            ld_wd[p] = s_dw[p].n

        dn_load(0)
        for oc in range(DBLK):
            p = oc % 2
            fq = oc % 4
            if oc + 1 < DBLK:
                dn_load(oc + 1)
            TE.wait_ge(s_dw[p].h, ld_wd[p])
            pmx = pm0 if p == 0 else pm1
            for tc in range(0, MINE, 512):
                w = min(512, MINE - tc)
                pm_wait(p)
                for kk in range(IBLK):
                    mm = TE.matmul(pmx[:, 0:w], wdsl2[p][:, kk * P:(kk + 1) * P],
                                   qs[:, kk * MINE + tc:kk * MINE + tc + w],
                                   start=(kk == 0), stop=(kk == IBLK - 1))
                    if kk == IBLK - 1:
                        s_t.inc(mm)
                t_done_wd[p] = s_t.n
                S.wait_ge(s_t.h, t_done_wd[p])
                S.wait_ge(s_v.h, gp_v[fq])
                S.wait_ge(s_t.h, fcm_t[fq])
                s_s.inc(S.activation(fcm2[fq][:, tc:tc + w], pmx[:, 0:w], AF.Identity,
                                     scale=consts[:, 5:6], bias=consts[:, 9:10]))
                s_done_dn[p] = s_s.n
                pm_g[p]["s"] = s_s.n
            yield
            TE.wait_ge(s_s.h, s_done_dn[p])
            TE.wait_ge(s_v.h, ptrw_g[p])
            ptw = ptrw[p]
            for k in range(NMT):
                mm = TE.transpose(ptw[:, k * P:(k + 1) * P], fcm2[fq][:, k * P:(k + 1) * P],
                                  ident[:])
                if k == NMT - 1:
                    s_t.inc(mm)
            t_done_wd[p] = s_t.n
            fcm_t[fq] = s_t.n
            V.wait_ge(s_t.h, t_done_wd[p])
            fTM_v = fTM.rearrange("p (k d) -> p k d", k=NMT)
            s_v.inc(V.tensor_copy(fTM_v[:, :, oc * P:(oc + 1) * P],
                                  ptw[:, 0:NMT * P].rearrange("p (k q) -> p k q", k=NMT)))
            ptrw_g[p] = s_v.n
            yield
        vf()
        for k in range(NMT):
            s_v.inc(ln_stats(ln2s[:, 2 * k:2 * k + 2], fTM[:, k * D:(k + 1) * D]))
        S.wait_ge(s_v.h, s_v.n)
        l4v = ln2s.ap().rearrange("p (k two) -> p two k", two=2)
        s_s.inc(rsqrt_cols(l4v[:, 1, :], sxv[:, 0:NMT], statw2[:, 0:NMT]))
        V.wait_ge(s_s.h, s_s.n)
        vf(V.tensor_tensor(statw2[:, 0:NMT], l4v[:, 0, :], sxv[:, 0:NMT], ALU.mult))
        s_v.inc(V.tensor_scalar(statw2[:, 0:NMT], statw2[:, 0:NMT], -1.0, None, ALU.mult))
        S.wait_ge(s_v.h, s_v.n)
        for k in range(NMT):
            mm = S.activation(zq[:, k * D:(k + 1) * D], fTM[:, k * D:(k + 1) * D],
                              AF.Identity, scale=sxv[:, k:k + 1], bias=statw2[:, k:k + 1])
            if k == NMT - 1:
                s_s.inc(mm)
        xg["ln4_s"] = s_s.n
        V.wait_ge(s_s.h, s_s.n)
        st_done_o = list(xg["o_st"])
        osc2 = [oscr, fscr]
        for k in range(NMT):
            p = k % 2
            V.wait_ge(s_st.h, st_done_o[p])
            s_v.inc(V.tensor_tensor(osc2[p][:, 0:D], um[:, k * D:(k + 1) * D],
                                    zq[:, k * D:(k + 1) * D], ALU.add))
            SY.wait_ge(s_v.h, s_v.n)
            s_st.inc(SY.dma_start(out_h[sidx * MINE + k * P:sidx * MINE + (k + 1) * P, :],
                                  osc2[p][:, 0:D]), 16)
            st_done_o[p] = s_st.n
            yield
        xg["o_st"] = list(st_done_o)
        xg["um_rd_v"] = s_v.n


    def gen_MIX(sidx):
        for _ in gen_A(sidx):
            yield
        for _ in gen_B(sidx):
            yield
        for _ in gen_C(sidx):
            yield

    def gen_DFull(sidx):
        for _ in gen_C2(sidx):
            yield
        for _ in gen_D_head(sidx):
            yield
        for _ in gen_D(sidx):
            yield

    def drive(mix, dside, rm=3, rd=4):
        di = 0
        while True:
            for _ in range(rm):
                try:
                    next(mix)
                except StopIteration:
                    return dside[di:]
            for _ in range(rd):
                if di < len(dside):
                    try:
                        next(dside[di])
                    except StopIteration:
                        di += 1

    dside = []
    for s in range(NSUP):
        if s >= 1:
            dside.append(gen_DFull(s - 1))
        dside = drive(gen_MIX(s), dside)
    dside.append(gen_DFull(NSUP - 1))
    for g in dside:
        for _ in g:
            pass
    barrier()

    SY.wait_ge(s_st.h, s_st.n)
    for cm in reversed(ctxs):
        cm.__exit__(None, None, None)
    return nc


def _prep_inputs(inputs, cfg):
    D, N = cfg["D"], cfg["N"]
    DBLK = D // P
    f = np.float32

    def arr(k):
        return np.asarray(inputs[k], f)

    gam = np.ascontiguousarray(arr("L0_attn_gamma").reshape(DBLK, P).T)
    bet = np.ascontiguousarray(arr("L0_attn_beta").reshape(DBLK, P).T)
    a_pp = np.ascontiguousarray(
        arr("L0_attn_A_log").reshape(DBLK, P, N).transpose(1, 0, 2).reshape(P, DBLK * N))
    return {
        "wdtT": np.ascontiguousarray(arr("L0_attn_W_dt").T),
        "wbT": np.ascontiguousarray(arr("L0_attn_W_B").T),
        "wcT": np.ascontiguousarray(arr("L0_attn_W_C").T),
        "gam_pp": gam, "bet_pp": bet, "a_pp": a_pp,
        "bdt_row": np.ascontiguousarray(arr("L0_attn_b_dt").reshape(1, D)),
        "dparam_row": np.ascontiguousarray(arr("L0_attn_D_param").reshape(1, D)),
        "wgT": np.ascontiguousarray(
            arr("L0_ffn_w_gate").T.reshape(D // P, P, -1, P)
            .transpose(2, 1, 0, 3).reshape(-1, P)),
        "wuT": np.ascontiguousarray(
            arr("L0_ffn_w_up").T.reshape(D // P, P, -1, P)
            .transpose(2, 1, 0, 3).reshape(-1, P)),
        "wdT": np.ascontiguousarray(
            arr("L0_ffn_w_down").T.reshape(-1, P, D // P, P)
            .transpose(2, 1, 0, 3).reshape(-1, P)),
    }


def kernel(**inputs):
    x = np.asarray(inputs["L0_attn_x"], np.float32)
    B, T, D = x.shape
    cfg = {"T": T, "D": D, "N": int(inputs["L0_attn_A_log"].shape[1]),
           "INNER": int(inputs["L0_ffn_w_gate"].shape[0]), "TCH": 512}
    nc = build_program(cfg)
    base = _prep_inputs(inputs, cfg)
    in_maps = []
    for c in range(8):
        b, j = c // 2, c % 2
        m = dict(base)
        m["x"] = np.ascontiguousarray(x[b])
        xb = x[b].reshape(T // P, P, D)
        m["xm"] = np.ascontiguousarray(xb[j::2].reshape(T // 2, D))
        m["parf"] = np.full((1, 1), j, np.float32)
        in_maps.append(m)
    res = run_bass_kernel_spmd(nc, in_maps, core_ids=list(range(8)))
    out = np.empty((B, T, D), np.float32)
    TCH = cfg["TCH"]
    MINE = TCH // 2
    NMT = MINE // P
    for c in range(8):
        b, j = c // 2, c % 2
        o = np.asarray(res.results[c]["out"])
        for s in range(T // TCH):
            for k in range(NMT):
                trow = s * TCH + (2 * k + j) * P
                out[b, trow:trow + P, :] = o[s * MINE + k * P:s * MINE + (k + 1) * P, :]
    return out


# version 41
# speedup vs baseline: 1.0194x; 1.0004x over previous
import numpy as np
import concourse.bass as bass
import concourse.mybir as mybir
from concourse.bass_utils import run_bass_kernel_spmd

F32 = mybir.dt.float32
BF16 = mybir.dt.bfloat16
I32 = mybir.dt.int32
AF = mybir.ActivationFunctionType
ALU = mybir.AluOpType
AX = mybir.AxisListType
ts = bass.ts

MAGIC = 12582912.0
LN_EPS = 1e-5
P = 128


def build_program(cfg):
    T, D, N, INNER = cfg["T"], cfg["D"], cfg["N"], cfg["INNER"]
    TCH = cfg["TCH"]
    DBG = cfg.get("debug", False)
    DBLK = D // P
    NSUP = T // TCH
    NT = TCH // P
    MINE = TCH // 2
    NMT = MINE // P
    IBLK = INNER // P
    WQC = min(INNER, 1024)

    nc = bass.Bass("TRN2", target_bir_lowering=False)

    def din(name, shape, dtype=F32):
        return nc.dram_tensor(name, shape, dtype, kind="ExternalInput").ap()

    x_h = din("x", [T, D])
    xm_h = din("xm", [T // 2, D])
    parf_h = din("parf", [1, 1])
    wdtT_h = din("wdtT", [D, D])
    wbT_h = din("wbT", [D, N])
    wcT_h = din("wcT", [D, N])
    gam_h = din("gam_pp", [P, DBLK])
    bet_h = din("bet_pp", [P, DBLK])
    bdt_h = din("bdt_row", [1, D])
    apar_h = din("a_pp", [P, DBLK * N])
    dpar_h = din("dparam_row", [1, D])
    wgT_h = din("wgT", [INNER // P * D, P])
    wuT_h = din("wuT", [INNER // P * D, P])
    wdT_h = din("wdT", [D // P * INNER, P])

    out_h = nc.dram_tensor("out", [NSUP * MINE, D], F32, kind="ExternalOutput").ap()

    def dsc(name, shape, dtype):
        return nc.dram_tensor(name, shape, dtype, kind="Internal").ap()

    bT_d = dsc("bT_d", [N, TCH], BF16)
    cT_d = dsc("cT_d", [N, TCH], BF16)
    wgq_d = dsc("wgq_d", [INNER // P * D, P], BF16)
    wuq_d = dsc("wuq_d", [INNER // P * D, P], BF16)
    wdq_d = dsc("wdq_d", [D // P * INNER, P], BF16)
    brow_d = dsc("brow_d", [1, D + 2 * N], F32)
    sq_d = dsc("sq_d", [2, MINE], BF16)
    wqs_d = dsc("wqs_d", [3, 2], F32)

    dbg = {}
    if DBG:
        for nm, shape in [("d_znT", [D, TCH]), ("d_dtT", [D, TCH]),
                          ("d_bT", [N, TCH]), ("d_cT", [N, TCH]),
                          ("d_wT", [D, TCH]), ("d_y", [TCH, D]),
                          ("d_u", [MINE, D]), ("d_z2q", [D, MINE]),
                          ("d_q", [INNER, MINE]), ("d_wgq", [D, INNER]),
                          ("d_f", [MINE, D]),
                          ("d_h", [P, TCH])]:
            dbg[nm] = nc.dram_tensor(nm, shape, F32, kind="ExternalOutput").ap()

    ctxs = []

    def sb(name, shape, dtype):
        cm = nc.sbuf_tensor(name, shape, dtype)
        t = cm.__enter__()
        ctxs.append(cm)
        return t

    def psb(name, shape, dtype=F32):
        cm = nc.psum_tensor(name, shape, dtype)
        t = cm.__enter__()
        ctxs.append(cm)
        return t

    ident = sb("ident", [P, P], BF16)
    onescol = sb("onescol", [P, 1], BF16)
    onesrow = sb("onesrow", [1, P], BF16)
    sqstg = sb("sqstg", [P, 8], BF16)
    sel_all = sb("sel_all", [N, N * P], BF16)
    bcT_all = sb("bcT_all", [N, 2 * TCH], BF16)
    sqTrow = sb("sqTrow", [1, 8 * P], BF16)
    gam = sb("gam", [P, DBLK], F32)
    bet = sb("bet", [P, DBLK], F32)
    apar = sb("apar", [P, DBLK * N], F32)
    drep = sb("drep", [P, D], BF16)
    bdtc = sb("bdtc", [P, DBLK], F32)
    bbc = sb("bbc", [N, 2], F32)
    parc = sb("parc", [P, 2], F32)
    consts = sb("consts", [P, 14], F32)
    wdtg = sb("wdtg", [P, DBLK * D], BF16)
    wbg = sb("wbg", [P, DBLK * N], BF16)
    wcg = sb("wcg", [P, DBLK * N], BF16)
    stats = sb("stats", [P, NT * 2], F32)
    sc1 = sb("sc1", [P, NT], F32)
    r1v = sb("r1v", [P, NT], F32)
    mnM2 = [sb("mnM0", [P, NMT], F32), sb("mnM1", [P, NMT], F32)]
    r1M2 = [sb("r1M0", [P, NMT], F32), sb("r1M1", [P, NMT], F32)]
    sc2 = sb("sc2", [P, NT], F32)
    statw = sb("statw", [P, 16], F32)
    statw2 = sb("statw2", [P, 16], F32)
    ln2s = sb("ln2s", [P, 2 * NMT], F32)
    ln3s = sb("ln3s", [P, 2 * NMT], F32)
    ln4s = sb("ln4s", [P, 2], F32)
    sxv = sb("sxv", [P, NMT], F32)
    sqv = sb("sqv", [P, NMT], F32)
    sqvr = sb("sqvr", [P, NMT], F32)
    accs = sb("accs", [P, DBLK * TCH], BF16)
    state = sb("state", [P, DBLK * N], F32)
    xts = sb("xts", [P, NT * D], F32)
    xt2 = [xts[:, 0:D], xts[:, D:2 * D]]
    xt0 = xt2[0]
    znt2 = [sb("znt", [P, D], BF16), sb("zntb", [P, D], BF16)]
    znt = znt2[0]
    bns = sb("bns", [P, 16], F32)

    a_b1 = sb("a_b1", [P, 2 * DBLK * TCH + IBLK * MINE], BF16)
    dtT = a_b1[:, 0:DBLK * TCH]
    wpT = a_b1[:, DBLK * TCH:2 * DBLK * TCH]
    qs = a_b1[:, 2 * DBLK * TCH:2 * DBLK * TCH + IBLK * MINE]
    a_b2 = sb("a_b2", [P, DBLK * TCH + 2 * NMT * D + MINE + DBLK * MINE], BF16)
    znT = a_b2[:, 0:DBLK * TCH]
    _o2 = DBLK * TCH
    yam = a_b2[:, _o2:_o2 + NMT * D]
    z1m = a_b2[:, _o2 + NMT * D:_o2 + 2 * NMT * D]
    ytd = a_b2[:, _o2 + 2 * NMT * D:_o2 + 2 * NMT * D + MINE]
    z2qT = a_b2[:, _o2 + 2 * NMT * D + MINE:_o2 + 2 * NMT * D + MINE + DBLK * MINE]
    SCRW = max(4 * TCH, 2 * D + 2 * N)
    FCOLS = max(DBLK * TCH, SCRW, 2 * NMT * D + 3 * D + 1024, NMT * D + 6 * WQC)
    a_f = sb("a_f", [P, FCOLS], F32)
    scr = a_f[:, 0:SCRW]

    wsc = a_f[:, NMT * D:NMT * D + WQC]
    um = a_f[:, 0:NMT * D]
    fTM = a_f[:, NMT * D:2 * NMT * D]
    fscr = a_f[:, 2 * NMT * D:2 * NMT * D + D]
    zscr = a_f[:, 2 * NMT * D + D:2 * NMT * D + 2 * D]
    oscr = a_f[:, 2 * NMT * D + 2 * D:2 * NMT * D + 3 * D]
    _bs = 2 * NMT * D + 3 * D
    bscr2 = [a_f[:, _bs:_bs + 512], a_f[:, _bs + 512:_bs + 1024]]
    a_b3 = sb("a_b3", [P, 21 * TCH + 4 * DBLK * P + 33 * MINE + 2 * IBLK * P + NMT * D], BF16)
    brep2 = [a_b3[:, j * TCH:(j + 1) * TCH] for j in range(5)]
    crep2 = [a_b3[:, (5 + j) * TCH:(6 + j) * TCH] for j in range(5)]
    abar2 = [a_b3[:, (10 + j) * TCH:(11 + j) * TCH] for j in range(5)]
    bxt2 = [a_b3[:, 15 * TCH:16 * TCH], a_b3[:, 16 * TCH:17 * TCH]]
    ht2 = [a_b3[:, 17 * TCH:18 * TCH], a_b3[:, 18 * TCH:19 * TCH]]
    cht2 = [a_b3[:, 19 * TCH:20 * TCH], a_b3[:, 20 * TCH:21 * TCH]]
    betac = sb("betac", [P, DBLK], BF16)
    _fb = 21 * TCH
    fw2 = [a_b3[:, _fb:_fb + DBLK * P], a_b3[:, _fb + DBLK * P:_fb + 2 * DBLK * P]]
    fu2 = [a_b3[:, _fb + 2 * DBLK * P:_fb + 3 * DBLK * P],
           a_b3[:, _fb + 3 * DBLK * P:_fb + 4 * DBLK * P]]
    _o = _fb + 4 * DBLK * P
    gt2 = [a_b3[:, _o + j * MINE:_o + (j + 1) * MINE] for j in range(4)]
    gt4 = a_b3[:, _o:_o + 4 * MINE]
    pt2 = [a_b3[:, _o + (4 + j) * MINE:_o + (5 + j) * MINE] for j in range(4)]
    pt4 = a_b3[:, _o + 4 * MINE:_o + 8 * MINE]
    mrow4 = a_b3[:, _o + 8 * MINE:_o + 12 * MINE]
    mrow24 = a_b3[:, _o + 12 * MINE:_o + 16 * MINE]
    mrow = a_b3[:, _o + 16 * MINE:_o + 17 * MINE]
    sqrep4 = a_b3[:, _o + 17 * MINE:_o + 21 * MINE]
    sqrrep4 = a_b3[:, _o + 21 * MINE:_o + 25 * MINE]
    qsc4 = a_b3[:, _o + 25 * MINE:_o + 29 * MINE]
    fcm2 = [a_b3[:, _o + (29 + j) * MINE:_o + (30 + j) * MINE] for j in range(4)]
    fc4 = a_b3[:, _o + 29 * MINE:_o + 33 * MINE]
    _w = _o + 33 * MINE
    wdsl2 = [a_b3[:, _w:_w + IBLK * P], a_b3[:, _w + IBLK * P:_w + 2 * IBLK * P]]
    zqb = a_b3[:, _w + 2 * IBLK * P:_w + 2 * IBLK * P + NMT * D]
    AB3 = _w + 2 * IBLK * P + NMT * D
    
    pm0 = psb("pm0", [P, 512])
    pm1 = psb("pm1", [P, 512])
    pm2 = psb("pm2", [P, 512])
    pm3 = psb("pm3", [P, 512])
    ptrp = psb("ptrp", [P, 2 * P], BF16)
    ptr = ptrp[:, 0:P]
    ptr2 = ptrp[:, P:2 * P]
    ptrw = [psb("ptrw0", [P, DBLK * P], BF16), psb("ptrw1", [P, DBLK * P], BF16)]
    pmq = psb("pmq", [P, 512])

    class Sem:
        def __init__(self, name):
            self.h = nc.alloc_semaphore(name=name)
            self.n = 0

        def inc(self, instr, k=1):
            instr.then_inc(self.h, k)
            self.n += k
            return instr

    s_ld, s_g, s_v, s_s, s_t, s_st, s_bar, s_ge = (Sem(x) for x in
        ("ld", "g", "v", "s", "t", "st", "bar", "ge"))
    s_rp = [Sem("rp0"), Sem("rp1"), Sem("rp2")]
    s_x = [Sem("x0"), Sem("x1"), Sem("x2")]
    s_fw = [Sem("fw0"), Sem("fw1")]
    s_dw = [Sem("dw0"), Sem("dw1")]
    s_gx = [Sem(f"gx{i}") for i in range(6)]
    s_gq3 = [Sem(f"gq{i}") for i in range(3)]
    s_gl3 = [Sem(f"gl{i}") for i in range(3)]

    V, S, TE, G, SY = nc.vector, nc.scalar, nc.tensor, nc.gpsimd, nc.sync

    def vf(instr=None):
        if instr is None:
            instr = V.engine_nop()
        s_v.inc(instr)
        V.wait_ge(s_v.h, s_v.n)
        return instr

    def sf(instr=None):
        if instr is None:
            instr = S.copy(statw[0:1, 13:14], consts[0:1, 7:8])
        s_s.inc(instr)
        S.wait_ge(s_s.h, s_s.n)
        return instr
    ALL_E = ("sync", "vector", "scalar", "tensor", "gpsimd")

    def barrier():
        SY.wait_ge(s_ld.h, s_ld.n)
        SY.wait_ge(s_st.h, s_st.n)
        G.wait_ge(s_g.h, s_g.n)
        G.wait_ge(s_ge.h, s_ge.n)
        for e in ALL_E:
            eng = getattr(nc, e)
            if e != "sync":
                eng.drain()
            eng.sem_inc(s_bar.h, 1)
        s_bar.n += len(ALL_E)
        for e in ALL_E:
            getattr(nc, e).wait_ge(s_bar.h, s_bar.n)

    NG = (D + 511) // 512

    def ln_stats(dst2, src):
        V.wait_ge(s_v.h, s_v.n)
        for gi in range(NG):
            lo, hi = gi * 512, min(D, (gi + 1) * 512)
            V.bn_stats(bns[:, 6 * gi:6 * gi + 6], src[:, lo:hi])
            vf()
        return V.bn_aggr(dst2, bns[:, 0:6 * NG])

    def rsqrt_cols(src_ap, dst_ap, tmp_ap):
        S.activation(tmp_ap, src_ap, AF.Ln, bias=LN_EPS)
        sf()
        return S.activation(dst_ap, tmp_ap, AF.Exp, scale=-0.5)

    ptrw_g = [0, 0]
    gp_v = [0, 0, 0, 0]
    q4v = [0]
    fcm_t = [0, 0, 0, 0]
    pm_g = {i: {"s": 0, "v": 0} for i in range(4)}
    xg = {"a_scal": 0, "zn_s": [0, 0], "znT_rd_t": 0, "bc_st": 0,
          "rp_fin": [0, 0], "acc_rd_t": 0, "yam_rd_s": 0, "um_rd_v": 0,
          "z2q_v": 0, "qs_v": 0, "o_st": [0, 0], "zn_v": 0,
          "pmq_s": 0, "sqT_t": 0, "bct_s": 0, "rep_t": 0, "ln4_s": 0}

    def pm_wait(bank):
        TE.wait_ge(s_s.h, pm_g[bank]["s"])
        TE.wait_ge(s_v.h, pm_g[bank]["v"])

    def finish():
        SY.wait_ge(s_ld.h, s_ld.n)
        SY.wait_ge(s_st.h, s_st.n)
        for cm in reversed(ctxs):
            cm.__exit__(None, None, None)
        return nc

    STOP = cfg.get("stop")

    V.memset(consts[:, 6:7], -0.5)
    vf()
    V.memset(consts[:, 7:8], 1.0)
    vf()
    V.memset(consts[:, 8:9], 0.5)
    vf()
    V.memset(consts[:, 9:10], 0.0)
    vf()
    V.memset(consts[:, 10:11], LN_EPS)
    vf()
    s_v.inc(V.memset(consts[:, 11:12], -0.5))
    for val, col in ((-0.5, 6), (1.0, 7), (0.5, 8), (0.0, 9), (LN_EPS, 10)):
        nc.const_aps.register(F32, val, consts[:, col:col + 1])
    s_ld.inc(SY.dma_start(gam[:], gam_h[:]), 16)
    s_ld.inc(SY.dma_start(bet[:], bet_h[:]), 16)
    s_ld.inc(SY.dma_start(apar[:], apar_h[:]), 16)
    s_g.inc(G.dma_start(drep[:], dpar_h.partition_broadcast(P)), 16)
    s_ld.inc(SY.dma_start(parc[:, 0:1], parf_h.partition_broadcast(P)), 16)
    s_ld.inc(SY.dma_start(scr[0:1, D + 2 * N:2 * D + 2 * N], bdt_h[0:1, 0:D]), 16)
    s_g.inc(G.dma_start(wdtg.ap().rearrange("p (c o) -> p c o", c=DBLK),
                        wdtT_h.rearrange("(c p) o -> p c o", p=P)), 16)
    s_g.inc(G.dma_start(wbg.ap().rearrange("p (c n) -> p c n", c=DBLK),
                        wbT_h.rearrange("(c p) n -> p c n", p=P)), 16)
    s_g.inc(G.dma_start(wcg.ap().rearrange("p (c n) -> p c n", c=DBLK),
                        wcT_h.rearrange("(c p) n -> p c n", p=P)), 16)

    G.wait_ge(s_g.h, s_g.n)
    s_ge.inc(G.memset(onesrow[:], 1.0))
    s_ge.inc(G.memset(sel_all[:], 0.0))
    G.wait_ge(s_ge.h, s_ge.n)
    for n_ in range(N):
        s_ge.inc(G.affine_select(out=sel_all[0:N, n_ * P:(n_ + 1) * P],
                                 in_=sel_all[0:N, n_ * P:(n_ + 1) * P],
                                 compare_op=ALU.not_equal, fill=1.0, base=-n_,
                                 pattern=[[0, P]], channel_multiplier=1))
    s_ge.inc(G.memset(ident[:], 0.0))
    G.wait_ge(s_ge.h, s_ge.n)
    s_ge.inc(G.affine_select(out=ident[:], in_=ident[:], compare_op=ALU.not_equal,
                             fill=1.0, base=0, pattern=[[-1, P]], channel_multiplier=1))

    S.wait_ge(s_ld.h, s_ld.n)
    S.wait_ge(s_v.h, s_v.n)
    S.activation(apar[:], apar[:], AF.Exp)
    sf()
    s_s.inc(S.mul(apar[:], apar[:], -1.0))

    V.wait_ge(s_ld.h, s_ld.n)
    V.wait_ge(s_g.h, s_g.n)
    V.tensor_scalar(parc[:, 1:2], parc[:, 0:1], -1.0, 1.0, ALU.mult, ALU.add)
    vf()
    V.tensor_copy(onescol[:], consts[:, 7:8])
    vf()
    s_v.inc(V.tensor_copy(betac[:, 0:DBLK], bet[:]))
    TE.wait_ge(s_v.h, s_v.n)
    TE.wait_ge(s_g.h, s_g.n)

    for oc in range(0, D, 512):
        w = min(512, D - oc)
        for c in range(DBLK):
            mm = TE.matmul(pm0[0:1, 0:w], betac[:, c:c + 1],
                           wdtg[:, c * D + oc:c * D + oc + w],
                           start=(c == 0), stop=(c == DBLK - 1))
            if c == DBLK - 1:
                s_t.inc(mm)
        V.wait_ge(s_t.h, s_t.n)
        s_v.inc(V.tensor_copy(scr[0:1, oc:oc + w], pm0[0:1, 0:w]))
        TE.wait_ge(s_v.h, s_v.n)
        pm_g[0]["v"] = s_v.n
    for wi, wsrc in enumerate((wbg, wcg)):
        for c in range(DBLK):
            mm = TE.matmul(pm1[0:1, 0:N], betac[:, c:c + 1], wsrc[:, c * N:(c + 1) * N],
                           start=(c == 0), stop=(c == DBLK - 1))
            if c == DBLK - 1:
                s_t.inc(mm)
        V.wait_ge(s_t.h, s_t.n)
        s_v.inc(V.tensor_copy(scr[0:1, D + wi * N:D + (wi + 1) * N], pm1[0:1, 0:N]))
        TE.wait_ge(s_v.h, s_v.n)
        pm_g[1]["v"] = s_v.n
    V.wait_ge(s_ld.h, s_ld.n)
    s_v.inc(V.tensor_tensor(scr[0:1, 0:D], scr[0:1, 0:D], scr[0:1, D + 2 * N:2 * D + 2 * N], ALU.add))
    SY.wait_ge(s_v.h, s_v.n)
    s_st.inc(SY.dma_start(brow_d[0:1, :], scr[0:1, 0:D + 2 * N]), 16)
    SY.wait_ge(s_st.h, s_st.n)
    s_ld.inc(SY.dma_start(bdtc.ap().rearrange("p (one c) -> p one c", one=1),
                          brow_d[0:1, 0:D].rearrange("one (c p) -> p one c", p=P), allow_slow_non_contiguous=True), 16)
    s_ld.inc(SY.dma_start(bbc[0:N, 0:2].rearrange("n (one two) -> n one two", one=1),
                          brow_d[0:1, D:D + 2 * N].rearrange("one (two n) -> n one two", n=N), allow_slow_non_contiguous=True), 16)

    V.wait_ge(s_s.h, s_s.n)
    V.wait_ge(s_ld.h, s_ld.n)
    for c in range(DBLK):
        V.tensor_scalar(wdtg[:, c * D:(c + 1) * D], wdtg[:, c * D:(c + 1) * D],
                        gam[:, c:c + 1], None, ALU.mult)
        V.tensor_scalar(wbg[:, c * N:(c + 1) * N], wbg[:, c * N:(c + 1) * N],
                        gam[:, c:c + 1], None, ALU.mult)
        V.tensor_scalar(wcg[:, c * N:(c + 1) * N], wcg[:, c * N:(c + 1) * N],
                        gam[:, c:c + 1], None, ALU.mult)
    barrier()


    if STOP == "prologue":
        return finish()
    wsc2 = [a_f[:, NMT * D + i * WQC:NMT * D + (i + 1) * WQC] for i in range(6)]
    wqa = [a_b3[:, i * WQC:(i + 1) * WQC] for i in range(6)]
    wqb = [a_b3[:, (6 + i) * WQC:(7 + i) * WQC] for i in range(6)]


    def gen_quant(w_h, wq_d, nrows, slot):
        CH_ROWS = WQC
        chunks = list(range(0, nrows, CH_ROWS))
        bufs = (2 * slot, 2 * slot + 1)
        ldsem = (s_gx[2 * slot], s_gx[2 * slot + 1])
        stsem = (s_fw[0], s_fw[1], s_dw[0])[slot]
        pmx = (pm0, pm1, pm2)[slot]
        sw = 4 * slot
        nmm = 0
        for r0 in chunks:
            rr = min(CH_ROWS, nrows - r0)
            nmm += (rr + 511) // 512
        k = 0
        s_done_abs = [0, 0]
        t_done_abs = [0, 0]
        ld_w = [0, 0]
        for ci, r0 in enumerate(chunks):
            b = ci % 2
            p = bufs[b]
            rr = min(CH_ROWS, nrows - r0)
            w = rr
            G.wait_ge(s_s.h, s_done_abs[b])
            ldsem[b].inc(G.dma_start(wsc2[p][:, 0:w].rearrange("p (a q) -> p a q", q=P),
                                     w_h[r0:r0 + rr, :].rearrange("(p a) q -> p a q", p=P)), 16)
            ld_w[b] = ldsem[b].n
            S.wait_ge(ldsem[b].h, ld_w[b])
            S.wait_ge(s_t.h, t_done_abs[b])
            s_s.inc(S.activation(wqa[p][:, 0:w], wsc2[p][:, 0:w], AF.Abs))
            s_done_abs[b] = s_s.n
            TE.wait_ge(s_s.h, s_done_abs[b])
            if ci == 0:
                pm_wait(slot)
            noff = (w + 511) // 512
            for oi, off in enumerate(range(0, w, 512)):
                ww = min(512, w - off)
                mm = TE.matmul(pmx[0:1, 0:ww], onescol[:], wqa[p][:, off:off + ww],
                               start=(k == 0), stop=(k == nmm - 1))
                if oi == noff - 1:
                    s_t.inc(mm)
                k += 1
            t_done_abs[b] = s_t.n
            yield
        V.wait_ge(s_t.h, s_t.n)
        s_v.inc(V.tensor_reduce(statw[0:1, sw:sw + 1], pmx[0:1, 0:512], AX.X, ALU.add))
        pm_g[slot]["v"] = s_v.n
        V.wait_ge(s_v.h, s_v.n)
        V.tensor_scalar(statw[0:1, sw + 1:sw + 2], statw[0:1, sw:sw + 1],
                        1.0 / float(nrows * P), 1e-5, ALU.mult, ALU.max)
        vf()
        V.reciprocal(statw[0:1, sw + 2:sw + 3], statw[0:1, sw + 1:sw + 2])
        vf()
        s_v.inc(V.tensor_scalar(statw[0:1, sw + 3:sw + 4], statw[0:1, sw + 1:sw + 2],
                                0.5, None, ALU.mult))
        G.wait_ge(s_v.h, s_v.n)
        s_gq3[slot].inc(G.dma_start(wqs_d[slot:slot + 1, 0:2], statw[0:1, sw + 2:sw + 4]), 16)
        G.wait_ge(s_gq3[slot].h, s_gq3[slot].n)
        s_gl3[slot].inc(G.dma_start(consts[:, slot:slot + 1],
                                    wqs_d[slot:slot + 1, 0:1].partition_broadcast(P)), 16)
        s_gl3[slot].inc(G.dma_start(consts[:, 3 + slot:4 + slot],
                                    wqs_d[slot:slot + 1, 1:2].partition_broadcast(P)), 16)
        gl_done = s_gl3[slot].n
        yield
        S.wait_ge(s_gl3[slot].h, gl_done)
        s_done_sg = list(s_done_abs)
        v_done_sg = [0, 0]
        st_all = stsem.n
        for ci, r0 in enumerate(chunks):
            b = ci % 2
            p = bufs[b]
            rr = min(CH_ROWS, nrows - r0)
            w = rr
            G.wait_ge(s_s.h, s_done_sg[b])
            ldsem[b].inc(G.dma_start(wsc2[p][:, 0:w].rearrange("p (a q) -> p a q", q=P),
                                     w_h[r0:r0 + rr, :].rearrange("(p a) q -> p a q", p=P)), 16)
            ld_w[b] = ldsem[b].n
            S.wait_ge(ldsem[b].h, ld_w[b])
            S.wait_ge(s_v.h, v_done_sg[b])
            S.wait_ge(stsem.h, st_all)
            S.activation(wqa[p][:, 0:w], wsc2[p][:, 0:w], AF.Sign,
                         scale=consts[:, slot:slot + 1], bias=consts[:, 6:7])
            s_s.inc(S.activation(wqb[p][:, 0:w], wsc2[p][:, 0:w], AF.Sign,
                                 scale=consts[:, slot:slot + 1], bias=consts[:, 8:9]))
            s_done_sg[b] = s_s.n
            V.wait_ge(s_s.h, s_done_sg[b])
            s_v.inc(V.tensor_tensor(wqa[p][:, 0:w], wqa[p][:, 0:w], wqb[p][:, 0:w],
                                    ALU.add))
            v_done_sg[b] = s_v.n
            SY.wait_ge(s_v.h, v_done_sg[b])
            stsem.inc(SY.dma_start(wq_d[r0:r0 + rr, :].rearrange("(p a) q -> p a q", p=P),
                                   wqa[p][:, 0:w].rearrange("p (a q) -> p a q", q=P)), 16)
            if ci >= 1:
                st_all = stsem.n - 16
            yield
        SY.wait_ge(stsem.h, stsem.n)

    qgens = [gen_quant(wgT_h, wgq_d, INNER // P * D, 0),
             gen_quant(wuT_h, wuq_d, INNER // P * D, 1),
             gen_quant(wdT_h, wdq_d, D // P * INNER, 2)]
    live = list(qgens)
    while live:
        for g in list(live):
            try:
                next(g)
            except StopIteration:
                live.remove(g)
    V.wait_ge(s_gl3[0].h, s_gl3[0].n)
    s_v.inc(V.tensor_scalar(consts[:, 12:13], consts[:, 3:4], 0.5, None, ALU.mult))
    barrier()

    if STOP == "quant":
        return finish()
    s_v.inc(V.memset(state[:], 0.0))
    vf()

    def gen_A(sidx):
        t0 = sidx * TCH
        kt0 = t0 // P
        v_done_x = [0, 0]
        ld_x = [0, 0]
        V.wait_ge(s_s.h, xg["a_scal"])
        a_st = 0
        for k in range(NT):
            p = k % 2
            if k < 2:
                SY.wait_ge(s_s.h, xg["zn_s"][0])
                SY.wait_ge(s_s.h, xg["zn_s"][1])
            s_x[p].inc(SY.dma_start(xts[:, k * D:(k + 1) * D],
                                    x_h[t0 + k * P:t0 + (k + 1) * P, :]), 16)
            ld_x[p] = s_x[p].n
            V.wait_ge(s_x[p].h, ld_x[p])
            s_v.inc(ln_stats(stats[:, 2 * k:2 * k + 2], xts[:, k * D:(k + 1) * D]))
            v_done_x[p] = s_v.n
            a_st = s_v.n
            yield
        mview = stats.ap().rearrange("p (k two) -> p two k", two=2)
        mean_v = mview[:, 0, :]
        var_v = mview[:, 1, :]
        S.wait_ge(s_v.h, a_st)
        s_s.inc(rsqrt_cols(var_v, sc1[:, 0:NT], statw[:, 0:NT]))
        a_rs = s_s.n
        yield
        V.wait_ge(s_s.h, a_rs)
        V.tensor_copy(r1v[:, 0:NT], sc1[:, 0:NT])
        vf()
        V.tensor_tensor(statw[:, 0:NT], sc1[:, 0:NT], sc1[:, 0:NT], ALU.mult)
        vf()
        s_v.inc(V.tensor_tensor(statw[:, 0:NT], statw[:, 0:NT], var_v, ALU.mult))
        S.wait_ge(s_v.h, s_v.n)
        s_s.inc(rsqrt_cols(statw[:, 0:NT], statw[:, NT:2 * NT], statw[:, 0:NT]))
        a_rs2 = s_s.n
        yield
        V.wait_ge(s_s.h, a_rs2)
        V.tensor_tensor(sc1[:, 0:NT], sc1[:, 0:NT], statw[:, NT:2 * NT], ALU.mult)
        vf()
        V.tensor_tensor(sc2[:, 0:NT], mean_v, sc1[:, 0:NT], ALU.mult)
        vf()
        vf()
        V.tensor_scalar(sc2[:, 0:NT], sc2[:, 0:NT], -1.0, None, ALU.mult)
        vf()
        mev = stats.ap().rearrange("p (k par mv) -> p par mv k", par=2, mv=2)
        for (dst, ev, od) in ((mnM2[sidx % 2], mev[:, 0, 0, :], mev[:, 1, 0, :]),
                              (r1M2[sidx % 2], r1v.ap().rearrange("p (k two) -> p two k", two=2)[:, 0, :],
                               r1v.ap().rearrange("p (k two) -> p two k", two=2)[:, 1, :])):
            V.tensor_scalar(statw[:, 0:NMT], od, parc[:, 0:1], None, ALU.mult)
            vf()
            V.scalar_tensor_tensor(dst[:, 0:NMT], ev, parc[:, 1:2],
                                   statw[:, 0:NMT], ALU.mult, ALU.add)
            vf()
        s_v.inc(V.engine_nop())
        S.wait_ge(s_v.h, s_v.n)
        yield
        s_done_zn = [0, 0]
        t_done_zn = [0, 0]
        V.wait_ge(s_t.h, xg["znT_rd_t"])
        znT_v = znT.rearrange("p (c t) -> p c t", c=DBLK)
        for k in range(NT):
            p = k % 2
            S.wait_ge(s_t.h, t_done_zn[p])
            s_s.inc(S.activation(znt2[p][:], xts[:, k * D:(k + 1) * D], AF.Identity,
                                 scale=sc1[:, k:k + 1], bias=sc2[:, k:k + 1]))
            s_done_zn[p] = s_s.n
            TE.wait_ge(s_s.h, s_done_zn[p])
            TE.wait_ge(s_v.h, ptrw_g[p])
            for c in range(DBLK):
                mm = TE.transpose(ptrw[p][:, c * P:(c + 1) * P],
                                  znt2[p][:, c * P:(c + 1) * P], ident[:])
                if c == DBLK - 1:
                    s_t.inc(mm)
            t_done_zn[p] = s_t.n
            V.wait_ge(s_t.h, t_done_zn[p])
            s_v.inc(V.tensor_copy(znT_v[:, :, k * P:(k + 1) * P],
                                  ptrw[p][:, 0:DBLK * P].rearrange("p (c q) -> p c q", c=DBLK)))
            ptrw_g[p] = s_v.n
            yield
        xg["a_scal"] = s_s.n
        xg["zn_s"] = list(s_done_zn)
        xg["zn_v"] = s_v.n

    def gen_B(sidx):
        t0 = sidx * TCH
        kt0 = t0 // P
        t_done_dt = [0, 0]
        s_done_dt = [0, 0]
        NTC = TCH // 512
        fscr2 = bscr2
        TE.wait_ge(s_v.h, xg["zn_v"])
        for c in range(DBLK):
            for ti, tc in enumerate(range(0, TCH, 512)):
                w = min(512, TCH - tc)
                g = (c * NTC + ti) % 2
                pmx = pm0 if g == 0 else pm1
                pm_wait(g)
                for kk in range(DBLK):
                    mm = TE.matmul(pmx[:, 0:w],
                                   wdtg[:, kk * D + c * P:kk * D + (c + 1) * P],
                                   znT[:, kk * TCH + tc:kk * TCH + tc + w],
                                   start=(kk == 0), stop=(kk == DBLK - 1))
                    if kk == DBLK - 1:
                        s_t.inc(mm)
                t_done_dt[g] = s_t.n
                S.wait_ge(s_t.h, t_done_dt[g])
                S.activation(fscr2[g][:, 0:w], pmx[:, 0:w], AF.Exp,
                             bias=bdtc[:, c:c + 1], scale=consts[:, 7:8])
                sf()
                s_s.inc(S.activation(dtT[:, c * TCH + tc:c * TCH + tc + w],
                                     fscr2[g][:, 0:w], AF.Ln, bias=1.0))
                s_done_dt[g] = s_s.n
                pm_g[g]["s"] = s_s.n
                yield
        s_done_bc = 0
        for wi, wsrc in enumerate((wbg, wcg)):
            for ti, tc in enumerate(range(0, TCH, 512)):
                w = min(512, TCH - tc)
                pm_wait(2)
                for kk in range(DBLK):
                    mm = TE.matmul(pm2[0:N, 0:w], wsrc[:, kk * N:(kk + 1) * N],
                                   znT[:, kk * TCH + tc:kk * TCH + tc + w],
                                   start=(kk == 0), stop=(kk == DBLK - 1))
                    if kk == DBLK - 1:
                        s_t.inc(mm)
                S.wait_ge(s_t.h, s_t.n)
                S.wait_ge(s_t.h, xg["rep_t"])
                s_s.inc(S.activation(bcT_all[0:N, wi * TCH + tc:wi * TCH + tc + w],
                                     pm2[0:N, 0:w], AF.Identity,
                                     bias=bbc[0:N, wi:wi + 1], scale=consts[0:N, 7:8]))
                s_done_bc = s_s.n
                pm_g[2]["s"] = s_s.n
                yield
        xg["bct_s"] = s_done_bc
        V.wait_ge(s_s.h, s_done_dt[0])
        V.wait_ge(s_s.h, s_done_dt[1])
        for c in range(DBLK):
            V.tensor_scalar(wpT[:, c * TCH:(c + 1) * TCH], znT[:, c * TCH:(c + 1) * TCH],
                            gam[:, c:c + 1], bet[:, c:c + 1], ALU.mult, ALU.add)
            vf()
            s_v.inc(V.tensor_tensor(wpT[:, c * TCH:(c + 1) * TCH],
                                    wpT[:, c * TCH:(c + 1) * TCH],
                                    dtT[:, c * TCH:(c + 1) * TCH], ALU.mult))
            yield
        xg["znT_rd_t"] = s_t.n
        xg["bc_st"] = s_st.n

    def gen_C(sidx):
        t0 = sidx * TCH
        kt0 = t0 // P
        v_done_rep = [0] * 5
        v_done_ab = [0] * 5
        s_done_ab = [0] * 5
        v_done_ht = [0, 0]
        g_done_ht = [0, 0]
        s_done_rep = [0] * 5
        V.wait_ge(s_t.h, xg["acc_rd_t"])
        V.wait_ge(s_v.h, s_v.n)
        for n in range(N + 1):
            p = n % 5
            if n < N:
                TE.wait_ge(s_s.h, xg["pmq_s"])
                TE.wait_ge(s_s.h, xg["bct_s"])
                s_t.inc(TE.matmul(pmq[:, 0:TCH], sel_all[0:N, n * P:(n + 1) * P],
                                  bcT_all[0:N, 0:TCH], start=True, stop=True))
                S.wait_ge(s_t.h, s_t.n)
                S.wait_ge(s_v.h, v_done_rep[p])
                s_s.inc(S.activation(brep2[p][:, 0:TCH], pmq[:, 0:TCH], AF.Identity,
                                     scale=consts[:, 7:8], bias=consts[:, 9:10]))
                xg["pmq_s"] = s_s.n
                TE.wait_ge(s_s.h, xg["pmq_s"])
                s_t.inc(TE.matmul(pmq[:, 0:TCH], sel_all[0:N, n * P:(n + 1) * P],
                                  bcT_all[0:N, TCH:2 * TCH], start=True, stop=True))
                xg["rep_t"] = s_t.n
                S.wait_ge(s_t.h, s_t.n)
                s_s.inc(S.activation(crep2[p][:, 0:TCH], pmq[:, 0:TCH], AF.Identity,
                                     scale=consts[:, 7:8], bias=consts[:, 9:10]))
                xg["pmq_s"] = s_s.n
                s_done_rep[p] = s_s.n
            if n == 0:
                continue
            nn = n - 1
            pp = nn % 5
            for c0 in range(0, DBLK, 2):
                cpair = (c0, c0 + 1)
                for c in cpair:
                    ac = c % 5
                    S.wait_ge(s_v.h, v_done_ab[ac])
                    s_s.inc(S.activation(abar2[ac][:, 0:TCH],
                                         dtT[:, c * TCH:(c + 1) * TCH], AF.Exp,
                                         scale=apar[:, c * N + nn:c * N + nn + 1],
                                         bias=consts[:, 9:10]))
                    s_done_ab[ac] = s_s.n
                V.wait_ge(s_s.h, s_done_rep[pp])
                V.wait_ge(s_s.h, s_done_ab[cpair[1] % 5])
                V.wait_ge(s_ge.h, g_done_ht[0])
                V.wait_ge(s_ge.h, g_done_ht[1])
                V.wait_ge(s_v.h, v_done_ht[0])
                V.wait_ge(s_v.h, v_done_ht[1])
                bx_n = [0, 0]
                sc_n = [0, 0]
                ch_n = [0, 0]
                for c in cpair:
                    pc = c % 2
                    s_v.inc(V.tensor_tensor(bxt2[pc][:, 0:TCH],
                                            wpT[:, c * TCH:(c + 1) * TCH],
                                            brep2[pp][:, 0:TCH], ALU.mult))
                    bx_n[pc] = s_v.n
                for c in cpair:
                    pc = c % 2
                    ac = c % 5
                    V.wait_ge(s_v.h, bx_n[pc])
                    s_v.inc(V.tensor_tensor_scan(ht2[pc][:, 0:TCH],
                                                 abar2[ac][:, 0:TCH],
                                                 bxt2[pc][:, 0:TCH],
                                                 state[:, c * N + nn:c * N + nn + 1],
                                                 ALU.mult, ALU.add))
                    sc_n[pc] = s_v.n
                    v_done_ab[ac] = s_v.n
                for c in cpair:
                    pc = c % 2
                    V.wait_ge(s_v.h, sc_n[pc])
                    s_v.inc(V.tensor_tensor(cht2[pc][:, 0:TCH], ht2[pc][:, 0:TCH],
                                            crep2[pp][:, 0:TCH], ALU.mult))
                    ch_n[pc] = s_v.n
                for c in cpair:
                    pc = c % 2
                    V.wait_ge(s_v.h, ch_n[pc])
                    if nn == 0:
                        s_v.inc(V.tensor_tensor(accs[:, c * TCH:(c + 1) * TCH],
                                                ht2[pc][:, 0:TCH], crep2[pp][:, 0:TCH],
                                                ALU.mult))
                    else:
                        s_v.inc(V.tensor_tensor(accs[:, c * TCH:(c + 1) * TCH],
                                                accs[:, c * TCH:(c + 1) * TCH],
                                                cht2[pc][:, 0:TCH], ALU.add))
                    v_done_ht[pc] = s_v.n
                    G.wait_ge(s_v.h, v_done_ht[pc])
                    s_ge.inc(G.tensor_copy(state[:, c * N + nn:c * N + nn + 1],
                                           ht2[pc][:, TCH - 1:TCH]))
                    g_done_ht[pc] = s_ge.n
                yield
            v_done_rep[pp] = s_v.n
        V.wait_ge(s_ge.h, s_ge.n)
        V.wait_ge(s_s.h, xg["yam_rd_s"])
        for c in range(DBLK):
            p = c % 2
            TE.wait_ge(s_v.h, max(ptrw_g[p], s_v.n if c < 2 else 0))
            for k in range(NT):
                mm = TE.transpose(ptrw[p][:, k * P:(k + 1) * P],
                                  accs[:, c * TCH + k * P:c * TCH + (k + 1) * P],
                                  ident[:])
                if k == NT - 1:
                    s_t.inc(mm)
            V.wait_ge(s_t.h, s_t.n)
            pv = ptrw[p][:, 0:NT * P].rearrange("p (k two q) -> p k two q", two=2, q=P)
            vf(V.tensor_scalar(ytd[:, 0:MINE].rearrange("p (k q) -> p k q", q=P),
                               pv[:, :, 1, :], parc[:, 0:1], None, ALU.mult))
            yv = yam.rearrange("p (k d) -> p k d", k=NMT)
            s_v.inc(V.scalar_tensor_tensor(yv[:, :, c * P:(c + 1) * P],
                                           pv[:, :, 0, :], parc[:, 1:2],
                                           ytd[:, 0:MINE].rearrange("p (k q) -> p k q", q=P),
                                           ALU.mult, ALU.add))
            ptrw_g[p] = s_v.n
            yield
        xg["acc_rd_t"] = s_t.n

    def gen_C2(sidx):
        t0 = sidx * TCH
        kt0 = t0 // P
        V.wait_ge(s_s.h, xg["ln4_s"])
        SY.wait_ge(s_v.h, xg["um_rd_v"])
        for k in range(NMT):
            s_ld.inc(SY.dma_start(um[:, k * D:(k + 1) * D],
                                  xm_h[sidx * MINE + k * P:sidx * MINE + (k + 1) * P, :]), 16)
        yield
        V.wait_ge(s_t.h, s_t.n)
        mnM = mnM2[sidx % 2]
        r1M = r1M2[sidx % 2]
        vf(V.tensor_tensor(statw2[:, 0:NMT], mnM[:, 0:NMT], r1M[:, 0:NMT], ALU.mult))
        s_v.inc(V.tensor_scalar(statw2[:, 0:NMT], statw2[:, 0:NMT], -1.0, None, ALU.mult))
        S.wait_ge(s_v.h, s_v.n)
        S.wait_ge(s_ld.h, s_ld.n)
        for k in range(NMT):
            mm = S.activation(z1m[:, k * D:(k + 1) * D], um[:, k * D:(k + 1) * D],
                              AF.Identity, scale=r1M[:, k:k + 1], bias=statw2[:, k:k + 1])
            if k == NMT - 1:
                s_s.inc(mm)
        c2_z1 = s_s.n
        yield
        V.wait_ge(s_s.h, c2_z1)
        for k in range(NMT):
            V.tensor_tensor(z1m[:, k * D:(k + 1) * D], z1m[:, k * D:(k + 1) * D],
                            drep[:], ALU.mult)
        vf()
        for k in range(NMT):
            V.tensor_tensor(yam[:, k * D:(k + 1) * D], yam[:, k * D:(k + 1) * D],
                            z1m[:, k * D:(k + 1) * D], ALU.add)
        vf()
        c2_st = 0
        for k in range(NMT):
            s_v.inc(ln_stats(ln3s[:, 2 * k:2 * k + 2], yam[:, k * D:(k + 1) * D]))
            c2_st = s_v.n
            yield
        S.wait_ge(s_v.h, c2_st)
        l3v = ln3s.ap().rearrange("p (k two) -> p two k", two=2)
        s_s.inc(rsqrt_cols(l3v[:, 1, :], sxv[:, 0:NMT], statw2[:, 0:NMT]))
        V.wait_ge(s_s.h, s_s.n)
        vf(V.tensor_tensor(statw2[:, 0:NMT], l3v[:, 0, :], sxv[:, 0:NMT], ALU.mult))
        s_v.inc(V.tensor_scalar(statw2[:, 0:NMT], statw2[:, 0:NMT], -1.0, None, ALU.mult))
        S.wait_ge(s_v.h, s_v.n)
        for k in range(NMT):
            mm = S.activation(fTM[:, k * D:(k + 1) * D], yam[:, k * D:(k + 1) * D],
                              AF.Identity, scale=sxv[:, k:k + 1], bias=statw2[:, k:k + 1])
            if k == NMT - 1:
                s_s.inc(mm)
        xg["yam_rd_s"] = s_s.n
        c2_f = s_s.n
        yield
        V.wait_ge(s_s.h, c2_f)
        for k in range(NMT):
            mm = V.tensor_tensor(um[:, k * D:(k + 1) * D], um[:, k * D:(k + 1) * D],
                                 fTM[:, k * D:(k + 1) * D], ALU.add)
            if k == NMT - 1:
                s_v.inc(mm)
        S.wait_ge(s_v.h, s_v.n)
        SY.wait_ge(s_v.h, s_v.n)

    def gen_D_head(sidx):
        t0 = sidx * TCH
        kt0 = t0 // P
        zq = zqb
        V.wait_ge(s_s.h, xg["ln4_s"])
        dh_st = 0
        for k in range(NMT):
            s_v.inc(ln_stats(ln2s[:, 2 * k:2 * k + 2], um[:, k * D:(k + 1) * D]))
            dh_st = s_v.n
            yield
        l2v = ln2s.ap().rearrange("p (k two) -> p two k", two=2)
        S.wait_ge(s_v.h, dh_st)
        s_s.inc(rsqrt_cols(l2v[:, 1, :], sxv[:, 0:NMT], statw2[:, 0:NMT]))
        V.wait_ge(s_s.h, s_s.n)
        vf(V.tensor_tensor(statw2[:, 0:NMT], l2v[:, 0, :], sxv[:, 0:NMT], ALU.mult))
        s_v.inc(V.tensor_scalar(statw2[:, 0:NMT], statw2[:, 0:NMT], -1.0, None, ALU.mult))
        S.wait_ge(s_v.h, s_v.n)
        for k in range(NMT):
            mm = S.activation(fTM[:, k * D:(k + 1) * D], um[:, k * D:(k + 1) * D],
                              AF.Identity, scale=sxv[:, k:k + 1], bias=statw2[:, k:k + 1])
            if k == NMT - 1:
                s_s.inc(mm)
        dh_z2 = s_s.n
        yield
        V.wait_ge(s_s.h, dh_z2)
        for k in range(NMT):
            V.tensor_reduce(statw2[:, k:k + 1], fTM[:, k * D:(k + 1) * D], AX.X,
                            ALU.max, apply_absolute_value=True)
        vf()
        yield
        V.tensor_scalar(statw2[:, 8:8 + NMT], statw2[:, 0:NMT], 1e-5, 1.0 / 127.0,
                        ALU.max, ALU.mult)
        vf()
        V.reciprocal(statw2[:, 0:NMT], statw2[:, 8:8 + NMT])
        vf()
        for k in range(NMT):
            V.tensor_scalar(fTM[:, k * D:(k + 1) * D], fTM[:, k * D:(k + 1) * D],
                            statw2[:, k:k + 1], MAGIC, ALU.mult, ALU.add)
        vf()
        for k in range(NMT):
            mm = V.tensor_scalar(zq[:, k * D:(k + 1) * D], fTM[:, k * D:(k + 1) * D],
                                 -MAGIC, statw2[:, 8 + k:9 + k], ALU.add, ALU.mult)
            if k == NMT - 1:
                s_v.inc(mm)
        TE.wait_ge(s_v.h, s_v.n)
        z2qT_v = z2qT.rearrange("p (c t) -> p c t", c=DBLK)
        for k in range(NMT):
            p = k % 2
            TE.wait_ge(s_v.h, ptrw_g[p])
            for c in range(DBLK):
                mm = TE.transpose(ptrw[p][:, c * P:(c + 1) * P],
                                  zq[:, k * D + c * P:k * D + (c + 1) * P], ident[:])
                if c == DBLK - 1:
                    s_t.inc(mm)
            V.wait_ge(s_t.h, s_t.n)
            s_v.inc(V.tensor_copy(z2qT_v[:, :, k * P:(k + 1) * P],
                                  ptrw[p][:, 0:DBLK * P].rearrange("p (c q) -> p c q", c=DBLK)))
            ptrw_g[p] = s_v.n
            yield
        xg["z2q_v"] = s_v.n
        TE.wait_ge(s_v.h, xg["z2q_v"])

    def gen_D(sidx):
        zq = zqb
        t_done_fw = [0, 0]
        s_done_gu = [0, 0, 0, 0]
        v_done_gp = gp_v
        ld_fw = [0, 0]

        def gu_load(ic):
            p = ic % 2
            SY.wait_ge(s_t.h, t_done_fw[p])
            s_fw[p].inc(SY.dma_start(fw2[p][:, 0:DBLK * P].rearrange("p (kk o) -> p kk o", kk=DBLK),
                                     wgq_d[ic * D:(ic + 1) * D, :].rearrange(
                                         "(p kk) o -> p kk o", p=P)), 16)
            s_fw[p].inc(SY.dma_start(fu2[p][:, 0:DBLK * P].rearrange("p (kk o) -> p kk o", kk=DBLK),
                                     wuq_d[ic * D:(ic + 1) * D, :].rearrange(
                                         "(p kk) o -> p kk o", p=P)), 16)
            ld_fw[p] = s_fw[p].n

        gu_load(0)
        for ic in range(IBLK):
            p = ic % 2
            q = ic % 4
            if ic + 1 < IBLK:
                gu_load(ic + 1)
            TE.wait_ge(s_fw[p].h, ld_fw[p])
            pmg = pm0 if p == 0 else pm2
            pmu = pm1 if p == 0 else pm3
            for tc in range(0, MINE, 512):
                w = min(512, MINE - tc)
                pm_wait(0 if p == 0 else 2)
                pm_wait(1 if p == 0 else 3)
                for kk in range(DBLK):
                    mm = TE.matmul(pmg[:, tc % 512:tc % 512 + w], fw2[p][:, kk * P:(kk + 1) * P],
                                   z2qT[:, kk * MINE + tc:kk * MINE + tc + w],
                                   start=(kk == 0), stop=(kk == DBLK - 1))
                    if kk == DBLK - 1:
                        s_t.inc(mm)
                for kk in range(DBLK):
                    mm = TE.matmul(pmu[:, tc % 512:tc % 512 + w], fu2[p][:, kk * P:(kk + 1) * P],
                                   z2qT[:, kk * MINE + tc:kk * MINE + tc + w],
                                   start=(kk == 0), stop=(kk == DBLK - 1))
                    if kk == DBLK - 1:
                        s_t.inc(mm)
                t_done_fw[p] = s_t.n
                S.wait_ge(s_t.h, t_done_fw[p])
                S.wait_ge(s_v.h, v_done_gp[q])
                s_s.inc(S.activation(gt2[q][:, tc:tc + w], pmg[:, tc % 512:tc % 512 + w],
                                     AF.Identity, scale=consts[:, 12:13],
                                     bias=consts[:, 9:10]))
                s_s.inc(S.activation(pt2[q][:, tc:tc + w], pmu[:, tc % 512:tc % 512 + w],
                                     AF.Identity, scale=consts[:, 4:5], bias=consts[:, 9:10]))
                s_done_gu[q] = s_s.n
                pm_g[0 if p == 0 else 2]["s"] = s_s.n
                pm_g[1 if p == 0 else 3]["s"] = s_s.n
            yield
            if q == 3:
                S.wait_ge(s_v.h, q4v[0])
                S.wait_ge(s_s.h, s_done_gu[q])
                s_s.inc(S.activation(sqrep4[:], gt4[:], AF.Tanh))
                V.wait_ge(s_s.h, s_s.n)
                vf(V.scalar_tensor_tensor(gt4[:], sqrep4[:], 1.0, gt4[:],
                                          ALU.add, ALU.mult))
                q4v[0] = s_v.n
                vf(V.tensor_tensor(qs[:, (ic - 3) * MINE:(ic + 1) * MINE], gt4[:],
                                   pt4[:], ALU.mult))
                for j in range(4):
                    v_done_gp[j] = s_v.n
                if ic == 3:
                    V.tensor_copy(mrow4[:], qs[:, 0:4 * MINE])
                    s_v.inc(V.tensor_copy(mrow24[:], qs[:, 0:4 * MINE]))
                else:
                    V.tensor_tensor(mrow4[:], mrow4[:],
                                    qs[:, (ic - 3) * MINE:(ic + 1) * MINE], ALU.max)
                    s_v.inc(V.tensor_tensor(mrow24[:], mrow24[:],
                                            qs[:, (ic - 3) * MINE:(ic + 1) * MINE],
                                            ALU.min))
                vf()
            yield
        vf(V.scalar_tensor_tensor(mrow4[:], mrow24[:], -1.0, mrow4[:],
                                  ALU.mult, ALU.max))
        vf(V.tensor_tensor(mrow4[:, 0:2 * MINE], mrow4[:, 0:2 * MINE],
                           mrow4[:, 2 * MINE:4 * MINE], ALU.max))
        vf(V.tensor_tensor(mrow[:, 0:MINE], mrow4[:, 0:MINE],
                           mrow4[:, MINE:2 * MINE], ALU.max))
        TE.wait_ge(s_v.h, s_v.n)
        for k in range(NMT):
            s_t.inc(TE.transpose(ptr[:, 0:P], mrow[:, k * P:(k + 1) * P], ident[:]))
            V.wait_ge(s_t.h, s_t.n)
            V.tensor_reduce(sqv[:, k:k + 1], ptr[:, 0:P], AX.X, ALU.max,
                            apply_absolute_value=True)
            vf()
            s_v.inc(V.tensor_copy(statw2[:, 15:16], sqv[:, k:k + 1]))
            TE.wait_ge(s_v.h, s_v.n)
            yield
        V.tensor_scalar(sqvr[:, 0:NMT], sqv[:, 0:NMT], 1e-5, 1.0 / 127.0,
                        ALU.max, ALU.mult)
        vf()
        V.reciprocal(sqv[:, 0:NMT], sqvr[:, 0:NMT])
        vf()
        V.wait_ge(s_t.h, xg["sqT_t"])
        V.tensor_copy(sqstg[:, 0:NMT], sqv[:, 0:NMT])
        vf()
        s_v.inc(V.tensor_copy(sqstg[:, NMT:2 * NMT], sqvr[:, 0:NMT]))
        TE.wait_ge(s_v.h, s_v.n)
        TE.wait_ge(s_v.h, ptrw_g[0])
        for j in range(2 * NMT):
            mm = TE.transpose(ptrw[0][0:1, j * P:(j + 1) * P], sqstg[:, j:j + 1], ident[:])
            if j == 2 * NMT - 1:
                s_t.inc(mm)
        xg["sqT_t"] = s_t.n
        S.wait_ge(s_t.h, s_t.n)
        s_s.inc(S.activation(sqTrow[0:1, 0:2 * NMT * P], ptrw[0][0:1, 0:2 * NMT * P],
                             AF.Identity, scale=consts[0:1, 7:8], bias=consts[0:1, 9:10]))
        V.wait_ge(s_s.h, s_s.n)
        s_v.inc(V.engine_nop())
        ptrw_g[0] = s_v.n
        TE.wait_ge(s_s.h, s_s.n)
        TE.wait_ge(s_s.h, xg["pmq_s"])
        for k in range(NMT):
            s_t.inc(TE.matmul(pmq[:, k * P:(k + 1) * P], onesrow[0:1, 0:P],
                              sqTrow[0:1, k * P:(k + 1) * P], start=True, stop=True))
            s_t.inc(TE.matmul(pmq[:, MINE + k * P:MINE + (k + 1) * P], onesrow[0:1, 0:P],
                              sqTrow[0:1, (NMT + k) * P:(NMT + k + 1) * P],
                              start=True, stop=True))
        S.wait_ge(s_t.h, s_t.n)
        S.wait_ge(s_v.h, xg["qs_v"])
        S.wait_ge(s_v.h, q4v[0])
        RB = 4 if 4 * MINE <= NMT * D else 2
        for rep in range(RB):
            s_s.inc(S.activation(sqrep4[:, rep * MINE:(rep + 1) * MINE],
                                 pmq[:, 0:MINE], AF.Identity,
                                 scale=consts[:, 7:8], bias=consts[:, 9:10]))
            s_s.inc(S.activation(sqrrep4[:, rep * MINE:(rep + 1) * MINE],
                                 pmq[:, MINE:2 * MINE], AF.Identity,
                                 scale=consts[:, 7:8], bias=consts[:, 9:10]))
        xg["pmq_s"] = s_s.n
        yield
        V.wait_ge(s_s.h, xg["pmq_s"])
        RW = RB * MINE
        for jj in range(IBLK // RB):
            sl = qs[:, jj * RW:(jj + 1) * RW]
            V.tensor_tensor(fTM[:, 0:RW], sl, sqrep4[:, 0:RW], ALU.mult)
            vf()
            V.tensor_scalar(fTM[:, 0:RW], fTM[:, 0:RW], MAGIC, -MAGIC,
                            ALU.add, ALU.add)
            vf()
            V.tensor_copy(qsc4[:, 0:RW], fTM[:, 0:RW])
            vf()
            s_v.inc(V.tensor_tensor(sl, qsc4[:, 0:RW], sqrrep4[:, 0:RW], ALU.mult))
            yield
        xg["qs_v"] = s_v.n
        TE.wait_ge(s_v.h, xg["qs_v"])
        t_done_wd = [0, 0]
        s_done_dn = [0, 0]
        v_done_fc = [0, 0]
        ld_wd = [0, 0]

        def dn_load(oc):
            p = oc % 2
            SY.wait_ge(s_t.h, t_done_wd[p])
            s_dw[p].inc(SY.dma_start(wdsl2[p][:, 0:IBLK * P].rearrange("p (kk o) -> p kk o", kk=IBLK),
                                     wdq_d[oc * INNER:(oc + 1) * INNER, :].rearrange(
                                         "(p kk) o -> p kk o", p=P)), 16)
            ld_wd[p] = s_dw[p].n

        dn_load(0)
        for oc in range(DBLK):
            p = oc % 2
            fq = oc % 4
            if oc + 1 < DBLK:
                dn_load(oc + 1)
            TE.wait_ge(s_dw[p].h, ld_wd[p])
            pmx = pm0 if p == 0 else pm1
            for tc in range(0, MINE, 512):
                w = min(512, MINE - tc)
                pm_wait(p)
                for kk in range(IBLK):
                    mm = TE.matmul(pmx[:, 0:w], wdsl2[p][:, kk * P:(kk + 1) * P],
                                   qs[:, kk * MINE + tc:kk * MINE + tc + w],
                                   start=(kk == 0), stop=(kk == IBLK - 1))
                    if kk == IBLK - 1:
                        s_t.inc(mm)
                t_done_wd[p] = s_t.n
                S.wait_ge(s_t.h, t_done_wd[p])
                S.wait_ge(s_v.h, gp_v[fq])
                S.wait_ge(s_t.h, fcm_t[fq])
                s_s.inc(S.activation(fcm2[fq][:, tc:tc + w], pmx[:, 0:w], AF.Identity,
                                     scale=consts[:, 5:6], bias=consts[:, 9:10]))
                s_done_dn[p] = s_s.n
                pm_g[p]["s"] = s_s.n
            yield
            TE.wait_ge(s_s.h, s_done_dn[p])
            TE.wait_ge(s_v.h, ptrw_g[p])
            ptw = ptrw[p]
            for k in range(NMT):
                mm = TE.transpose(ptw[:, k * P:(k + 1) * P], fcm2[fq][:, k * P:(k + 1) * P],
                                  ident[:])
                if k == NMT - 1:
                    s_t.inc(mm)
            t_done_wd[p] = s_t.n
            fcm_t[fq] = s_t.n
            V.wait_ge(s_t.h, t_done_wd[p])
            fTM_v = fTM.rearrange("p (k d) -> p k d", k=NMT)
            s_v.inc(V.tensor_copy(fTM_v[:, :, oc * P:(oc + 1) * P],
                                  ptw[:, 0:NMT * P].rearrange("p (k q) -> p k q", k=NMT)))
            ptrw_g[p] = s_v.n
            yield
        vf()
        for k in range(NMT):
            s_v.inc(ln_stats(ln2s[:, 2 * k:2 * k + 2], fTM[:, k * D:(k + 1) * D]))
        S.wait_ge(s_v.h, s_v.n)
        l4v = ln2s.ap().rearrange("p (k two) -> p two k", two=2)
        s_s.inc(rsqrt_cols(l4v[:, 1, :], sxv[:, 0:NMT], statw2[:, 0:NMT]))
        V.wait_ge(s_s.h, s_s.n)
        vf(V.tensor_tensor(statw2[:, 0:NMT], l4v[:, 0, :], sxv[:, 0:NMT], ALU.mult))
        s_v.inc(V.tensor_scalar(statw2[:, 0:NMT], statw2[:, 0:NMT], -1.0, None, ALU.mult))
        S.wait_ge(s_v.h, s_v.n)
        for k in range(NMT):
            mm = S.activation(zq[:, k * D:(k + 1) * D], fTM[:, k * D:(k + 1) * D],
                              AF.Identity, scale=sxv[:, k:k + 1], bias=statw2[:, k:k + 1])
            if k == NMT - 1:
                s_s.inc(mm)
        xg["ln4_s"] = s_s.n
        V.wait_ge(s_s.h, s_s.n)
        st_done_o = list(xg["o_st"])
        osc2 = [oscr, fscr]
        for k in range(NMT):
            p = k % 2
            V.wait_ge(s_st.h, st_done_o[p])
            s_v.inc(V.tensor_tensor(osc2[p][:, 0:D], um[:, k * D:(k + 1) * D],
                                    zq[:, k * D:(k + 1) * D], ALU.add))
            SY.wait_ge(s_v.h, s_v.n)
            s_st.inc(SY.dma_start(out_h[sidx * MINE + k * P:sidx * MINE + (k + 1) * P, :],
                                  osc2[p][:, 0:D]), 16)
            st_done_o[p] = s_st.n
            yield
        xg["o_st"] = list(st_done_o)
        xg["um_rd_v"] = s_v.n


    def gen_MIX(sidx):
        for _ in gen_A(sidx):
            yield
        for _ in gen_B(sidx):
            yield
        for _ in gen_C(sidx):
            yield

    def gen_DFull(sidx):
        for _ in gen_C2(sidx):
            yield
        for _ in gen_D_head(sidx):
            yield
        for _ in gen_D(sidx):
            yield

    def drive(mix, dside, rm=3, rd=4):
        di = 0
        while True:
            for _ in range(rm):
                try:
                    next(mix)
                except StopIteration:
                    return dside[di:]
            for _ in range(rd):
                if di < len(dside):
                    try:
                        next(dside[di])
                    except StopIteration:
                        di += 1

    dside = []
    for s in range(NSUP):
        if s >= 1:
            dside.append(gen_DFull(s - 1))
        dside = drive(gen_MIX(s), dside)
    dside.append(gen_DFull(NSUP - 1))
    for g in dside:
        for _ in g:
            pass
    barrier()

    SY.wait_ge(s_st.h, s_st.n)
    for cm in reversed(ctxs):
        cm.__exit__(None, None, None)
    return nc


def _prep_inputs(inputs, cfg):
    D, N = cfg["D"], cfg["N"]
    DBLK = D // P
    f = np.float32

    def arr(k):
        return np.asarray(inputs[k], f)

    gam = np.ascontiguousarray(arr("L0_attn_gamma").reshape(DBLK, P).T)
    bet = np.ascontiguousarray(arr("L0_attn_beta").reshape(DBLK, P).T)
    a_pp = np.ascontiguousarray(
        arr("L0_attn_A_log").reshape(DBLK, P, N).transpose(1, 0, 2).reshape(P, DBLK * N))
    return {
        "wdtT": np.ascontiguousarray(arr("L0_attn_W_dt").T),
        "wbT": np.ascontiguousarray(arr("L0_attn_W_B").T),
        "wcT": np.ascontiguousarray(arr("L0_attn_W_C").T),
        "gam_pp": gam, "bet_pp": bet, "a_pp": a_pp,
        "bdt_row": np.ascontiguousarray(arr("L0_attn_b_dt").reshape(1, D)),
        "dparam_row": np.ascontiguousarray(arr("L0_attn_D_param").reshape(1, D)),
        "wgT": np.ascontiguousarray(
            arr("L0_ffn_w_gate").T.reshape(D // P, P, -1, P)
            .transpose(2, 1, 0, 3).reshape(-1, P)),
        "wuT": np.ascontiguousarray(
            arr("L0_ffn_w_up").T.reshape(D // P, P, -1, P)
            .transpose(2, 1, 0, 3).reshape(-1, P)),
        "wdT": np.ascontiguousarray(
            arr("L0_ffn_w_down").T.reshape(-1, P, D // P, P)
            .transpose(2, 1, 0, 3).reshape(-1, P)),
    }


def kernel(**inputs):
    x = np.asarray(inputs["L0_attn_x"], np.float32)
    B, T, D = x.shape
    cfg = {"T": T, "D": D, "N": int(inputs["L0_attn_A_log"].shape[1]),
           "INNER": int(inputs["L0_ffn_w_gate"].shape[0]), "TCH": 512}
    nc = build_program(cfg)
    base = _prep_inputs(inputs, cfg)
    in_maps = []
    for c in range(8):
        b, j = c // 2, c % 2
        m = dict(base)
        m["x"] = np.ascontiguousarray(x[b])
        xb = x[b].reshape(T // P, P, D)
        m["xm"] = np.ascontiguousarray(xb[j::2].reshape(T // 2, D))
        m["parf"] = np.full((1, 1), j, np.float32)
        in_maps.append(m)
    res = run_bass_kernel_spmd(nc, in_maps, core_ids=list(range(8)))
    out = np.empty((B, T, D), np.float32)
    TCH = cfg["TCH"]
    MINE = TCH // 2
    NMT = MINE // P
    for c in range(8):
        b, j = c // 2, c % 2
        o = np.asarray(res.results[c]["out"])
        for s in range(T // TCH):
            for k in range(NMT):
                trow = s * TCH + (2 * k + j) * P
                out[b, trow:trow + P, :] = o[s * MINE + k * P:s * MINE + (k + 1) * P, :]
    return out


# version 42
# speedup vs baseline: 1.0336x; 1.0139x over previous
import numpy as np
import concourse.bass as bass
import concourse.mybir as mybir
from concourse.bass_utils import run_bass_kernel_spmd

F32 = mybir.dt.float32
BF16 = mybir.dt.bfloat16
I32 = mybir.dt.int32
AF = mybir.ActivationFunctionType
ALU = mybir.AluOpType
AX = mybir.AxisListType
ts = bass.ts

MAGIC = 12582912.0
LN_EPS = 1e-5
P = 128


def build_program(cfg):
    T, D, N, INNER = cfg["T"], cfg["D"], cfg["N"], cfg["INNER"]
    TCH = cfg["TCH"]
    DBG = cfg.get("debug", False)
    DBLK = D // P
    NSUP = T // TCH
    NT = TCH // P
    MINE = TCH // 2
    NMT = MINE // P
    IBLK = INNER // P
    WQC = min(INNER, 1024)

    nc = bass.Bass("TRN2", target_bir_lowering=False)

    def din(name, shape, dtype=F32):
        return nc.dram_tensor(name, shape, dtype, kind="ExternalInput").ap()

    x_h = din("x", [T, D])
    xm_h = din("xm", [T // 2, D])
    parf_h = din("parf", [1, 1])
    wdtT_h = din("wdtT", [D, D])
    wbT_h = din("wbT", [D, N])
    wcT_h = din("wcT", [D, N])
    gam_h = din("gam_pp", [P, DBLK])
    bet_h = din("bet_pp", [P, DBLK])
    bdt_h = din("bdt_row", [1, D])
    apar_h = din("a_pp", [P, DBLK * N])
    dpar_h = din("dparam_row", [1, D])
    wgT_h = din("wgT", [INNER // P * D, P])
    wuT_h = din("wuT", [INNER // P * D, P])
    wdT_h = din("wdT", [D // P * INNER, P])

    out_h = nc.dram_tensor("out", [NSUP * MINE, D], F32, kind="ExternalOutput").ap()

    def dsc(name, shape, dtype):
        return nc.dram_tensor(name, shape, dtype, kind="Internal").ap()

    bT_d = dsc("bT_d", [N, TCH], BF16)
    cT_d = dsc("cT_d", [N, TCH], BF16)
    wgq_d = dsc("wgq_d", [INNER // P * D, P], BF16)
    wuq_d = dsc("wuq_d", [INNER // P * D, P], BF16)
    wdq_d = dsc("wdq_d", [D // P * INNER, P], BF16)
    brow_d = dsc("brow_d", [1, D + 2 * N], F32)
    sq_d = dsc("sq_d", [2, MINE], BF16)
    wqs_d = dsc("wqs_d", [3, 2], F32)

    dbg = {}
    if DBG:
        for nm, shape in [("d_znT", [D, TCH]), ("d_dtT", [D, TCH]),
                          ("d_bT", [N, TCH]), ("d_cT", [N, TCH]),
                          ("d_wT", [D, TCH]), ("d_y", [TCH, D]),
                          ("d_u", [MINE, D]), ("d_z2q", [D, MINE]),
                          ("d_q", [INNER, MINE]), ("d_wgq", [D, INNER]),
                          ("d_f", [MINE, D]),
                          ("d_h", [P, TCH])]:
            dbg[nm] = nc.dram_tensor(nm, shape, F32, kind="ExternalOutput").ap()

    ctxs = []

    def sb(name, shape, dtype):
        cm = nc.sbuf_tensor(name, shape, dtype)
        t = cm.__enter__()
        ctxs.append(cm)
        return t

    def psb(name, shape, dtype=F32):
        cm = nc.psum_tensor(name, shape, dtype)
        t = cm.__enter__()
        ctxs.append(cm)
        return t

    ident = sb("ident", [P, P], BF16)
    onescol = sb("onescol", [P, 1], BF16)
    onesrow = sb("onesrow", [1, P], BF16)
    sqstg = sb("sqstg", [P, 8], BF16)
    sel_all = sb("sel_all", [N, N * P], BF16)
    bcT_all = sb("bcT_all", [N, 2 * TCH], BF16)
    sqTrow = sb("sqTrow", [1, 8 * P], BF16)
    gam = sb("gam", [P, DBLK], F32)
    bet = sb("bet", [P, DBLK], F32)
    apar = sb("apar", [P, DBLK * N], F32)
    drep = sb("drep", [P, D], BF16)
    bdtc = sb("bdtc", [P, DBLK], F32)
    bbc = sb("bbc", [N, 2], F32)
    parc = sb("parc", [P, 2], F32)
    consts = sb("consts", [P, 14], F32)
    wdtg = sb("wdtg", [P, DBLK * D], BF16)
    wbg = sb("wbg", [P, DBLK * N], BF16)
    wcg = sb("wcg", [P, DBLK * N], BF16)
    stats = sb("stats", [P, NT * 2], F32)
    sc1 = sb("sc1", [P, NT], F32)
    r1v = sb("r1v", [P, NT], F32)
    mnM2 = [sb("mnM0", [P, NMT], F32), sb("mnM1", [P, NMT], F32)]
    r1M2 = [sb("r1M0", [P, NMT], F32), sb("r1M1", [P, NMT], F32)]
    sc2 = sb("sc2", [P, NT], F32)
    statw = sb("statw", [P, 16], F32)
    statw2 = sb("statw2", [P, 16], F32)
    ln2s = sb("ln2s", [P, 2 * NMT], F32)
    ln3s = sb("ln3s", [P, 2 * NMT], F32)
    ln4s = sb("ln4s", [P, 2], F32)
    sxv = sb("sxv", [P, NMT], F32)
    sqv = sb("sqv", [P, NMT], F32)
    sqvr = sb("sqvr", [P, NMT], F32)
    accs = sb("accs", [P, DBLK * TCH], BF16)
    state = sb("state", [P, DBLK * N], F32)
    xts = sb("xts", [P, NT * D], F32)
    xt2 = [xts[:, 0:D], xts[:, D:2 * D]]
    xt0 = xt2[0]
    znt2 = [sb("znt", [P, D], BF16), sb("zntb", [P, D], BF16)]
    znt = znt2[0]
    bns = sb("bns", [P, 16], F32)

    a_b1 = sb("a_b1", [P, 2 * DBLK * TCH + IBLK * MINE], BF16)
    dtT = a_b1[:, 0:DBLK * TCH]
    wpT = a_b1[:, DBLK * TCH:2 * DBLK * TCH]
    qs = a_b1[:, 2 * DBLK * TCH:2 * DBLK * TCH + IBLK * MINE]
    a_b2 = sb("a_b2", [P, DBLK * TCH + 2 * NMT * D + MINE + DBLK * MINE], BF16)
    znT = a_b2[:, 0:DBLK * TCH]
    _o2 = DBLK * TCH
    yam = a_b2[:, _o2:_o2 + NMT * D]
    z1m = a_b2[:, _o2 + NMT * D:_o2 + 2 * NMT * D]
    ytd = a_b2[:, _o2 + 2 * NMT * D:_o2 + 2 * NMT * D + MINE]
    z2qT = a_b2[:, _o2 + 2 * NMT * D + MINE:_o2 + 2 * NMT * D + MINE + DBLK * MINE]
    SCRW = max(4 * TCH, 2 * D + 2 * N)
    FCOLS = max(DBLK * TCH, SCRW, 2 * NMT * D + 3 * D + 1024, NMT * D + 6 * WQC)
    a_f = sb("a_f", [P, FCOLS], F32)
    scr = a_f[:, 0:SCRW]

    wsc = a_f[:, NMT * D:NMT * D + WQC]
    um = a_f[:, 0:NMT * D]
    fTM = a_f[:, NMT * D:2 * NMT * D]
    fscr = a_f[:, 2 * NMT * D:2 * NMT * D + D]
    zscr = a_f[:, 2 * NMT * D + D:2 * NMT * D + 2 * D]
    oscr = a_f[:, 2 * NMT * D + 2 * D:2 * NMT * D + 3 * D]
    _bs = 2 * NMT * D + 3 * D
    bscr2 = [a_f[:, _bs:_bs + 512], a_f[:, _bs + 512:_bs + 1024]]
    a_b3 = sb("a_b3", [P, 21 * TCH + 4 * DBLK * P + 33 * MINE + 2 * IBLK * P + NMT * D], BF16)
    brep2 = [a_b3[:, j * TCH:(j + 1) * TCH] for j in range(5)]
    crep2 = [a_b3[:, (5 + j) * TCH:(6 + j) * TCH] for j in range(5)]
    abar2 = [a_b3[:, (10 + j) * TCH:(11 + j) * TCH] for j in range(5)]
    bxt2 = [a_b3[:, 15 * TCH:16 * TCH], a_b3[:, 16 * TCH:17 * TCH]]
    ht2 = [a_b3[:, 17 * TCH:18 * TCH], a_b3[:, 18 * TCH:19 * TCH]]
    cht2 = [a_b3[:, 19 * TCH:20 * TCH], a_b3[:, 20 * TCH:21 * TCH]]
    betac = sb("betac", [P, DBLK], BF16)
    _fb = 21 * TCH
    fw2 = [a_b3[:, _fb:_fb + DBLK * P], a_b3[:, _fb + DBLK * P:_fb + 2 * DBLK * P]]
    fu2 = [a_b3[:, _fb + 2 * DBLK * P:_fb + 3 * DBLK * P],
           a_b3[:, _fb + 3 * DBLK * P:_fb + 4 * DBLK * P]]
    _o = _fb + 4 * DBLK * P
    gt2 = [a_b3[:, _o + j * MINE:_o + (j + 1) * MINE] for j in range(4)]
    gt4 = a_b3[:, _o:_o + 4 * MINE]
    pt2 = [a_b3[:, _o + (4 + j) * MINE:_o + (5 + j) * MINE] for j in range(4)]
    pt4 = a_b3[:, _o + 4 * MINE:_o + 8 * MINE]
    mrow4 = a_b3[:, _o + 8 * MINE:_o + 12 * MINE]
    mrow24 = a_b3[:, _o + 12 * MINE:_o + 16 * MINE]
    mrow = a_b3[:, _o + 16 * MINE:_o + 17 * MINE]
    sqrep4 = a_b3[:, _o + 17 * MINE:_o + 21 * MINE]
    sqrrep4 = a_b3[:, _o + 21 * MINE:_o + 25 * MINE]
    qsc4 = a_b3[:, _o + 25 * MINE:_o + 29 * MINE]
    fcm2 = [a_b3[:, _o + (29 + j) * MINE:_o + (30 + j) * MINE] for j in range(4)]
    fc4 = a_b3[:, _o + 29 * MINE:_o + 33 * MINE]
    _w = _o + 33 * MINE
    wdsl2 = [a_b3[:, _w:_w + IBLK * P], a_b3[:, _w + IBLK * P:_w + 2 * IBLK * P]]
    zqb = a_b3[:, _w + 2 * IBLK * P:_w + 2 * IBLK * P + NMT * D]
    AB3 = _w + 2 * IBLK * P + NMT * D
    
    pm0 = psb("pm0", [P, 512])
    pm1 = psb("pm1", [P, 512])
    pm2 = psb("pm2", [P, 512])
    pm3 = psb("pm3", [P, 512])
    ptrp = psb("ptrp", [P, 2 * P], BF16)
    ptr = ptrp[:, 0:P]
    ptr2 = ptrp[:, P:2 * P]
    ptrw = [psb("ptrw0", [P, DBLK * P], BF16), psb("ptrw1", [P, DBLK * P], BF16)]
    pmq = psb("pmq", [P, 512])

    class Sem:
        def __init__(self, name):
            self.h = nc.alloc_semaphore(name=name)
            self.n = 0

        def inc(self, instr, k=1):
            instr.then_inc(self.h, k)
            self.n += k
            return instr

    s_ld, s_g, s_v, s_s, s_t, s_st, s_bar, s_ge = (Sem(x) for x in
        ("ld", "g", "v", "s", "t", "st", "bar", "ge"))
    s_rp = [Sem("rp0"), Sem("rp1"), Sem("rp2")]
    s_x = [Sem("x0"), Sem("x1"), Sem("x2")]
    s_fw = [Sem("fw0"), Sem("fw1")]
    s_dw = [Sem("dw0"), Sem("dw1")]
    s_gx = [Sem(f"gx{i}") for i in range(6)]
    s_gq3 = [Sem(f"gq{i}") for i in range(3)]
    s_gl3 = [Sem(f"gl{i}") for i in range(3)]

    V, S, TE, G, SY = nc.vector, nc.scalar, nc.tensor, nc.gpsimd, nc.sync

    def vf(instr=None):
        if instr is None:
            instr = V.engine_nop()
        s_v.inc(instr)
        V.wait_ge(s_v.h, s_v.n)
        return instr

    def sf(instr=None):
        if instr is None:
            instr = S.copy(statw[0:1, 13:14], consts[0:1, 7:8])
        s_s.inc(instr)
        S.wait_ge(s_s.h, s_s.n)
        return instr
    ALL_E = ("sync", "vector", "scalar", "tensor", "gpsimd")

    def barrier():
        SY.wait_ge(s_ld.h, s_ld.n)
        SY.wait_ge(s_st.h, s_st.n)
        G.wait_ge(s_g.h, s_g.n)
        G.wait_ge(s_ge.h, s_ge.n)
        for e in ALL_E:
            eng = getattr(nc, e)
            if e != "sync":
                eng.drain()
            eng.sem_inc(s_bar.h, 1)
        s_bar.n += len(ALL_E)
        for e in ALL_E:
            getattr(nc, e).wait_ge(s_bar.h, s_bar.n)

    NG = (D + 511) // 512

    def ln_stats(dst2, src):
        V.wait_ge(s_v.h, s_v.n)
        for gi in range(NG):
            lo, hi = gi * 512, min(D, (gi + 1) * 512)
            V.bn_stats(bns[:, 6 * gi:6 * gi + 6], src[:, lo:hi])
            vf()
        return V.bn_aggr(dst2, bns[:, 0:6 * NG])

    def rsqrt_cols(src_ap, dst_ap, tmp_ap):
        S.activation(tmp_ap, src_ap, AF.Ln, bias=LN_EPS)
        sf()
        return S.activation(dst_ap, tmp_ap, AF.Exp, scale=-0.5)

    ptrw_g = [0, 0]
    gp_v = [0, 0, 0, 0]
    q4v = [0]
    fcm_t = [0, 0, 0, 0]
    pm_g = {i: {"s": 0, "v": 0} for i in range(4)}
    xg = {"a_scal": 0, "zn_s": [0, 0], "znT_rd_t": 0, "bc_st": 0,
          "rp_fin": [0, 0], "acc_rd_t": 0, "yam_rd_s": 0, "um_rd_v": 0,
          "z2q_v": 0, "qs_v": 0, "o_st": [0, 0], "zn_v": 0,
          "pmq_s": 0, "sqT_t": 0, "bct_s": 0, "rep_t": 0, "ln4_s": 0}

    def pm_wait(bank):
        TE.wait_ge(s_s.h, pm_g[bank]["s"])
        TE.wait_ge(s_v.h, pm_g[bank]["v"])

    def finish():
        SY.wait_ge(s_ld.h, s_ld.n)
        SY.wait_ge(s_st.h, s_st.n)
        for cm in reversed(ctxs):
            cm.__exit__(None, None, None)
        return nc

    STOP = cfg.get("stop")

    V.memset(consts[:, 6:7], -0.5)
    vf()
    V.memset(consts[:, 7:8], 1.0)
    vf()
    V.memset(consts[:, 8:9], 0.5)
    vf()
    V.memset(consts[:, 9:10], 0.0)
    vf()
    V.memset(consts[:, 10:11], LN_EPS)
    vf()
    s_v.inc(V.memset(consts[:, 11:12], -0.5))
    for val, col in ((-0.5, 6), (1.0, 7), (0.5, 8), (0.0, 9), (LN_EPS, 10)):
        nc.const_aps.register(F32, val, consts[:, col:col + 1])
    s_ld.inc(SY.dma_start(gam[:], gam_h[:]), 16)
    s_ld.inc(SY.dma_start(bet[:], bet_h[:]), 16)
    s_ld.inc(SY.dma_start(apar[:], apar_h[:]), 16)
    s_g.inc(G.dma_start(drep[:], dpar_h.partition_broadcast(P)), 16)
    s_ld.inc(SY.dma_start(parc[:, 0:1], parf_h.partition_broadcast(P)), 16)
    s_ld.inc(SY.dma_start(scr[0:1, D + 2 * N:2 * D + 2 * N], bdt_h[0:1, 0:D]), 16)
    s_g.inc(G.dma_start(wdtg.ap().rearrange("p (c o) -> p c o", c=DBLK),
                        wdtT_h.rearrange("(c p) o -> p c o", p=P)), 16)
    s_g.inc(G.dma_start(wbg.ap().rearrange("p (c n) -> p c n", c=DBLK),
                        wbT_h.rearrange("(c p) n -> p c n", p=P)), 16)
    s_g.inc(G.dma_start(wcg.ap().rearrange("p (c n) -> p c n", c=DBLK),
                        wcT_h.rearrange("(c p) n -> p c n", p=P)), 16)

    G.wait_ge(s_g.h, s_g.n)
    s_ge.inc(G.memset(onesrow[:], 1.0))
    s_ge.inc(G.memset(sel_all[:], 0.0))
    G.wait_ge(s_ge.h, s_ge.n)
    for n_ in range(N):
        s_ge.inc(G.affine_select(out=sel_all[0:N, n_ * P:(n_ + 1) * P],
                                 in_=sel_all[0:N, n_ * P:(n_ + 1) * P],
                                 compare_op=ALU.not_equal, fill=1.0, base=-n_,
                                 pattern=[[0, P]], channel_multiplier=1))
    s_ge.inc(G.memset(ident[:], 0.0))
    G.wait_ge(s_ge.h, s_ge.n)
    s_ge.inc(G.affine_select(out=ident[:], in_=ident[:], compare_op=ALU.not_equal,
                             fill=1.0, base=0, pattern=[[-1, P]], channel_multiplier=1))

    S.wait_ge(s_ld.h, s_ld.n)
    S.wait_ge(s_v.h, s_v.n)
    S.activation(apar[:], apar[:], AF.Exp)
    sf()
    s_s.inc(S.mul(apar[:], apar[:], -1.0))

    V.wait_ge(s_ld.h, s_ld.n)
    V.wait_ge(s_g.h, s_g.n)
    V.tensor_scalar(parc[:, 1:2], parc[:, 0:1], -1.0, 1.0, ALU.mult, ALU.add)
    vf()
    V.tensor_copy(onescol[:], consts[:, 7:8])
    vf()
    s_v.inc(V.tensor_copy(betac[:, 0:DBLK], bet[:]))
    TE.wait_ge(s_v.h, s_v.n)
    TE.wait_ge(s_g.h, s_g.n)

    for oc in range(0, D, 512):
        w = min(512, D - oc)
        for c in range(DBLK):
            mm = TE.matmul(pm0[0:1, 0:w], betac[:, c:c + 1],
                           wdtg[:, c * D + oc:c * D + oc + w],
                           start=(c == 0), stop=(c == DBLK - 1))
            if c == DBLK - 1:
                s_t.inc(mm)
        V.wait_ge(s_t.h, s_t.n)
        s_v.inc(V.tensor_copy(scr[0:1, oc:oc + w], pm0[0:1, 0:w]))
        TE.wait_ge(s_v.h, s_v.n)
        pm_g[0]["v"] = s_v.n
    for wi, wsrc in enumerate((wbg, wcg)):
        for c in range(DBLK):
            mm = TE.matmul(pm1[0:1, 0:N], betac[:, c:c + 1], wsrc[:, c * N:(c + 1) * N],
                           start=(c == 0), stop=(c == DBLK - 1))
            if c == DBLK - 1:
                s_t.inc(mm)
        V.wait_ge(s_t.h, s_t.n)
        s_v.inc(V.tensor_copy(scr[0:1, D + wi * N:D + (wi + 1) * N], pm1[0:1, 0:N]))
        TE.wait_ge(s_v.h, s_v.n)
        pm_g[1]["v"] = s_v.n
    V.wait_ge(s_ld.h, s_ld.n)
    s_v.inc(V.tensor_tensor(scr[0:1, 0:D], scr[0:1, 0:D], scr[0:1, D + 2 * N:2 * D + 2 * N], ALU.add))
    SY.wait_ge(s_v.h, s_v.n)
    s_st.inc(SY.dma_start(brow_d[0:1, :], scr[0:1, 0:D + 2 * N]), 16)
    SY.wait_ge(s_st.h, s_st.n)
    s_ld.inc(SY.dma_start(bdtc.ap().rearrange("p (one c) -> p one c", one=1),
                          brow_d[0:1, 0:D].rearrange("one (c p) -> p one c", p=P), allow_slow_non_contiguous=True), 16)
    s_ld.inc(SY.dma_start(bbc[0:N, 0:2].rearrange("n (one two) -> n one two", one=1),
                          brow_d[0:1, D:D + 2 * N].rearrange("one (two n) -> n one two", n=N), allow_slow_non_contiguous=True), 16)

    V.wait_ge(s_s.h, s_s.n)
    V.wait_ge(s_ld.h, s_ld.n)
    for c in range(DBLK):
        V.tensor_scalar(wdtg[:, c * D:(c + 1) * D], wdtg[:, c * D:(c + 1) * D],
                        gam[:, c:c + 1], None, ALU.mult)
        V.tensor_scalar(wbg[:, c * N:(c + 1) * N], wbg[:, c * N:(c + 1) * N],
                        gam[:, c:c + 1], None, ALU.mult)
        V.tensor_scalar(wcg[:, c * N:(c + 1) * N], wcg[:, c * N:(c + 1) * N],
                        gam[:, c:c + 1], None, ALU.mult)
    barrier()


    if STOP == "prologue":
        return finish()
    wsc2 = [a_f[:, NMT * D + i * WQC:NMT * D + (i + 1) * WQC] for i in range(6)]
    wqa = [a_b3[:, i * WQC:(i + 1) * WQC] for i in range(6)]
    wqb = [a_b3[:, (6 + i) * WQC:(7 + i) * WQC] for i in range(6)]


    def gen_quant(w_h, wq_d, nrows, slot):
        CH_ROWS = WQC
        chunks = list(range(0, nrows, CH_ROWS))
        bufs = (2 * slot, 2 * slot + 1)
        ldsem = (s_gx[2 * slot], s_gx[2 * slot + 1])
        stsem = (s_fw[0], s_fw[1], s_dw[0])[slot]
        pmx = (pm0, pm1, pm2)[slot]
        sw = 4 * slot
        nmm = 0
        for r0 in chunks:
            rr = min(CH_ROWS, nrows - r0)
            nmm += (rr + 511) // 512
        k = 0
        s_done_abs = [0, 0]
        t_done_abs = [0, 0]
        ld_w = [0, 0]
        for ci, r0 in enumerate(chunks):
            b = ci % 2
            p = bufs[b]
            rr = min(CH_ROWS, nrows - r0)
            w = rr
            G.wait_ge(s_s.h, s_done_abs[b])
            ldsem[b].inc(G.dma_start(wsc2[p][:, 0:w].rearrange("p (a q) -> p a q", q=P),
                                     w_h[r0:r0 + rr, :].rearrange("(p a) q -> p a q", p=P)), 16)
            ld_w[b] = ldsem[b].n
            S.wait_ge(ldsem[b].h, ld_w[b])
            S.wait_ge(s_t.h, t_done_abs[b])
            s_s.inc(S.activation(wqa[p][:, 0:w], wsc2[p][:, 0:w], AF.Abs))
            s_done_abs[b] = s_s.n
            TE.wait_ge(s_s.h, s_done_abs[b])
            if ci == 0:
                pm_wait(slot)
            noff = (w + 511) // 512
            for oi, off in enumerate(range(0, w, 512)):
                ww = min(512, w - off)
                mm = TE.matmul(pmx[0:1, 0:ww], onescol[:], wqa[p][:, off:off + ww],
                               start=(k == 0), stop=(k == nmm - 1))
                if oi == noff - 1:
                    s_t.inc(mm)
                k += 1
            t_done_abs[b] = s_t.n
            yield
        V.wait_ge(s_t.h, s_t.n)
        s_v.inc(V.tensor_reduce(statw[0:1, sw:sw + 1], pmx[0:1, 0:512], AX.X, ALU.add))
        pm_g[slot]["v"] = s_v.n
        V.wait_ge(s_v.h, s_v.n)
        V.tensor_scalar(statw[0:1, sw + 1:sw + 2], statw[0:1, sw:sw + 1],
                        1.0 / float(nrows * P), 1e-5, ALU.mult, ALU.max)
        vf()
        V.reciprocal(statw[0:1, sw + 2:sw + 3], statw[0:1, sw + 1:sw + 2])
        vf()
        s_v.inc(V.tensor_scalar(statw[0:1, sw + 3:sw + 4], statw[0:1, sw + 1:sw + 2],
                                0.5, None, ALU.mult))
        G.wait_ge(s_v.h, s_v.n)
        s_gq3[slot].inc(G.dma_start(wqs_d[slot:slot + 1, 0:2], statw[0:1, sw + 2:sw + 4]), 16)
        G.wait_ge(s_gq3[slot].h, s_gq3[slot].n)
        s_gl3[slot].inc(G.dma_start(consts[:, slot:slot + 1],
                                    wqs_d[slot:slot + 1, 0:1].partition_broadcast(P)), 16)
        s_gl3[slot].inc(G.dma_start(consts[:, 3 + slot:4 + slot],
                                    wqs_d[slot:slot + 1, 1:2].partition_broadcast(P)), 16)
        gl_done = s_gl3[slot].n
        yield
        S.wait_ge(s_gl3[slot].h, gl_done)
        s_done_sg = list(s_done_abs)
        v_done_sg = [0, 0]
        st_all = stsem.n
        for ci, r0 in enumerate(chunks):
            b = ci % 2
            p = bufs[b]
            rr = min(CH_ROWS, nrows - r0)
            w = rr
            G.wait_ge(s_s.h, s_done_sg[b])
            ldsem[b].inc(G.dma_start(wsc2[p][:, 0:w].rearrange("p (a q) -> p a q", q=P),
                                     w_h[r0:r0 + rr, :].rearrange("(p a) q -> p a q", p=P)), 16)
            ld_w[b] = ldsem[b].n
            S.wait_ge(ldsem[b].h, ld_w[b])
            S.wait_ge(s_v.h, v_done_sg[b])
            S.wait_ge(stsem.h, st_all)
            S.activation(wqa[p][:, 0:w], wsc2[p][:, 0:w], AF.Sign,
                         scale=consts[:, slot:slot + 1], bias=consts[:, 6:7])
            s_s.inc(S.activation(wqb[p][:, 0:w], wsc2[p][:, 0:w], AF.Sign,
                                 scale=consts[:, slot:slot + 1], bias=consts[:, 8:9]))
            s_done_sg[b] = s_s.n
            V.wait_ge(s_s.h, s_done_sg[b])
            s_v.inc(V.tensor_tensor(wqa[p][:, 0:w], wqa[p][:, 0:w], wqb[p][:, 0:w],
                                    ALU.add))
            v_done_sg[b] = s_v.n
            SY.wait_ge(s_v.h, v_done_sg[b])
            stsem.inc(SY.dma_start(wq_d[r0:r0 + rr, :].rearrange("(p a) q -> p a q", p=P),
                                   wqa[p][:, 0:w].rearrange("p (a q) -> p a q", q=P)), 16)
            if ci >= 1:
                st_all = stsem.n - 16
            yield
        SY.wait_ge(stsem.h, stsem.n)

    qgens = [gen_quant(wgT_h, wgq_d, INNER // P * D, 0),
             gen_quant(wuT_h, wuq_d, INNER // P * D, 1),
             gen_quant(wdT_h, wdq_d, D // P * INNER, 2)]
    live = list(qgens)
    while live:
        for g in list(live):
            try:
                next(g)
            except StopIteration:
                live.remove(g)
    V.wait_ge(s_gl3[0].h, s_gl3[0].n)
    s_v.inc(V.tensor_scalar(consts[:, 12:13], consts[:, 3:4], 0.5, None, ALU.mult))
    barrier()

    if STOP == "quant":
        return finish()
    s_v.inc(V.memset(state[:], 0.0))
    vf()

    def gen_A(sidx):
        t0 = sidx * TCH
        kt0 = t0 // P
        v_done_x = [0, 0]
        ld_x = [0, 0]
        V.wait_ge(s_s.h, xg["a_scal"])
        a_st = 0
        for k in range(NT):
            p = k % 2
            if k < 2:
                SY.wait_ge(s_s.h, xg["zn_s"][0])
                SY.wait_ge(s_s.h, xg["zn_s"][1])
            s_x[p].inc(SY.dma_start(xts[:, k * D:(k + 1) * D],
                                    x_h[t0 + k * P:t0 + (k + 1) * P, :]), 16)
            ld_x[p] = s_x[p].n
            V.wait_ge(s_x[p].h, ld_x[p])
            s_v.inc(ln_stats(stats[:, 2 * k:2 * k + 2], xts[:, k * D:(k + 1) * D]))
            v_done_x[p] = s_v.n
            a_st = s_v.n
            yield
        mview = stats.ap().rearrange("p (k two) -> p two k", two=2)
        mean_v = mview[:, 0, :]
        var_v = mview[:, 1, :]
        S.wait_ge(s_v.h, a_st)
        s_s.inc(rsqrt_cols(var_v, sc1[:, 0:NT], statw[:, 0:NT]))
        a_rs = s_s.n
        yield
        V.wait_ge(s_s.h, a_rs)
        V.tensor_copy(r1v[:, 0:NT], sc1[:, 0:NT])
        vf()
        V.tensor_tensor(statw[:, 0:NT], sc1[:, 0:NT], sc1[:, 0:NT], ALU.mult)
        vf()
        s_v.inc(V.tensor_tensor(statw[:, 0:NT], statw[:, 0:NT], var_v, ALU.mult))
        S.wait_ge(s_v.h, s_v.n)
        s_s.inc(rsqrt_cols(statw[:, 0:NT], statw[:, NT:2 * NT], statw[:, 0:NT]))
        a_rs2 = s_s.n
        yield
        V.wait_ge(s_s.h, a_rs2)
        V.tensor_tensor(sc1[:, 0:NT], sc1[:, 0:NT], statw[:, NT:2 * NT], ALU.mult)
        vf()
        V.tensor_tensor(sc2[:, 0:NT], mean_v, sc1[:, 0:NT], ALU.mult)
        vf()
        vf()
        V.tensor_scalar(sc2[:, 0:NT], sc2[:, 0:NT], -1.0, None, ALU.mult)
        vf()
        mev = stats.ap().rearrange("p (k par mv) -> p par mv k", par=2, mv=2)
        for (dst, ev, od) in ((mnM2[sidx % 2], mev[:, 0, 0, :], mev[:, 1, 0, :]),
                              (r1M2[sidx % 2], r1v.ap().rearrange("p (k two) -> p two k", two=2)[:, 0, :],
                               r1v.ap().rearrange("p (k two) -> p two k", two=2)[:, 1, :])):
            V.tensor_scalar(statw[:, 0:NMT], od, parc[:, 0:1], None, ALU.mult)
            vf()
            V.scalar_tensor_tensor(dst[:, 0:NMT], ev, parc[:, 1:2],
                                   statw[:, 0:NMT], ALU.mult, ALU.add)
            vf()
        s_v.inc(V.engine_nop())
        S.wait_ge(s_v.h, s_v.n)
        yield
        s_done_zn = [0, 0]
        t_done_zn = [0, 0]
        V.wait_ge(s_t.h, xg["znT_rd_t"])
        znT_v = znT.rearrange("p (c t) -> p c t", c=DBLK)
        for k in range(NT):
            p = k % 2
            S.wait_ge(s_t.h, t_done_zn[p])
            s_s.inc(S.activation(znt2[p][:], xts[:, k * D:(k + 1) * D], AF.Identity,
                                 scale=sc1[:, k:k + 1], bias=sc2[:, k:k + 1]))
            s_done_zn[p] = s_s.n
            TE.wait_ge(s_s.h, s_done_zn[p])
            TE.wait_ge(s_v.h, ptrw_g[p])
            for c in range(DBLK):
                mm = TE.transpose(ptrw[p][:, c * P:(c + 1) * P],
                                  znt2[p][:, c * P:(c + 1) * P], ident[:])
                if c == DBLK - 1:
                    s_t.inc(mm)
            t_done_zn[p] = s_t.n
            V.wait_ge(s_t.h, t_done_zn[p])
            s_v.inc(V.tensor_copy(znT_v[:, :, k * P:(k + 1) * P],
                                  ptrw[p][:, 0:DBLK * P].rearrange("p (c q) -> p c q", c=DBLK)))
            ptrw_g[p] = s_v.n
            yield
        xg["a_scal"] = s_s.n
        xg["zn_s"] = list(s_done_zn)
        xg["zn_v"] = s_v.n

    def gen_B(sidx):
        t0 = sidx * TCH
        kt0 = t0 // P
        t_done_dt = [0, 0]
        s_done_dt = [0, 0]
        NTC = TCH // 512
        fscr2 = bscr2
        TE.wait_ge(s_v.h, xg["zn_v"])
        for c in range(DBLK):
            for ti, tc in enumerate(range(0, TCH, 512)):
                w = min(512, TCH - tc)
                g = (c * NTC + ti) % 2
                pmx = pm0 if g == 0 else pm1
                pm_wait(g)
                for kk in range(DBLK):
                    mm = TE.matmul(pmx[:, 0:w],
                                   wdtg[:, kk * D + c * P:kk * D + (c + 1) * P],
                                   znT[:, kk * TCH + tc:kk * TCH + tc + w],
                                   start=(kk == 0), stop=(kk == DBLK - 1))
                    if kk == DBLK - 1:
                        s_t.inc(mm)
                t_done_dt[g] = s_t.n
                S.wait_ge(s_t.h, t_done_dt[g])
                S.activation(fscr2[g][:, 0:w], pmx[:, 0:w], AF.Exp,
                             bias=bdtc[:, c:c + 1], scale=consts[:, 7:8])
                sf()
                s_s.inc(S.activation(dtT[:, c * TCH + tc:c * TCH + tc + w],
                                     fscr2[g][:, 0:w], AF.Ln, bias=1.0))
                s_done_dt[g] = s_s.n
                pm_g[g]["s"] = s_s.n
                yield
        s_done_bc = 0
        for wi, wsrc in enumerate((wbg, wcg)):
            for ti, tc in enumerate(range(0, TCH, 512)):
                w = min(512, TCH - tc)
                pm_wait(2)
                for kk in range(DBLK):
                    mm = TE.matmul(pm2[0:N, 0:w], wsrc[:, kk * N:(kk + 1) * N],
                                   znT[:, kk * TCH + tc:kk * TCH + tc + w],
                                   start=(kk == 0), stop=(kk == DBLK - 1))
                    if kk == DBLK - 1:
                        s_t.inc(mm)
                S.wait_ge(s_t.h, s_t.n)
                S.wait_ge(s_t.h, xg["rep_t"])
                s_s.inc(S.activation(bcT_all[0:N, wi * TCH + tc:wi * TCH + tc + w],
                                     pm2[0:N, 0:w], AF.Identity,
                                     bias=bbc[0:N, wi:wi + 1], scale=consts[0:N, 7:8]))
                s_done_bc = s_s.n
                pm_g[2]["s"] = s_s.n
                yield
        xg["bct_s"] = s_done_bc
        V.wait_ge(s_s.h, s_done_dt[0])
        V.wait_ge(s_s.h, s_done_dt[1])
        for c in range(DBLK):
            V.tensor_scalar(wpT[:, c * TCH:(c + 1) * TCH], znT[:, c * TCH:(c + 1) * TCH],
                            gam[:, c:c + 1], bet[:, c:c + 1], ALU.mult, ALU.add)
            vf()
            s_v.inc(V.tensor_tensor(wpT[:, c * TCH:(c + 1) * TCH],
                                    wpT[:, c * TCH:(c + 1) * TCH],
                                    dtT[:, c * TCH:(c + 1) * TCH], ALU.mult))
            yield
        xg["znT_rd_t"] = s_t.n
        xg["bc_st"] = s_st.n

    def gen_C(sidx):
        t0 = sidx * TCH
        kt0 = t0 // P
        v_done_rep = [0] * 5
        v_done_ab = [0] * 5
        s_done_ab = [0] * 5
        v_done_ht = [0, 0]
        g_done_ht = [0, 0]
        s_done_rep = [0] * 5
        V.wait_ge(s_t.h, xg["acc_rd_t"])
        V.wait_ge(s_v.h, s_v.n)
        for n in range(N + 1):
            p = n % 5
            if n < N:
                TE.wait_ge(s_s.h, xg["pmq_s"])
                TE.wait_ge(s_s.h, xg["bct_s"])
                s_t.inc(TE.matmul(pmq[:, 0:TCH], sel_all[0:N, n * P:(n + 1) * P],
                                  bcT_all[0:N, 0:TCH], start=True, stop=True))
                S.wait_ge(s_t.h, s_t.n)
                S.wait_ge(s_v.h, v_done_rep[p])
                s_s.inc(S.activation(brep2[p][:, 0:TCH], pmq[:, 0:TCH], AF.Identity,
                                     scale=consts[:, 7:8], bias=consts[:, 9:10]))
                xg["pmq_s"] = s_s.n
                TE.wait_ge(s_s.h, xg["pmq_s"])
                s_t.inc(TE.matmul(pmq[:, 0:TCH], sel_all[0:N, n * P:(n + 1) * P],
                                  bcT_all[0:N, TCH:2 * TCH], start=True, stop=True))
                xg["rep_t"] = s_t.n
                S.wait_ge(s_t.h, s_t.n)
                s_s.inc(S.activation(crep2[p][:, 0:TCH], pmq[:, 0:TCH], AF.Identity,
                                     scale=consts[:, 7:8], bias=consts[:, 9:10]))
                xg["pmq_s"] = s_s.n
                s_done_rep[p] = s_s.n
            if n == 0:
                continue
            nn = n - 1
            pp = nn % 5
            for c0 in range(0, DBLK, 2):
                cpair = (c0, c0 + 1)
                for c in cpair:
                    ac = c % 5
                    S.wait_ge(s_v.h, v_done_ab[ac])
                    s_s.inc(S.activation(abar2[ac][:, 0:TCH],
                                         dtT[:, c * TCH:(c + 1) * TCH], AF.Exp,
                                         scale=apar[:, c * N + nn:c * N + nn + 1],
                                         bias=consts[:, 9:10]))
                    s_done_ab[ac] = s_s.n
                V.wait_ge(s_s.h, s_done_rep[pp])
                V.wait_ge(s_s.h, s_done_ab[cpair[1] % 5])
                V.wait_ge(s_ge.h, g_done_ht[0])
                V.wait_ge(s_ge.h, g_done_ht[1])
                V.wait_ge(s_v.h, v_done_ht[0])
                V.wait_ge(s_v.h, v_done_ht[1])
                bx_n = [0, 0]
                sc_n = [0, 0]
                ch_n = [0, 0]
                for c in cpair:
                    pc = c % 2
                    s_v.inc(V.tensor_tensor(bxt2[pc][:, 0:TCH],
                                            wpT[:, c * TCH:(c + 1) * TCH],
                                            brep2[pp][:, 0:TCH], ALU.mult))
                    bx_n[pc] = s_v.n
                for c in cpair:
                    pc = c % 2
                    ac = c % 5
                    V.wait_ge(s_v.h, bx_n[pc])
                    s_v.inc(V.tensor_tensor_scan(ht2[pc][:, 0:TCH],
                                                 abar2[ac][:, 0:TCH],
                                                 bxt2[pc][:, 0:TCH],
                                                 state[:, c * N + nn:c * N + nn + 1],
                                                 ALU.mult, ALU.add))
                    sc_n[pc] = s_v.n
                    v_done_ab[ac] = s_v.n
                for c in cpair:
                    pc = c % 2
                    V.wait_ge(s_v.h, sc_n[pc])
                    s_v.inc(V.tensor_tensor(cht2[pc][:, 0:TCH], ht2[pc][:, 0:TCH],
                                            crep2[pp][:, 0:TCH], ALU.mult))
                    ch_n[pc] = s_v.n
                for c in cpair:
                    pc = c % 2
                    V.wait_ge(s_v.h, ch_n[pc])
                    if nn == 0:
                        s_v.inc(V.tensor_tensor(accs[:, c * TCH:(c + 1) * TCH],
                                                ht2[pc][:, 0:TCH], crep2[pp][:, 0:TCH],
                                                ALU.mult))
                    else:
                        s_v.inc(V.tensor_tensor(accs[:, c * TCH:(c + 1) * TCH],
                                                accs[:, c * TCH:(c + 1) * TCH],
                                                cht2[pc][:, 0:TCH], ALU.add))
                    v_done_ht[pc] = s_v.n
                    G.wait_ge(s_v.h, v_done_ht[pc])
                    s_ge.inc(G.tensor_copy(state[:, c * N + nn:c * N + nn + 1],
                                           ht2[pc][:, TCH - 1:TCH]))
                    g_done_ht[pc] = s_ge.n
                yield
            v_done_rep[pp] = s_v.n
        V.wait_ge(s_ge.h, s_ge.n)
        V.wait_ge(s_s.h, xg["yam_rd_s"])
        for c in range(DBLK):
            p = c % 2
            TE.wait_ge(s_v.h, max(ptrw_g[p], s_v.n if c < 2 else 0))
            for k in range(NT):
                mm = TE.transpose(ptrw[p][:, k * P:(k + 1) * P],
                                  accs[:, c * TCH + k * P:c * TCH + (k + 1) * P],
                                  ident[:])
                if k == NT - 1:
                    s_t.inc(mm)
            V.wait_ge(s_t.h, s_t.n)
            pv = ptrw[p][:, 0:NT * P].rearrange("p (k two q) -> p k two q", two=2, q=P)
            vf(V.tensor_scalar(ytd[:, 0:MINE].rearrange("p (k q) -> p k q", q=P),
                               pv[:, :, 1, :], parc[:, 0:1], None, ALU.mult))
            yv = yam.rearrange("p (k d) -> p k d", k=NMT)
            s_v.inc(V.scalar_tensor_tensor(yv[:, :, c * P:(c + 1) * P],
                                           pv[:, :, 0, :], parc[:, 1:2],
                                           ytd[:, 0:MINE].rearrange("p (k q) -> p k q", q=P),
                                           ALU.mult, ALU.add))
            ptrw_g[p] = s_v.n
            yield
        xg["acc_rd_t"] = s_t.n

    def gen_C2(sidx):
        t0 = sidx * TCH
        kt0 = t0 // P
        V.wait_ge(s_s.h, xg["ln4_s"])
        SY.wait_ge(s_v.h, xg["um_rd_v"])
        for k in range(NMT):
            s_ld.inc(SY.dma_start(um[:, k * D:(k + 1) * D],
                                  xm_h[sidx * MINE + k * P:sidx * MINE + (k + 1) * P, :]), 16)
        yield
        V.wait_ge(s_t.h, s_t.n)
        mnM = mnM2[sidx % 2]
        r1M = r1M2[sidx % 2]
        vf(V.tensor_tensor(statw2[:, 0:NMT], mnM[:, 0:NMT], r1M[:, 0:NMT], ALU.mult))
        s_v.inc(V.tensor_scalar(statw2[:, 0:NMT], statw2[:, 0:NMT], -1.0, None, ALU.mult))
        S.wait_ge(s_v.h, s_v.n)
        S.wait_ge(s_ld.h, s_ld.n)
        for k in range(NMT):
            mm = S.activation(z1m[:, k * D:(k + 1) * D], um[:, k * D:(k + 1) * D],
                              AF.Identity, scale=r1M[:, k:k + 1], bias=statw2[:, k:k + 1])
            if k == NMT - 1:
                s_s.inc(mm)
        c2_z1 = s_s.n
        yield
        V.wait_ge(s_s.h, c2_z1)
        for k in range(NMT):
            V.tensor_tensor(z1m[:, k * D:(k + 1) * D], z1m[:, k * D:(k + 1) * D],
                            drep[:], ALU.mult)
        vf()
        for k in range(NMT):
            V.tensor_tensor(yam[:, k * D:(k + 1) * D], yam[:, k * D:(k + 1) * D],
                            z1m[:, k * D:(k + 1) * D], ALU.add)
        vf()
        c2_st = 0
        for k in range(NMT):
            s_v.inc(ln_stats(ln3s[:, 2 * k:2 * k + 2], yam[:, k * D:(k + 1) * D]))
            c2_st = s_v.n
            yield
        S.wait_ge(s_v.h, c2_st)
        l3v = ln3s.ap().rearrange("p (k two) -> p two k", two=2)
        s_s.inc(rsqrt_cols(l3v[:, 1, :], sxv[:, 0:NMT], statw2[:, 0:NMT]))
        V.wait_ge(s_s.h, s_s.n)
        vf(V.tensor_tensor(statw2[:, 0:NMT], l3v[:, 0, :], sxv[:, 0:NMT], ALU.mult))
        s_v.inc(V.tensor_scalar(statw2[:, 0:NMT], statw2[:, 0:NMT], -1.0, None, ALU.mult))
        S.wait_ge(s_v.h, s_v.n)
        for k in range(NMT):
            mm = S.activation(fTM[:, k * D:(k + 1) * D], yam[:, k * D:(k + 1) * D],
                              AF.Identity, scale=sxv[:, k:k + 1], bias=statw2[:, k:k + 1])
            if k == NMT - 1:
                s_s.inc(mm)
        xg["yam_rd_s"] = s_s.n
        c2_f = s_s.n
        yield
        V.wait_ge(s_s.h, c2_f)
        for k in range(NMT):
            mm = V.tensor_tensor(um[:, k * D:(k + 1) * D], um[:, k * D:(k + 1) * D],
                                 fTM[:, k * D:(k + 1) * D], ALU.add)
            if k == NMT - 1:
                s_v.inc(mm)
        S.wait_ge(s_v.h, s_v.n)
        SY.wait_ge(s_v.h, s_v.n)

    def gen_D_head(sidx):
        t0 = sidx * TCH
        kt0 = t0 // P
        zq = zqb
        V.wait_ge(s_s.h, xg["ln4_s"])
        dh_st = 0
        for k in range(NMT):
            s_v.inc(ln_stats(ln2s[:, 2 * k:2 * k + 2], um[:, k * D:(k + 1) * D]))
            dh_st = s_v.n
            yield
        l2v = ln2s.ap().rearrange("p (k two) -> p two k", two=2)
        S.wait_ge(s_v.h, dh_st)
        s_s.inc(rsqrt_cols(l2v[:, 1, :], sxv[:, 0:NMT], statw2[:, 0:NMT]))
        V.wait_ge(s_s.h, s_s.n)
        vf(V.tensor_tensor(statw2[:, 0:NMT], l2v[:, 0, :], sxv[:, 0:NMT], ALU.mult))
        s_v.inc(V.tensor_scalar(statw2[:, 0:NMT], statw2[:, 0:NMT], -1.0, None, ALU.mult))
        S.wait_ge(s_v.h, s_v.n)
        for k in range(NMT):
            mm = S.activation(fTM[:, k * D:(k + 1) * D], um[:, k * D:(k + 1) * D],
                              AF.Identity, scale=sxv[:, k:k + 1], bias=statw2[:, k:k + 1])
            if k == NMT - 1:
                s_s.inc(mm)
        dh_z2 = s_s.n
        yield
        V.wait_ge(s_s.h, dh_z2)
        for k in range(NMT):
            V.tensor_reduce(statw2[:, k:k + 1], fTM[:, k * D:(k + 1) * D], AX.X,
                            ALU.max, apply_absolute_value=True)
        vf()
        yield
        V.tensor_scalar(statw2[:, 8:8 + NMT], statw2[:, 0:NMT], 1e-5, 1.0 / 127.0,
                        ALU.max, ALU.mult)
        vf()
        V.reciprocal(statw2[:, 0:NMT], statw2[:, 8:8 + NMT])
        vf()
        for k in range(NMT):
            V.tensor_scalar(fTM[:, k * D:(k + 1) * D], fTM[:, k * D:(k + 1) * D],
                            statw2[:, k:k + 1], MAGIC, ALU.mult, ALU.add)
        vf()
        for k in range(NMT):
            mm = V.tensor_scalar(zq[:, k * D:(k + 1) * D], fTM[:, k * D:(k + 1) * D],
                                 -MAGIC, statw2[:, 8 + k:9 + k], ALU.add, ALU.mult)
            if k == NMT - 1:
                s_v.inc(mm)
        TE.wait_ge(s_v.h, s_v.n)
        z2qT_v = z2qT.rearrange("p (c t) -> p c t", c=DBLK)
        for k in range(NMT):
            p = k % 2
            TE.wait_ge(s_v.h, ptrw_g[p])
            for c in range(DBLK):
                mm = TE.transpose(ptrw[p][:, c * P:(c + 1) * P],
                                  zq[:, k * D + c * P:k * D + (c + 1) * P], ident[:])
                if c == DBLK - 1:
                    s_t.inc(mm)
            V.wait_ge(s_t.h, s_t.n)
            s_v.inc(V.tensor_copy(z2qT_v[:, :, k * P:(k + 1) * P],
                                  ptrw[p][:, 0:DBLK * P].rearrange("p (c q) -> p c q", c=DBLK)))
            ptrw_g[p] = s_v.n
            yield
        xg["z2q_v"] = s_v.n
        TE.wait_ge(s_v.h, xg["z2q_v"])

    def gen_D(sidx):
        zq = zqb
        t_done_fw = [0, 0]
        s_done_gu = [0, 0, 0, 0]
        v_done_gp = gp_v
        ld_fw = [0, 0]

        def gu_load(ic):
            p = ic % 2
            SY.wait_ge(s_t.h, t_done_fw[p])
            s_fw[p].inc(SY.dma_start(fw2[p][:, 0:DBLK * P].rearrange("p (kk o) -> p kk o", kk=DBLK),
                                     wgq_d[ic * D:(ic + 1) * D, :].rearrange(
                                         "(p kk) o -> p kk o", p=P)), 16)
            s_fw[p].inc(SY.dma_start(fu2[p][:, 0:DBLK * P].rearrange("p (kk o) -> p kk o", kk=DBLK),
                                     wuq_d[ic * D:(ic + 1) * D, :].rearrange(
                                         "(p kk) o -> p kk o", p=P)), 16)
            ld_fw[p] = s_fw[p].n

        gu_load(0)
        for ic in range(IBLK):
            p = ic % 2
            q = ic % 4
            if ic + 1 < IBLK:
                gu_load(ic + 1)
            TE.wait_ge(s_fw[p].h, ld_fw[p])
            pmg = pm0 if p == 0 else pm2
            pmu = pm1 if p == 0 else pm3
            for tc in range(0, MINE, 512):
                w = min(512, MINE - tc)
                pm_wait(0 if p == 0 else 2)
                pm_wait(1 if p == 0 else 3)
                for kk in range(DBLK):
                    mm = TE.matmul(pmg[:, tc % 512:tc % 512 + w], fw2[p][:, kk * P:(kk + 1) * P],
                                   z2qT[:, kk * MINE + tc:kk * MINE + tc + w],
                                   start=(kk == 0), stop=(kk == DBLK - 1))
                    if kk == DBLK - 1:
                        s_t.inc(mm)
                for kk in range(DBLK):
                    mm = TE.matmul(pmu[:, tc % 512:tc % 512 + w], fu2[p][:, kk * P:(kk + 1) * P],
                                   z2qT[:, kk * MINE + tc:kk * MINE + tc + w],
                                   start=(kk == 0), stop=(kk == DBLK - 1))
                    if kk == DBLK - 1:
                        s_t.inc(mm)
                t_done_fw[p] = s_t.n
                S.wait_ge(s_t.h, t_done_fw[p])
                S.wait_ge(s_v.h, v_done_gp[q])
                s_s.inc(S.activation(gt2[q][:, tc:tc + w], pmg[:, tc % 512:tc % 512 + w],
                                     AF.Identity, scale=consts[:, 12:13],
                                     bias=consts[:, 9:10]))
                s_s.inc(S.activation(pt2[q][:, tc:tc + w], pmu[:, tc % 512:tc % 512 + w],
                                     AF.Identity, scale=consts[:, 4:5], bias=consts[:, 9:10]))
                s_done_gu[q] = s_s.n
                pm_g[0 if p == 0 else 2]["s"] = s_s.n
                pm_g[1 if p == 0 else 3]["s"] = s_s.n
            yield
            if q == 3:
                S.wait_ge(s_v.h, q4v[0])
                S.wait_ge(s_s.h, s_done_gu[q])
                s_s.inc(S.activation(sqrep4[:], gt4[:], AF.Tanh))
                V.wait_ge(s_s.h, s_s.n)
                vf(V.scalar_tensor_tensor(gt4[:], sqrep4[:], 1.0, gt4[:],
                                          ALU.add, ALU.mult))
                q4v[0] = s_v.n
                vf(V.tensor_tensor(qs[:, (ic - 3) * MINE:(ic + 1) * MINE], gt4[:],
                                   pt4[:], ALU.mult))
                for j in range(4):
                    v_done_gp[j] = s_v.n
                if ic == 3:
                    V.tensor_copy(mrow4[:], qs[:, 0:4 * MINE])
                    s_v.inc(V.tensor_copy(mrow24[:], qs[:, 0:4 * MINE]))
                else:
                    V.tensor_tensor(mrow4[:], mrow4[:],
                                    qs[:, (ic - 3) * MINE:(ic + 1) * MINE], ALU.max)
                    s_v.inc(V.tensor_tensor(mrow24[:], mrow24[:],
                                            qs[:, (ic - 3) * MINE:(ic + 1) * MINE],
                                            ALU.min))
                vf()
            yield
        vf(V.scalar_tensor_tensor(mrow4[:], mrow24[:], -1.0, mrow4[:],
                                  ALU.mult, ALU.max))
        vf(V.tensor_tensor(mrow4[:, 0:2 * MINE], mrow4[:, 0:2 * MINE],
                           mrow4[:, 2 * MINE:4 * MINE], ALU.max))
        vf(V.tensor_tensor(mrow[:, 0:MINE], mrow4[:, 0:MINE],
                           mrow4[:, MINE:2 * MINE], ALU.max))
        TE.wait_ge(s_v.h, s_v.n)
        for k in range(NMT):
            s_t.inc(TE.transpose(ptr[:, 0:P], mrow[:, k * P:(k + 1) * P], ident[:]))
            V.wait_ge(s_t.h, s_t.n)
            V.tensor_reduce(sqv[:, k:k + 1], ptr[:, 0:P], AX.X, ALU.max,
                            apply_absolute_value=True)
            vf()
            s_v.inc(V.tensor_copy(statw2[:, 15:16], sqv[:, k:k + 1]))
            TE.wait_ge(s_v.h, s_v.n)
            yield
        V.tensor_scalar(sqvr[:, 0:NMT], sqv[:, 0:NMT], 1e-5, 1.0 / 127.0,
                        ALU.max, ALU.mult)
        vf()
        V.reciprocal(sqv[:, 0:NMT], sqvr[:, 0:NMT])
        vf()
        V.wait_ge(s_t.h, xg["sqT_t"])
        V.tensor_copy(sqstg[:, 0:NMT], sqv[:, 0:NMT])
        vf()
        s_v.inc(V.tensor_copy(sqstg[:, NMT:2 * NMT], sqvr[:, 0:NMT]))
        TE.wait_ge(s_v.h, s_v.n)
        TE.wait_ge(s_v.h, ptrw_g[0])
        for j in range(2 * NMT):
            mm = TE.transpose(ptrw[0][0:1, j * P:(j + 1) * P], sqstg[:, j:j + 1], ident[:])
            if j == 2 * NMT - 1:
                s_t.inc(mm)
        xg["sqT_t"] = s_t.n
        S.wait_ge(s_t.h, s_t.n)
        s_s.inc(S.activation(sqTrow[0:1, 0:2 * NMT * P], ptrw[0][0:1, 0:2 * NMT * P],
                             AF.Identity, scale=consts[0:1, 7:8], bias=consts[0:1, 9:10]))
        V.wait_ge(s_s.h, s_s.n)
        s_v.inc(V.engine_nop())
        ptrw_g[0] = s_v.n
        TE.wait_ge(s_s.h, s_s.n)
        TE.wait_ge(s_s.h, xg["pmq_s"])
        for k in range(NMT):
            s_t.inc(TE.matmul(pmq[:, k * P:(k + 1) * P], onesrow[0:1, 0:P],
                              sqTrow[0:1, k * P:(k + 1) * P], start=True, stop=True))
            s_t.inc(TE.matmul(pmq[:, MINE + k * P:MINE + (k + 1) * P], onesrow[0:1, 0:P],
                              sqTrow[0:1, (NMT + k) * P:(NMT + k + 1) * P],
                              start=True, stop=True))
        S.wait_ge(s_t.h, s_t.n)
        S.wait_ge(s_v.h, xg["qs_v"])
        S.wait_ge(s_v.h, q4v[0])
        RB = 4 if 4 * MINE <= NMT * D else 2
        for rep in range(RB):
            s_s.inc(S.activation(sqrep4[:, rep * MINE:(rep + 1) * MINE],
                                 pmq[:, 0:MINE], AF.Identity,
                                 scale=consts[:, 7:8], bias=consts[:, 9:10]))
            s_s.inc(S.activation(sqrrep4[:, rep * MINE:(rep + 1) * MINE],
                                 pmq[:, MINE:2 * MINE], AF.Identity,
                                 scale=consts[:, 7:8], bias=consts[:, 9:10]))
        xg["pmq_s"] = s_s.n
        yield
        V.wait_ge(s_s.h, xg["pmq_s"])
        RW = RB * MINE
        for jj in range(IBLK // RB):
            sl = qs[:, jj * RW:(jj + 1) * RW]
            V.tensor_tensor(fTM[:, 0:RW], sl, sqrep4[:, 0:RW], ALU.mult)
            vf()
            V.tensor_scalar(fTM[:, 0:RW], fTM[:, 0:RW], MAGIC, -MAGIC,
                            ALU.add, ALU.add)
            vf()
            V.tensor_copy(qsc4[:, 0:RW], fTM[:, 0:RW])
            vf()
            s_v.inc(V.tensor_tensor(sl, qsc4[:, 0:RW], sqrrep4[:, 0:RW], ALU.mult))
            yield
        xg["qs_v"] = s_v.n
        TE.wait_ge(s_v.h, xg["qs_v"])
        t_done_wd = [0, 0]
        s_done_dn = [0, 0]
        v_done_fc = [0, 0]
        ld_wd = [0, 0]

        def dn_load(oc):
            p = oc % 2
            SY.wait_ge(s_t.h, t_done_wd[p])
            s_dw[p].inc(SY.dma_start(wdsl2[p][:, 0:IBLK * P].rearrange("p (kk o) -> p kk o", kk=IBLK),
                                     wdq_d[oc * INNER:(oc + 1) * INNER, :].rearrange(
                                         "(p kk) o -> p kk o", p=P)), 16)
            ld_wd[p] = s_dw[p].n

        dn_load(0)
        for oc in range(DBLK):
            p = oc % 2
            fq = oc % 4
            if oc + 1 < DBLK:
                dn_load(oc + 1)
            TE.wait_ge(s_dw[p].h, ld_wd[p])
            pmx = pm0 if p == 0 else pm1
            for tc in range(0, MINE, 512):
                w = min(512, MINE - tc)
                pm_wait(p)
                for kk in range(IBLK):
                    mm = TE.matmul(pmx[:, 0:w], wdsl2[p][:, kk * P:(kk + 1) * P],
                                   qs[:, kk * MINE + tc:kk * MINE + tc + w],
                                   start=(kk == 0), stop=(kk == IBLK - 1))
                    if kk == IBLK - 1:
                        s_t.inc(mm)
                t_done_wd[p] = s_t.n
                S.wait_ge(s_t.h, t_done_wd[p])
                S.wait_ge(s_v.h, gp_v[fq])
                S.wait_ge(s_t.h, fcm_t[fq])
                s_s.inc(S.activation(fcm2[fq][:, tc:tc + w], pmx[:, 0:w], AF.Identity,
                                     scale=consts[:, 5:6], bias=consts[:, 9:10]))
                s_done_dn[p] = s_s.n
                pm_g[p]["s"] = s_s.n
            yield
            TE.wait_ge(s_s.h, s_done_dn[p])
            TE.wait_ge(s_v.h, ptrw_g[p])
            ptw = ptrw[p]
            for k in range(NMT):
                mm = TE.transpose(ptw[:, k * P:(k + 1) * P], fcm2[fq][:, k * P:(k + 1) * P],
                                  ident[:])
                if k == NMT - 1:
                    s_t.inc(mm)
            t_done_wd[p] = s_t.n
            fcm_t[fq] = s_t.n
            V.wait_ge(s_t.h, t_done_wd[p])
            fTM_v = fTM.rearrange("p (k d) -> p k d", k=NMT)
            s_v.inc(V.tensor_copy(fTM_v[:, :, oc * P:(oc + 1) * P],
                                  ptw[:, 0:NMT * P].rearrange("p (k q) -> p k q", k=NMT)))
            ptrw_g[p] = s_v.n
            yield
        vf()
        for k in range(NMT):
            s_v.inc(ln_stats(ln2s[:, 2 * k:2 * k + 2], fTM[:, k * D:(k + 1) * D]))
        S.wait_ge(s_v.h, s_v.n)
        l4v = ln2s.ap().rearrange("p (k two) -> p two k", two=2)
        s_s.inc(rsqrt_cols(l4v[:, 1, :], sxv[:, 0:NMT], statw2[:, 0:NMT]))
        V.wait_ge(s_s.h, s_s.n)
        vf(V.tensor_tensor(statw2[:, 0:NMT], l4v[:, 0, :], sxv[:, 0:NMT], ALU.mult))
        s_v.inc(V.tensor_scalar(statw2[:, 0:NMT], statw2[:, 0:NMT], -1.0, None, ALU.mult))
        S.wait_ge(s_v.h, s_v.n)
        for k in range(NMT):
            mm = S.activation(zq[:, k * D:(k + 1) * D], fTM[:, k * D:(k + 1) * D],
                              AF.Identity, scale=sxv[:, k:k + 1], bias=statw2[:, k:k + 1])
            if k == NMT - 1:
                s_s.inc(mm)
        xg["ln4_s"] = s_s.n
        V.wait_ge(s_s.h, s_s.n)
        st_done_o = list(xg["o_st"])
        osc2 = [oscr, fscr]
        for k in range(NMT):
            p = k % 2
            V.wait_ge(s_st.h, st_done_o[p])
            s_v.inc(V.tensor_tensor(osc2[p][:, 0:D], um[:, k * D:(k + 1) * D],
                                    zq[:, k * D:(k + 1) * D], ALU.add))
            SY.wait_ge(s_v.h, s_v.n)
            s_st.inc(SY.dma_start(out_h[sidx * MINE + k * P:sidx * MINE + (k + 1) * P, :],
                                  osc2[p][:, 0:D]), 16)
            st_done_o[p] = s_st.n
            yield
        xg["o_st"] = list(st_done_o)
        xg["um_rd_v"] = s_v.n


    def gen_MIX(sidx):
        for _ in gen_A(sidx):
            yield
        for _ in gen_B(sidx):
            yield
        for _ in gen_C(sidx):
            yield

    def gen_DFull(sidx):
        for _ in gen_C2(sidx):
            yield
        for _ in gen_D_head(sidx):
            yield
        for _ in gen_D(sidx):
            yield

    def drive(mix, dside, ab_units=29):
        di = 0
        m = 0
        while True:
            rm, rd = (2, 2) if m < ab_units else (3, 4)
            for _ in range(rm):
                try:
                    next(mix)
                    m += 1
                except StopIteration:
                    return dside[di:]
            for _ in range(rd):
                if di < len(dside):
                    try:
                        next(dside[di])
                    except StopIteration:
                        di += 1

    dside = []
    for s in range(NSUP):
        if s >= 1:
            dside.append(gen_DFull(s - 1))
        dside = drive(gen_MIX(s), dside)
    dside.append(gen_DFull(NSUP - 1))
    for g in dside:
        for _ in g:
            pass
    barrier()

    SY.wait_ge(s_st.h, s_st.n)
    for cm in reversed(ctxs):
        cm.__exit__(None, None, None)
    return nc


def _prep_inputs(inputs, cfg):
    D, N = cfg["D"], cfg["N"]
    DBLK = D // P
    f = np.float32

    def arr(k):
        return np.asarray(inputs[k], f)

    gam = np.ascontiguousarray(arr("L0_attn_gamma").reshape(DBLK, P).T)
    bet = np.ascontiguousarray(arr("L0_attn_beta").reshape(DBLK, P).T)
    a_pp = np.ascontiguousarray(
        arr("L0_attn_A_log").reshape(DBLK, P, N).transpose(1, 0, 2).reshape(P, DBLK * N))
    return {
        "wdtT": np.ascontiguousarray(arr("L0_attn_W_dt").T),
        "wbT": np.ascontiguousarray(arr("L0_attn_W_B").T),
        "wcT": np.ascontiguousarray(arr("L0_attn_W_C").T),
        "gam_pp": gam, "bet_pp": bet, "a_pp": a_pp,
        "bdt_row": np.ascontiguousarray(arr("L0_attn_b_dt").reshape(1, D)),
        "dparam_row": np.ascontiguousarray(arr("L0_attn_D_param").reshape(1, D)),
        "wgT": np.ascontiguousarray(
            arr("L0_ffn_w_gate").T.reshape(D // P, P, -1, P)
            .transpose(2, 1, 0, 3).reshape(-1, P)),
        "wuT": np.ascontiguousarray(
            arr("L0_ffn_w_up").T.reshape(D // P, P, -1, P)
            .transpose(2, 1, 0, 3).reshape(-1, P)),
        "wdT": np.ascontiguousarray(
            arr("L0_ffn_w_down").T.reshape(-1, P, D // P, P)
            .transpose(2, 1, 0, 3).reshape(-1, P)),
    }


def kernel(**inputs):
    x = np.asarray(inputs["L0_attn_x"], np.float32)
    B, T, D = x.shape
    cfg = {"T": T, "D": D, "N": int(inputs["L0_attn_A_log"].shape[1]),
           "INNER": int(inputs["L0_ffn_w_gate"].shape[0]), "TCH": 512}
    nc = build_program(cfg)
    base = _prep_inputs(inputs, cfg)
    in_maps = []
    for c in range(8):
        b, j = c // 2, c % 2
        m = dict(base)
        m["x"] = np.ascontiguousarray(x[b])
        xb = x[b].reshape(T // P, P, D)
        m["xm"] = np.ascontiguousarray(xb[j::2].reshape(T // 2, D))
        m["parf"] = np.full((1, 1), j, np.float32)
        in_maps.append(m)
    res = run_bass_kernel_spmd(nc, in_maps, core_ids=list(range(8)))
    out = np.empty((B, T, D), np.float32)
    TCH = cfg["TCH"]
    MINE = TCH // 2
    NMT = MINE // P
    for c in range(8):
        b, j = c // 2, c % 2
        o = np.asarray(res.results[c]["out"])
        for s in range(T // TCH):
            for k in range(NMT):
                trow = s * TCH + (2 * k + j) * P
                out[b, trow:trow + P, :] = o[s * MINE + k * P:s * MINE + (k + 1) * P, :]
    return out


# version 43 (final)
# speedup vs baseline: 1.0466x; 1.0126x over previous
import numpy as np
import concourse.bass as bass
import concourse.mybir as mybir
from concourse.bass_utils import run_bass_kernel_spmd

F32 = mybir.dt.float32
BF16 = mybir.dt.bfloat16
I32 = mybir.dt.int32
AF = mybir.ActivationFunctionType
ALU = mybir.AluOpType
AX = mybir.AxisListType
ts = bass.ts

MAGIC = 12582912.0
LN_EPS = 1e-5
P = 128


def build_program(cfg):
    T, D, N, INNER = cfg["T"], cfg["D"], cfg["N"], cfg["INNER"]
    TCH = cfg["TCH"]
    DBG = cfg.get("debug", False)
    DBLK = D // P
    NSUP = T // TCH
    NT = TCH // P
    MINE = TCH // 2
    NMT = MINE // P
    IBLK = INNER // P
    WQC = min(INNER, 1024)

    nc = bass.Bass("TRN2", target_bir_lowering=False)

    def din(name, shape, dtype=F32):
        return nc.dram_tensor(name, shape, dtype, kind="ExternalInput").ap()

    x_h = din("x", [T, D])
    xm_h = din("xm", [T // 2, D])
    parf_h = din("parf", [1, 1])
    wdtT_h = din("wdtT", [D, D])
    wbT_h = din("wbT", [D, N])
    wcT_h = din("wcT", [D, N])
    gam_h = din("gam_pp", [P, DBLK])
    bet_h = din("bet_pp", [P, DBLK])
    bdt_h = din("bdt_row", [1, D])
    apar_h = din("a_pp", [P, DBLK * N])
    dpar_h = din("dparam_row", [1, D])
    wgT_h = din("wgT", [INNER // P * D, P])
    wuT_h = din("wuT", [INNER // P * D, P])
    wdT_h = din("wdT", [D // P * INNER, P])

    out_h = nc.dram_tensor("out", [NSUP * MINE, D], F32, kind="ExternalOutput").ap()

    def dsc(name, shape, dtype):
        return nc.dram_tensor(name, shape, dtype, kind="Internal").ap()

    bT_d = dsc("bT_d", [N, TCH], BF16)
    cT_d = dsc("cT_d", [N, TCH], BF16)
    wgq_d = dsc("wgq_d", [INNER // P * D, P], BF16)
    wuq_d = dsc("wuq_d", [INNER // P * D, P], BF16)
    wdq_d = dsc("wdq_d", [D // P * INNER, P], BF16)
    brow_d = dsc("brow_d", [1, D + 2 * N], F32)
    sq_d = dsc("sq_d", [2, MINE], BF16)
    wqs_d = dsc("wqs_d", [3, 2], F32)

    dbg = {}
    if DBG:
        for nm, shape in [("d_znT", [D, TCH]), ("d_dtT", [D, TCH]),
                          ("d_bT", [N, TCH]), ("d_cT", [N, TCH]),
                          ("d_wT", [D, TCH]), ("d_y", [TCH, D]),
                          ("d_u", [MINE, D]), ("d_z2q", [D, MINE]),
                          ("d_q", [INNER, MINE]), ("d_wgq", [D, INNER]),
                          ("d_f", [MINE, D]),
                          ("d_h", [P, TCH])]:
            dbg[nm] = nc.dram_tensor(nm, shape, F32, kind="ExternalOutput").ap()

    ctxs = []

    def sb(name, shape, dtype):
        cm = nc.sbuf_tensor(name, shape, dtype)
        t = cm.__enter__()
        ctxs.append(cm)
        return t

    def psb(name, shape, dtype=F32):
        cm = nc.psum_tensor(name, shape, dtype)
        t = cm.__enter__()
        ctxs.append(cm)
        return t

    ident = sb("ident", [P, P], BF16)
    onescol = sb("onescol", [P, 1], BF16)
    onesrow = sb("onesrow", [1, P], BF16)
    sqstg = sb("sqstg", [P, 8], BF16)
    sel_all = sb("sel_all", [N, N * P], BF16)
    bcT_all = sb("bcT_all", [N, 2 * TCH], BF16)
    sqTrow = sb("sqTrow", [1, 8 * P], BF16)
    gam = sb("gam", [P, DBLK], F32)
    bet = sb("bet", [P, DBLK], F32)
    apar = sb("apar", [P, DBLK * N], F32)
    drep = sb("drep", [P, D], BF16)
    bdtc = sb("bdtc", [P, DBLK], F32)
    bbc = sb("bbc", [N, 2], F32)
    parc = sb("parc", [P, 2], F32)
    consts = sb("consts", [P, 14], F32)
    wdtg = sb("wdtg", [P, DBLK * D], BF16)
    wbg = sb("wbg", [P, DBLK * N], BF16)
    wcg = sb("wcg", [P, DBLK * N], BF16)
    stats = sb("stats", [P, NT * 2], F32)
    sc1 = sb("sc1", [P, NT], F32)
    r1v = sb("r1v", [P, NT], F32)
    mnM2 = [sb("mnM0", [P, NMT], F32), sb("mnM1", [P, NMT], F32)]
    r1M2 = [sb("r1M0", [P, NMT], F32), sb("r1M1", [P, NMT], F32)]
    sc2 = sb("sc2", [P, NT], F32)
    statw = sb("statw", [P, 16], F32)
    statw2 = sb("statw2", [P, 16], F32)
    ln2s = sb("ln2s", [P, 2 * NMT], F32)
    ln3s = sb("ln3s", [P, 2 * NMT], F32)
    ln4s = sb("ln4s", [P, 2], F32)
    sxv = sb("sxv", [P, NMT], F32)
    sqv = sb("sqv", [P, NMT], F32)
    sqvr = sb("sqvr", [P, NMT], F32)
    accs = sb("accs", [P, DBLK * TCH], BF16)
    state = sb("state", [P, DBLK * N], F32)
    xts = sb("xts", [P, NT * D], F32)
    xt2 = [xts[:, 0:D], xts[:, D:2 * D]]
    xt0 = xt2[0]
    znt2 = [sb("znt", [P, D], BF16), sb("zntb", [P, D], BF16)]
    znt = znt2[0]
    bns = sb("bns", [P, 16], F32)

    a_b1 = sb("a_b1", [P, 2 * DBLK * TCH + IBLK * MINE], BF16)
    dtT = a_b1[:, 0:DBLK * TCH]
    wpT = a_b1[:, DBLK * TCH:2 * DBLK * TCH]
    qs = a_b1[:, 2 * DBLK * TCH:2 * DBLK * TCH + IBLK * MINE]
    a_b2 = sb("a_b2", [P, DBLK * TCH + 2 * NMT * D + MINE + DBLK * MINE], BF16)
    znT = a_b2[:, 0:DBLK * TCH]
    _o2 = DBLK * TCH
    yam = a_b2[:, _o2:_o2 + NMT * D]
    z1m = a_b2[:, _o2 + NMT * D:_o2 + 2 * NMT * D]
    ytd = a_b2[:, _o2 + 2 * NMT * D:_o2 + 2 * NMT * D + MINE]
    z2qT = a_b2[:, _o2 + 2 * NMT * D + MINE:_o2 + 2 * NMT * D + MINE + DBLK * MINE]
    SCRW = max(4 * TCH, 2 * D + 2 * N)
    FCOLS = max(DBLK * TCH, SCRW, 2 * NMT * D + 3 * D + 1024, NMT * D + 6 * WQC)
    a_f = sb("a_f", [P, FCOLS], F32)
    scr = a_f[:, 0:SCRW]

    wsc = a_f[:, NMT * D:NMT * D + WQC]
    um = a_f[:, 0:NMT * D]
    fTM = a_f[:, NMT * D:2 * NMT * D]
    fscr = a_f[:, 2 * NMT * D:2 * NMT * D + D]
    zscr = a_f[:, 2 * NMT * D + D:2 * NMT * D + 2 * D]
    oscr = a_f[:, 2 * NMT * D + 2 * D:2 * NMT * D + 3 * D]
    _bs = 2 * NMT * D + 3 * D
    bscr2 = [a_f[:, _bs:_bs + 512], a_f[:, _bs + 512:_bs + 1024]]
    a_b3 = sb("a_b3", [P, 21 * TCH + 4 * DBLK * P + 33 * MINE + 2 * IBLK * P + NMT * D], BF16)
    brep2 = [a_b3[:, j * TCH:(j + 1) * TCH] for j in range(5)]
    crep2 = [a_b3[:, (5 + j) * TCH:(6 + j) * TCH] for j in range(5)]
    abar2 = [a_b3[:, (10 + j) * TCH:(11 + j) * TCH] for j in range(5)]
    bxt2 = [a_b3[:, 15 * TCH:16 * TCH], a_b3[:, 16 * TCH:17 * TCH]]
    ht2 = [a_b3[:, 17 * TCH:18 * TCH], a_b3[:, 18 * TCH:19 * TCH]]
    cht2 = [a_b3[:, 19 * TCH:20 * TCH], a_b3[:, 20 * TCH:21 * TCH]]
    betac = sb("betac", [P, DBLK], BF16)
    _fb = 21 * TCH
    fw2 = [a_b3[:, _fb:_fb + DBLK * P], a_b3[:, _fb + DBLK * P:_fb + 2 * DBLK * P]]
    fu2 = [a_b3[:, _fb + 2 * DBLK * P:_fb + 3 * DBLK * P],
           a_b3[:, _fb + 3 * DBLK * P:_fb + 4 * DBLK * P]]
    _o = _fb + 4 * DBLK * P
    gt2 = [a_b3[:, _o + j * MINE:_o + (j + 1) * MINE] for j in range(4)]
    gt4 = a_b3[:, _o:_o + 4 * MINE]
    pt2 = [a_b3[:, _o + (4 + j) * MINE:_o + (5 + j) * MINE] for j in range(4)]
    pt4 = a_b3[:, _o + 4 * MINE:_o + 8 * MINE]
    mrow4 = a_b3[:, _o + 8 * MINE:_o + 12 * MINE]
    mrow24 = a_b3[:, _o + 12 * MINE:_o + 16 * MINE]
    mrow = a_b3[:, _o + 16 * MINE:_o + 17 * MINE]
    sqrep4 = a_b3[:, _o + 17 * MINE:_o + 21 * MINE]
    sqrrep4 = a_b3[:, _o + 21 * MINE:_o + 25 * MINE]
    qsc4 = a_b3[:, _o + 25 * MINE:_o + 29 * MINE]
    fcm2 = [a_b3[:, _o + (29 + j) * MINE:_o + (30 + j) * MINE] for j in range(4)]
    fc4 = a_b3[:, _o + 29 * MINE:_o + 33 * MINE]
    _w = _o + 33 * MINE
    wdsl2 = [a_b3[:, _w:_w + IBLK * P], a_b3[:, _w + IBLK * P:_w + 2 * IBLK * P]]
    zqb = a_b3[:, _w + 2 * IBLK * P:_w + 2 * IBLK * P + NMT * D]
    AB3 = _w + 2 * IBLK * P + NMT * D
    
    pm0 = psb("pm0", [P, 512])
    pm1 = psb("pm1", [P, 512])
    pm2 = psb("pm2", [P, 512])
    pm3 = psb("pm3", [P, 512])
    ptrp = psb("ptrp", [P, 2 * P], BF16)
    ptr = ptrp[:, 0:P]
    ptr2 = ptrp[:, P:2 * P]
    ptrw = [psb("ptrw0", [P, DBLK * P], BF16), psb("ptrw1", [P, DBLK * P], BF16)]
    pmq = psb("pmq", [P, 512])

    class Sem:
        def __init__(self, name):
            self.h = nc.alloc_semaphore(name=name)
            self.n = 0

        def inc(self, instr, k=1):
            instr.then_inc(self.h, k)
            self.n += k
            return instr

    s_ld, s_g, s_v, s_s, s_t, s_st, s_bar, s_ge = (Sem(x) for x in
        ("ld", "g", "v", "s", "t", "st", "bar", "ge"))
    s_rp = [Sem("rp0"), Sem("rp1"), Sem("rp2")]
    s_x = [Sem("x0"), Sem("x1"), Sem("x2")]
    s_fw = [Sem("fw0"), Sem("fw1")]
    s_dw = [Sem("dw0"), Sem("dw1")]
    s_gx = [Sem(f"gx{i}") for i in range(6)]
    s_gq3 = [Sem(f"gq{i}") for i in range(3)]
    s_gl3 = [Sem(f"gl{i}") for i in range(3)]

    V, S, TE, G, SY = nc.vector, nc.scalar, nc.tensor, nc.gpsimd, nc.sync

    def vf(instr=None):
        if instr is None:
            instr = V.engine_nop()
        s_v.inc(instr)
        V.wait_ge(s_v.h, s_v.n)
        return instr

    def sf(instr=None):
        if instr is None:
            instr = S.copy(statw[0:1, 13:14], consts[0:1, 7:8])
        s_s.inc(instr)
        S.wait_ge(s_s.h, s_s.n)
        return instr
    ALL_E = ("sync", "vector", "scalar", "tensor", "gpsimd")

    def barrier():
        SY.wait_ge(s_ld.h, s_ld.n)
        SY.wait_ge(s_st.h, s_st.n)
        G.wait_ge(s_g.h, s_g.n)
        G.wait_ge(s_ge.h, s_ge.n)
        for e in ALL_E:
            eng = getattr(nc, e)
            if e != "sync":
                eng.drain()
            eng.sem_inc(s_bar.h, 1)
        s_bar.n += len(ALL_E)
        for e in ALL_E:
            getattr(nc, e).wait_ge(s_bar.h, s_bar.n)

    NG = (D + 511) // 512

    def ln_stats(dst2, src):
        V.wait_ge(s_v.h, s_v.n)
        for gi in range(NG):
            lo, hi = gi * 512, min(D, (gi + 1) * 512)
            V.bn_stats(bns[:, 6 * gi:6 * gi + 6], src[:, lo:hi])
            vf()
        return V.bn_aggr(dst2, bns[:, 0:6 * NG])

    def rsqrt_cols(src_ap, dst_ap, tmp_ap):
        S.activation(tmp_ap, src_ap, AF.Ln, bias=LN_EPS)
        sf()
        return S.activation(dst_ap, tmp_ap, AF.Exp, scale=-0.5)

    ptrw_g = [0, 0]
    gp_v = [0, 0, 0, 0]
    q4v = [0]
    fcm_t = [0, 0, 0, 0]
    pm_g = {i: {"s": 0, "v": 0} for i in range(4)}
    xg = {"a_scal": 0, "zn_s": [0, 0], "znT_rd_t": 0, "bc_st": 0,
          "rp_fin": [0, 0], "acc_rd_t": 0, "yam_rd_s": 0, "um_rd_v": 0,
          "z2q_v": 0, "qs_v": 0, "o_st": [0, 0], "zn_v": 0,
          "pmq_s": 0, "sqT_t": 0, "bct_s": 0, "rep_t": 0, "ln4_s": 0}

    def pm_wait(bank):
        TE.wait_ge(s_s.h, pm_g[bank]["s"])
        TE.wait_ge(s_v.h, pm_g[bank]["v"])

    def finish():
        SY.wait_ge(s_ld.h, s_ld.n)
        SY.wait_ge(s_st.h, s_st.n)
        for cm in reversed(ctxs):
            cm.__exit__(None, None, None)
        return nc

    STOP = cfg.get("stop")

    V.memset(consts[:, 6:7], -0.5)
    vf()
    V.memset(consts[:, 7:8], 1.0)
    vf()
    V.memset(consts[:, 8:9], 0.5)
    vf()
    V.memset(consts[:, 9:10], 0.0)
    vf()
    V.memset(consts[:, 10:11], LN_EPS)
    vf()
    s_v.inc(V.memset(consts[:, 11:12], -0.5))
    for val, col in ((-0.5, 6), (1.0, 7), (0.5, 8), (0.0, 9), (LN_EPS, 10)):
        nc.const_aps.register(F32, val, consts[:, col:col + 1])
    s_ld.inc(SY.dma_start(gam[:], gam_h[:]), 16)
    s_ld.inc(SY.dma_start(bet[:], bet_h[:]), 16)
    s_ld.inc(SY.dma_start(apar[:], apar_h[:]), 16)
    s_g.inc(G.dma_start(drep[:], dpar_h.partition_broadcast(P)), 16)
    s_ld.inc(SY.dma_start(parc[:, 0:1], parf_h.partition_broadcast(P)), 16)
    s_ld.inc(SY.dma_start(scr[0:1, D + 2 * N:2 * D + 2 * N], bdt_h[0:1, 0:D]), 16)
    s_g.inc(G.dma_start(wdtg.ap().rearrange("p (c o) -> p c o", c=DBLK),
                        wdtT_h.rearrange("(c p) o -> p c o", p=P)), 16)
    s_g.inc(G.dma_start(wbg.ap().rearrange("p (c n) -> p c n", c=DBLK),
                        wbT_h.rearrange("(c p) n -> p c n", p=P)), 16)
    s_g.inc(G.dma_start(wcg.ap().rearrange("p (c n) -> p c n", c=DBLK),
                        wcT_h.rearrange("(c p) n -> p c n", p=P)), 16)

    G.wait_ge(s_g.h, s_g.n)
    s_ge.inc(G.memset(onesrow[:], 1.0))
    s_ge.inc(G.memset(sel_all[:], 0.0))
    G.wait_ge(s_ge.h, s_ge.n)
    for n_ in range(N):
        s_ge.inc(G.affine_select(out=sel_all[0:N, n_ * P:(n_ + 1) * P],
                                 in_=sel_all[0:N, n_ * P:(n_ + 1) * P],
                                 compare_op=ALU.not_equal, fill=1.0, base=-n_,
                                 pattern=[[0, P]], channel_multiplier=1))
    s_ge.inc(G.memset(ident[:], 0.0))
    G.wait_ge(s_ge.h, s_ge.n)
    s_ge.inc(G.affine_select(out=ident[:], in_=ident[:], compare_op=ALU.not_equal,
                             fill=1.0, base=0, pattern=[[-1, P]], channel_multiplier=1))

    S.wait_ge(s_ld.h, s_ld.n)
    S.wait_ge(s_v.h, s_v.n)
    S.activation(apar[:], apar[:], AF.Exp)
    sf()
    s_s.inc(S.mul(apar[:], apar[:], -1.0))

    V.wait_ge(s_ld.h, s_ld.n)
    V.wait_ge(s_g.h, s_g.n)
    V.tensor_scalar(parc[:, 1:2], parc[:, 0:1], -1.0, 1.0, ALU.mult, ALU.add)
    vf()
    V.tensor_copy(onescol[:], consts[:, 7:8])
    vf()
    s_v.inc(V.tensor_copy(betac[:, 0:DBLK], bet[:]))
    TE.wait_ge(s_v.h, s_v.n)
    TE.wait_ge(s_g.h, s_g.n)

    for oc in range(0, D, 512):
        w = min(512, D - oc)
        for c in range(DBLK):
            mm = TE.matmul(pm0[0:1, 0:w], betac[:, c:c + 1],
                           wdtg[:, c * D + oc:c * D + oc + w],
                           start=(c == 0), stop=(c == DBLK - 1))
            if c == DBLK - 1:
                s_t.inc(mm)
        V.wait_ge(s_t.h, s_t.n)
        s_v.inc(V.tensor_copy(scr[0:1, oc:oc + w], pm0[0:1, 0:w]))
        TE.wait_ge(s_v.h, s_v.n)
        pm_g[0]["v"] = s_v.n
    for wi, wsrc in enumerate((wbg, wcg)):
        for c in range(DBLK):
            mm = TE.matmul(pm1[0:1, 0:N], betac[:, c:c + 1], wsrc[:, c * N:(c + 1) * N],
                           start=(c == 0), stop=(c == DBLK - 1))
            if c == DBLK - 1:
                s_t.inc(mm)
        V.wait_ge(s_t.h, s_t.n)
        s_v.inc(V.tensor_copy(scr[0:1, D + wi * N:D + (wi + 1) * N], pm1[0:1, 0:N]))
        TE.wait_ge(s_v.h, s_v.n)
        pm_g[1]["v"] = s_v.n
    V.wait_ge(s_ld.h, s_ld.n)
    s_v.inc(V.tensor_tensor(scr[0:1, 0:D], scr[0:1, 0:D], scr[0:1, D + 2 * N:2 * D + 2 * N], ALU.add))
    SY.wait_ge(s_v.h, s_v.n)
    s_st.inc(SY.dma_start(brow_d[0:1, :], scr[0:1, 0:D + 2 * N]), 16)
    SY.wait_ge(s_st.h, s_st.n)
    s_ld.inc(SY.dma_start(bdtc.ap().rearrange("p (one c) -> p one c", one=1),
                          brow_d[0:1, 0:D].rearrange("one (c p) -> p one c", p=P), allow_slow_non_contiguous=True), 16)
    s_ld.inc(SY.dma_start(bbc[0:N, 0:2].rearrange("n (one two) -> n one two", one=1),
                          brow_d[0:1, D:D + 2 * N].rearrange("one (two n) -> n one two", n=N), allow_slow_non_contiguous=True), 16)

    V.wait_ge(s_s.h, s_s.n)
    V.wait_ge(s_ld.h, s_ld.n)
    for c in range(DBLK):
        V.tensor_scalar(wdtg[:, c * D:(c + 1) * D], wdtg[:, c * D:(c + 1) * D],
                        gam[:, c:c + 1], None, ALU.mult)
        V.tensor_scalar(wbg[:, c * N:(c + 1) * N], wbg[:, c * N:(c + 1) * N],
                        gam[:, c:c + 1], None, ALU.mult)
        V.tensor_scalar(wcg[:, c * N:(c + 1) * N], wcg[:, c * N:(c + 1) * N],
                        gam[:, c:c + 1], None, ALU.mult)
    barrier()


    if STOP == "prologue":
        return finish()
    wsc2 = [a_f[:, NMT * D + i * WQC:NMT * D + (i + 1) * WQC] for i in range(6)]
    wqa = [a_b3[:, i * WQC:(i + 1) * WQC] for i in range(6)]
    wqb = [a_b3[:, (6 + i) * WQC:(7 + i) * WQC] for i in range(6)]


    def gen_quant(w_h, wq_d, nrows, slot):
        CH_ROWS = WQC
        chunks = list(range(0, nrows, CH_ROWS))
        bufs = (2 * slot, 2 * slot + 1)
        ldsem = (s_gx[2 * slot], s_gx[2 * slot + 1])
        stsem = (s_fw[0], s_fw[1], s_dw[0])[slot]
        pmx = (pm0, pm1, pm2)[slot]
        sw = 4 * slot
        nmm = 0
        for r0 in chunks:
            rr = min(CH_ROWS, nrows - r0)
            nmm += (rr + 511) // 512
        k = 0
        s_done_abs = [0, 0]
        t_done_abs = [0, 0]
        ld_w = [0, 0]
        for ci, r0 in enumerate(chunks):
            b = ci % 2
            p = bufs[b]
            rr = min(CH_ROWS, nrows - r0)
            w = rr
            G.wait_ge(s_s.h, s_done_abs[b])
            ldsem[b].inc(G.dma_start(wsc2[p][:, 0:w].rearrange("p (a q) -> p a q", q=P),
                                     w_h[r0:r0 + rr, :].rearrange("(p a) q -> p a q", p=P)), 16)
            ld_w[b] = ldsem[b].n
            S.wait_ge(ldsem[b].h, ld_w[b])
            S.wait_ge(s_t.h, t_done_abs[b])
            s_s.inc(S.activation(wqa[p][:, 0:w], wsc2[p][:, 0:w], AF.Abs))
            s_done_abs[b] = s_s.n
            TE.wait_ge(s_s.h, s_done_abs[b])
            if ci == 0:
                pm_wait(slot)
            noff = (w + 511) // 512
            for oi, off in enumerate(range(0, w, 512)):
                ww = min(512, w - off)
                mm = TE.matmul(pmx[0:1, 0:ww], onescol[:], wqa[p][:, off:off + ww],
                               start=(k == 0), stop=(k == nmm - 1))
                if oi == noff - 1:
                    s_t.inc(mm)
                k += 1
            t_done_abs[b] = s_t.n
            yield
        V.wait_ge(s_t.h, s_t.n)
        s_v.inc(V.tensor_reduce(statw[0:1, sw:sw + 1], pmx[0:1, 0:512], AX.X, ALU.add))
        pm_g[slot]["v"] = s_v.n
        V.wait_ge(s_v.h, s_v.n)
        V.tensor_scalar(statw[0:1, sw + 1:sw + 2], statw[0:1, sw:sw + 1],
                        1.0 / float(nrows * P), 1e-5, ALU.mult, ALU.max)
        vf()
        V.reciprocal(statw[0:1, sw + 2:sw + 3], statw[0:1, sw + 1:sw + 2])
        vf()
        s_v.inc(V.tensor_scalar(statw[0:1, sw + 3:sw + 4], statw[0:1, sw + 1:sw + 2],
                                0.5, None, ALU.mult))
        G.wait_ge(s_v.h, s_v.n)
        s_gq3[slot].inc(G.dma_start(wqs_d[slot:slot + 1, 0:2], statw[0:1, sw + 2:sw + 4]), 16)
        G.wait_ge(s_gq3[slot].h, s_gq3[slot].n)
        s_gl3[slot].inc(G.dma_start(consts[:, slot:slot + 1],
                                    wqs_d[slot:slot + 1, 0:1].partition_broadcast(P)), 16)
        s_gl3[slot].inc(G.dma_start(consts[:, 3 + slot:4 + slot],
                                    wqs_d[slot:slot + 1, 1:2].partition_broadcast(P)), 16)
        gl_done = s_gl3[slot].n
        yield
        S.wait_ge(s_gl3[slot].h, gl_done)
        s_done_sg = list(s_done_abs)
        v_done_sg = [0, 0]
        st_all = stsem.n
        for ci, r0 in enumerate(chunks):
            b = ci % 2
            p = bufs[b]
            rr = min(CH_ROWS, nrows - r0)
            w = rr
            G.wait_ge(s_s.h, s_done_sg[b])
            ldsem[b].inc(G.dma_start(wsc2[p][:, 0:w].rearrange("p (a q) -> p a q", q=P),
                                     w_h[r0:r0 + rr, :].rearrange("(p a) q -> p a q", p=P)), 16)
            ld_w[b] = ldsem[b].n
            S.wait_ge(ldsem[b].h, ld_w[b])
            S.wait_ge(s_v.h, v_done_sg[b])
            S.wait_ge(stsem.h, st_all)
            S.activation(wqa[p][:, 0:w], wsc2[p][:, 0:w], AF.Sign,
                         scale=consts[:, slot:slot + 1], bias=consts[:, 6:7])
            s_s.inc(S.activation(wqb[p][:, 0:w], wsc2[p][:, 0:w], AF.Sign,
                                 scale=consts[:, slot:slot + 1], bias=consts[:, 8:9]))
            s_done_sg[b] = s_s.n
            V.wait_ge(s_s.h, s_done_sg[b])
            s_v.inc(V.tensor_tensor(wqa[p][:, 0:w], wqa[p][:, 0:w], wqb[p][:, 0:w],
                                    ALU.add))
            v_done_sg[b] = s_v.n
            SY.wait_ge(s_v.h, v_done_sg[b])
            stsem.inc(SY.dma_start(wq_d[r0:r0 + rr, :].rearrange("(p a) q -> p a q", p=P),
                                   wqa[p][:, 0:w].rearrange("p (a q) -> p a q", q=P)), 16)
            if ci >= 1:
                st_all = stsem.n - 16
            yield
        SY.wait_ge(stsem.h, stsem.n)

    qgens = [gen_quant(wgT_h, wgq_d, INNER // P * D, 0),
             gen_quant(wuT_h, wuq_d, INNER // P * D, 1),
             gen_quant(wdT_h, wdq_d, D // P * INNER, 2)]
    live = list(qgens)
    while live:
        for g in list(live):
            try:
                next(g)
            except StopIteration:
                live.remove(g)
    V.wait_ge(s_gl3[0].h, s_gl3[0].n)
    s_v.inc(V.tensor_scalar(consts[:, 12:13], consts[:, 3:4], 0.5, None, ALU.mult))
    barrier()

    if STOP == "quant":
        return finish()
    s_v.inc(V.memset(state[:], 0.0))
    vf()

    def gen_A(sidx):
        t0 = sidx * TCH
        kt0 = t0 // P
        v_done_x = [0, 0]
        ld_x = [0, 0]
        V.wait_ge(s_s.h, xg["a_scal"])
        a_st = 0
        for k in range(NT):
            p = k % 2
            if k < 2:
                SY.wait_ge(s_s.h, xg["zn_s"][0])
                SY.wait_ge(s_s.h, xg["zn_s"][1])
            s_x[p].inc(SY.dma_start(xts[:, k * D:(k + 1) * D],
                                    x_h[t0 + k * P:t0 + (k + 1) * P, :]), 16)
            ld_x[p] = s_x[p].n
            V.wait_ge(s_x[p].h, ld_x[p])
            s_v.inc(ln_stats(stats[:, 2 * k:2 * k + 2], xts[:, k * D:(k + 1) * D]))
            v_done_x[p] = s_v.n
            a_st = s_v.n
            yield
        mview = stats.ap().rearrange("p (k two) -> p two k", two=2)
        mean_v = mview[:, 0, :]
        var_v = mview[:, 1, :]
        S.wait_ge(s_v.h, a_st)
        s_s.inc(rsqrt_cols(var_v, sc1[:, 0:NT], statw[:, 0:NT]))
        a_rs = s_s.n
        yield
        V.wait_ge(s_s.h, a_rs)
        V.tensor_copy(r1v[:, 0:NT], sc1[:, 0:NT])
        vf()
        V.tensor_tensor(statw[:, 0:NT], sc1[:, 0:NT], sc1[:, 0:NT], ALU.mult)
        vf()
        s_v.inc(V.tensor_tensor(statw[:, 0:NT], statw[:, 0:NT], var_v, ALU.mult))
        S.wait_ge(s_v.h, s_v.n)
        s_s.inc(rsqrt_cols(statw[:, 0:NT], statw[:, NT:2 * NT], statw[:, 0:NT]))
        a_rs2 = s_s.n
        yield
        V.wait_ge(s_s.h, a_rs2)
        V.tensor_tensor(sc1[:, 0:NT], sc1[:, 0:NT], statw[:, NT:2 * NT], ALU.mult)
        vf()
        V.tensor_tensor(sc2[:, 0:NT], mean_v, sc1[:, 0:NT], ALU.mult)
        vf()
        vf()
        V.tensor_scalar(sc2[:, 0:NT], sc2[:, 0:NT], -1.0, None, ALU.mult)
        vf()
        mev = stats.ap().rearrange("p (k par mv) -> p par mv k", par=2, mv=2)
        for (dst, ev, od) in ((mnM2[sidx % 2], mev[:, 0, 0, :], mev[:, 1, 0, :]),
                              (r1M2[sidx % 2], r1v.ap().rearrange("p (k two) -> p two k", two=2)[:, 0, :],
                               r1v.ap().rearrange("p (k two) -> p two k", two=2)[:, 1, :])):
            V.tensor_scalar(statw[:, 0:NMT], od, parc[:, 0:1], None, ALU.mult)
            vf()
            V.scalar_tensor_tensor(dst[:, 0:NMT], ev, parc[:, 1:2],
                                   statw[:, 0:NMT], ALU.mult, ALU.add)
            vf()
        s_v.inc(V.engine_nop())
        S.wait_ge(s_v.h, s_v.n)
        yield
        s_done_zn = [0, 0]
        t_done_zn = [0, 0]
        V.wait_ge(s_t.h, xg["znT_rd_t"])
        znT_v = znT.rearrange("p (c t) -> p c t", c=DBLK)
        for k in range(NT):
            p = k % 2
            S.wait_ge(s_t.h, t_done_zn[p])
            s_s.inc(S.activation(znt2[p][:], xts[:, k * D:(k + 1) * D], AF.Identity,
                                 scale=sc1[:, k:k + 1], bias=sc2[:, k:k + 1]))
            s_done_zn[p] = s_s.n
            TE.wait_ge(s_s.h, s_done_zn[p])
            TE.wait_ge(s_v.h, ptrw_g[p])
            for c in range(DBLK):
                mm = TE.transpose(ptrw[p][:, c * P:(c + 1) * P],
                                  znt2[p][:, c * P:(c + 1) * P], ident[:])
                if c == DBLK - 1:
                    s_t.inc(mm)
            t_done_zn[p] = s_t.n
            V.wait_ge(s_t.h, t_done_zn[p])
            s_v.inc(V.tensor_copy(znT_v[:, :, k * P:(k + 1) * P],
                                  ptrw[p][:, 0:DBLK * P].rearrange("p (c q) -> p c q", c=DBLK)))
            ptrw_g[p] = s_v.n
            yield
        xg["a_scal"] = s_s.n
        xg["zn_s"] = list(s_done_zn)
        xg["zn_v"] = s_v.n

    def gen_B(sidx):
        t0 = sidx * TCH
        kt0 = t0 // P
        t_done_dt = [0, 0]
        s_done_dt = [0, 0]
        NTC = TCH // 512
        fscr2 = bscr2
        TE.wait_ge(s_v.h, xg["zn_v"])
        for c in range(DBLK):
            for ti, tc in enumerate(range(0, TCH, 512)):
                w = min(512, TCH - tc)
                g = (c * NTC + ti) % 2
                pmx = pm0 if g == 0 else pm1
                pm_wait(g)
                for kk in range(DBLK):
                    mm = TE.matmul(pmx[:, 0:w],
                                   wdtg[:, kk * D + c * P:kk * D + (c + 1) * P],
                                   znT[:, kk * TCH + tc:kk * TCH + tc + w],
                                   start=(kk == 0), stop=(kk == DBLK - 1))
                    if kk == DBLK - 1:
                        s_t.inc(mm)
                t_done_dt[g] = s_t.n
                S.wait_ge(s_t.h, t_done_dt[g])
                S.activation(fscr2[g][:, 0:w], pmx[:, 0:w], AF.Exp,
                             bias=bdtc[:, c:c + 1], scale=consts[:, 7:8])
                sf()
                s_s.inc(S.activation(dtT[:, c * TCH + tc:c * TCH + tc + w],
                                     fscr2[g][:, 0:w], AF.Ln, bias=1.0))
                s_done_dt[g] = s_s.n
                pm_g[g]["s"] = s_s.n
                yield
        s_done_bc = 0
        for wi, wsrc in enumerate((wbg, wcg)):
            for ti, tc in enumerate(range(0, TCH, 512)):
                w = min(512, TCH - tc)
                pm_wait(2)
                for kk in range(DBLK):
                    mm = TE.matmul(pm2[0:N, 0:w], wsrc[:, kk * N:(kk + 1) * N],
                                   znT[:, kk * TCH + tc:kk * TCH + tc + w],
                                   start=(kk == 0), stop=(kk == DBLK - 1))
                    if kk == DBLK - 1:
                        s_t.inc(mm)
                S.wait_ge(s_t.h, s_t.n)
                S.wait_ge(s_t.h, xg["rep_t"])
                s_s.inc(S.activation(bcT_all[0:N, wi * TCH + tc:wi * TCH + tc + w],
                                     pm2[0:N, 0:w], AF.Identity,
                                     bias=bbc[0:N, wi:wi + 1], scale=consts[0:N, 7:8]))
                s_done_bc = s_s.n
                pm_g[2]["s"] = s_s.n
                yield
        xg["bct_s"] = s_done_bc
        V.wait_ge(s_s.h, s_done_dt[0])
        V.wait_ge(s_s.h, s_done_dt[1])
        for c in range(DBLK):
            V.tensor_scalar(wpT[:, c * TCH:(c + 1) * TCH], znT[:, c * TCH:(c + 1) * TCH],
                            gam[:, c:c + 1], bet[:, c:c + 1], ALU.mult, ALU.add)
            vf()
            s_v.inc(V.tensor_tensor(wpT[:, c * TCH:(c + 1) * TCH],
                                    wpT[:, c * TCH:(c + 1) * TCH],
                                    dtT[:, c * TCH:(c + 1) * TCH], ALU.mult))
            yield
        xg["znT_rd_t"] = s_t.n
        xg["bc_st"] = s_st.n

    def gen_C(sidx):
        t0 = sidx * TCH
        kt0 = t0 // P
        v_done_rep = [0] * 5
        v_done_ab = [0] * 5
        s_done_ab = [0] * 5
        v_done_ht = [0, 0]
        g_done_ht = [0, 0]
        s_done_rep = [0] * 5
        V.wait_ge(s_t.h, xg["acc_rd_t"])
        V.wait_ge(s_v.h, s_v.n)
        for n in range(N + 1):
            p = n % 5
            if n < N:
                TE.wait_ge(s_s.h, xg["pmq_s"])
                TE.wait_ge(s_s.h, xg["bct_s"])
                s_t.inc(TE.matmul(pmq[:, 0:TCH], sel_all[0:N, n * P:(n + 1) * P],
                                  bcT_all[0:N, 0:TCH], start=True, stop=True))
                S.wait_ge(s_t.h, s_t.n)
                S.wait_ge(s_v.h, v_done_rep[p])
                s_s.inc(S.activation(brep2[p][:, 0:TCH], pmq[:, 0:TCH], AF.Identity,
                                     scale=consts[:, 7:8], bias=consts[:, 9:10]))
                xg["pmq_s"] = s_s.n
                TE.wait_ge(s_s.h, xg["pmq_s"])
                s_t.inc(TE.matmul(pmq[:, 0:TCH], sel_all[0:N, n * P:(n + 1) * P],
                                  bcT_all[0:N, TCH:2 * TCH], start=True, stop=True))
                xg["rep_t"] = s_t.n
                S.wait_ge(s_t.h, s_t.n)
                s_s.inc(S.activation(crep2[p][:, 0:TCH], pmq[:, 0:TCH], AF.Identity,
                                     scale=consts[:, 7:8], bias=consts[:, 9:10]))
                xg["pmq_s"] = s_s.n
                s_done_rep[p] = s_s.n
            if n == 0:
                continue
            nn = n - 1
            pp = nn % 5
            for c0 in range(0, DBLK, 2):
                cpair = (c0, c0 + 1)
                for c in cpair:
                    ac = c % 5
                    S.wait_ge(s_v.h, v_done_ab[ac])
                    s_s.inc(S.activation(abar2[ac][:, 0:TCH],
                                         dtT[:, c * TCH:(c + 1) * TCH], AF.Exp,
                                         scale=apar[:, c * N + nn:c * N + nn + 1],
                                         bias=consts[:, 9:10]))
                    s_done_ab[ac] = s_s.n
                V.wait_ge(s_s.h, s_done_rep[pp])
                V.wait_ge(s_s.h, s_done_ab[cpair[1] % 5])
                V.wait_ge(s_ge.h, g_done_ht[0])
                V.wait_ge(s_ge.h, g_done_ht[1])
                V.wait_ge(s_v.h, v_done_ht[0])
                V.wait_ge(s_v.h, v_done_ht[1])
                bx_n = [0, 0]
                sc_n = [0, 0]
                ch_n = [0, 0]
                for c in cpair:
                    pc = c % 2
                    s_v.inc(V.tensor_tensor(bxt2[pc][:, 0:TCH],
                                            wpT[:, c * TCH:(c + 1) * TCH],
                                            brep2[pp][:, 0:TCH], ALU.mult))
                    bx_n[pc] = s_v.n
                for c in cpair:
                    pc = c % 2
                    ac = c % 5
                    V.wait_ge(s_v.h, bx_n[pc])
                    s_v.inc(V.tensor_tensor_scan(ht2[pc][:, 0:TCH],
                                                 abar2[ac][:, 0:TCH],
                                                 bxt2[pc][:, 0:TCH],
                                                 state[:, c * N + nn:c * N + nn + 1],
                                                 ALU.mult, ALU.add))
                    sc_n[pc] = s_v.n
                    v_done_ab[ac] = s_v.n
                for c in cpair:
                    pc = c % 2
                    V.wait_ge(s_v.h, sc_n[pc])
                    s_v.inc(V.tensor_tensor(cht2[pc][:, 0:TCH], ht2[pc][:, 0:TCH],
                                            crep2[pp][:, 0:TCH], ALU.mult))
                    ch_n[pc] = s_v.n
                for c in cpair:
                    pc = c % 2
                    V.wait_ge(s_v.h, ch_n[pc])
                    if nn == 0:
                        s_v.inc(V.tensor_tensor(accs[:, c * TCH:(c + 1) * TCH],
                                                ht2[pc][:, 0:TCH], crep2[pp][:, 0:TCH],
                                                ALU.mult))
                    else:
                        s_v.inc(V.tensor_tensor(accs[:, c * TCH:(c + 1) * TCH],
                                                accs[:, c * TCH:(c + 1) * TCH],
                                                cht2[pc][:, 0:TCH], ALU.add))
                    v_done_ht[pc] = s_v.n
                    G.wait_ge(s_v.h, v_done_ht[pc])
                    s_ge.inc(G.tensor_copy(state[:, c * N + nn:c * N + nn + 1],
                                           ht2[pc][:, TCH - 1:TCH]))
                    g_done_ht[pc] = s_ge.n
                yield
            v_done_rep[pp] = s_v.n
        V.wait_ge(s_ge.h, s_ge.n)
        V.wait_ge(s_s.h, xg["yam_rd_s"])
        for c in range(DBLK):
            p = c % 2
            TE.wait_ge(s_v.h, max(ptrw_g[p], s_v.n if c < 2 else 0))
            for k in range(NT):
                mm = TE.transpose(ptrw[p][:, k * P:(k + 1) * P],
                                  accs[:, c * TCH + k * P:c * TCH + (k + 1) * P],
                                  ident[:])
                if k == NT - 1:
                    s_t.inc(mm)
            V.wait_ge(s_t.h, s_t.n)
            pv = ptrw[p][:, 0:NT * P].rearrange("p (k two q) -> p k two q", two=2, q=P)
            vf(V.tensor_scalar(ytd[:, 0:MINE].rearrange("p (k q) -> p k q", q=P),
                               pv[:, :, 1, :], parc[:, 0:1], None, ALU.mult))
            yv = yam.rearrange("p (k d) -> p k d", k=NMT)
            s_v.inc(V.scalar_tensor_tensor(yv[:, :, c * P:(c + 1) * P],
                                           pv[:, :, 0, :], parc[:, 1:2],
                                           ytd[:, 0:MINE].rearrange("p (k q) -> p k q", q=P),
                                           ALU.mult, ALU.add))
            ptrw_g[p] = s_v.n
            yield
        xg["acc_rd_t"] = s_t.n

    def gen_C2(sidx):
        t0 = sidx * TCH
        kt0 = t0 // P
        V.wait_ge(s_s.h, xg["ln4_s"])
        SY.wait_ge(s_v.h, xg["um_rd_v"])
        for k in range(NMT):
            s_ld.inc(SY.dma_start(um[:, k * D:(k + 1) * D],
                                  xm_h[sidx * MINE + k * P:sidx * MINE + (k + 1) * P, :]), 16)
        yield
        V.wait_ge(s_t.h, s_t.n)
        mnM = mnM2[sidx % 2]
        r1M = r1M2[sidx % 2]
        vf(V.tensor_tensor(statw2[:, 0:NMT], mnM[:, 0:NMT], r1M[:, 0:NMT], ALU.mult))
        s_v.inc(V.tensor_scalar(statw2[:, 0:NMT], statw2[:, 0:NMT], -1.0, None, ALU.mult))
        S.wait_ge(s_v.h, s_v.n)
        S.wait_ge(s_ld.h, s_ld.n)
        for k in range(NMT):
            mm = S.activation(z1m[:, k * D:(k + 1) * D], um[:, k * D:(k + 1) * D],
                              AF.Identity, scale=r1M[:, k:k + 1], bias=statw2[:, k:k + 1])
            if k == NMT - 1:
                s_s.inc(mm)
        c2_z1 = s_s.n
        yield
        V.wait_ge(s_s.h, c2_z1)
        for k in range(NMT):
            V.tensor_tensor(z1m[:, k * D:(k + 1) * D], z1m[:, k * D:(k + 1) * D],
                            drep[:], ALU.mult)
        vf()
        for k in range(NMT):
            V.tensor_tensor(yam[:, k * D:(k + 1) * D], yam[:, k * D:(k + 1) * D],
                            z1m[:, k * D:(k + 1) * D], ALU.add)
        vf()
        c2_st = 0
        for k in range(NMT):
            s_v.inc(ln_stats(ln3s[:, 2 * k:2 * k + 2], yam[:, k * D:(k + 1) * D]))
            c2_st = s_v.n
            yield
        S.wait_ge(s_v.h, c2_st)
        l3v = ln3s.ap().rearrange("p (k two) -> p two k", two=2)
        s_s.inc(rsqrt_cols(l3v[:, 1, :], sxv[:, 0:NMT], statw2[:, 0:NMT]))
        V.wait_ge(s_s.h, s_s.n)
        vf(V.tensor_tensor(statw2[:, 0:NMT], l3v[:, 0, :], sxv[:, 0:NMT], ALU.mult))
        s_v.inc(V.tensor_scalar(statw2[:, 0:NMT], statw2[:, 0:NMT], -1.0, None, ALU.mult))
        S.wait_ge(s_v.h, s_v.n)
        for k in range(NMT):
            mm = S.activation(fTM[:, k * D:(k + 1) * D], yam[:, k * D:(k + 1) * D],
                              AF.Identity, scale=sxv[:, k:k + 1], bias=statw2[:, k:k + 1])
            if k == NMT - 1:
                s_s.inc(mm)
        xg["yam_rd_s"] = s_s.n
        c2_f = s_s.n
        yield
        V.wait_ge(s_s.h, c2_f)
        for k in range(NMT):
            mm = V.tensor_tensor(um[:, k * D:(k + 1) * D], um[:, k * D:(k + 1) * D],
                                 fTM[:, k * D:(k + 1) * D], ALU.add)
            if k == NMT - 1:
                s_v.inc(mm)
        S.wait_ge(s_v.h, s_v.n)
        SY.wait_ge(s_v.h, s_v.n)

    def gen_D_head(sidx):
        t0 = sidx * TCH
        kt0 = t0 // P
        zq = zqb
        V.wait_ge(s_s.h, xg["ln4_s"])
        dh_st = 0
        for k in range(NMT):
            s_v.inc(ln_stats(ln2s[:, 2 * k:2 * k + 2], um[:, k * D:(k + 1) * D]))
            dh_st = s_v.n
            yield
        l2v = ln2s.ap().rearrange("p (k two) -> p two k", two=2)
        S.wait_ge(s_v.h, dh_st)
        s_s.inc(rsqrt_cols(l2v[:, 1, :], sxv[:, 0:NMT], statw2[:, 0:NMT]))
        V.wait_ge(s_s.h, s_s.n)
        vf(V.tensor_tensor(statw2[:, 0:NMT], l2v[:, 0, :], sxv[:, 0:NMT], ALU.mult))
        s_v.inc(V.tensor_scalar(statw2[:, 0:NMT], statw2[:, 0:NMT], -1.0, None, ALU.mult))
        S.wait_ge(s_v.h, s_v.n)
        for k in range(NMT):
            mm = S.activation(fTM[:, k * D:(k + 1) * D], um[:, k * D:(k + 1) * D],
                              AF.Identity, scale=sxv[:, k:k + 1], bias=statw2[:, k:k + 1])
            if k == NMT - 1:
                s_s.inc(mm)
        dh_z2 = s_s.n
        yield
        V.wait_ge(s_s.h, dh_z2)
        for k in range(NMT):
            V.tensor_reduce(statw2[:, k:k + 1], fTM[:, k * D:(k + 1) * D], AX.X,
                            ALU.max, apply_absolute_value=True)
        vf()
        yield
        V.tensor_scalar(statw2[:, 8:8 + NMT], statw2[:, 0:NMT], 1e-5, 1.0 / 127.0,
                        ALU.max, ALU.mult)
        vf()
        V.reciprocal(statw2[:, 0:NMT], statw2[:, 8:8 + NMT])
        vf()
        for k in range(NMT):
            V.tensor_scalar(fTM[:, k * D:(k + 1) * D], fTM[:, k * D:(k + 1) * D],
                            statw2[:, k:k + 1], MAGIC, ALU.mult, ALU.add)
        vf()
        for k in range(NMT):
            mm = V.tensor_scalar(zq[:, k * D:(k + 1) * D], fTM[:, k * D:(k + 1) * D],
                                 -MAGIC, statw2[:, 8 + k:9 + k], ALU.add, ALU.mult)
            if k == NMT - 1:
                s_v.inc(mm)
        TE.wait_ge(s_v.h, s_v.n)
        z2qT_v = z2qT.rearrange("p (c t) -> p c t", c=DBLK)
        for k in range(NMT):
            p = k % 2
            TE.wait_ge(s_v.h, ptrw_g[p])
            for c in range(DBLK):
                mm = TE.transpose(ptrw[p][:, c * P:(c + 1) * P],
                                  zq[:, k * D + c * P:k * D + (c + 1) * P], ident[:])
                if c == DBLK - 1:
                    s_t.inc(mm)
            V.wait_ge(s_t.h, s_t.n)
            s_v.inc(V.tensor_copy(z2qT_v[:, :, k * P:(k + 1) * P],
                                  ptrw[p][:, 0:DBLK * P].rearrange("p (c q) -> p c q", c=DBLK)))
            ptrw_g[p] = s_v.n
            yield
        xg["z2q_v"] = s_v.n
        TE.wait_ge(s_v.h, xg["z2q_v"])

    def gen_D(sidx):
        zq = zqb
        t_done_fw = [0, 0]
        s_done_gu = [0, 0, 0, 0]
        v_done_gp = gp_v
        ld_fw = [0, 0]

        def gu_load(ic):
            p = ic % 2
            SY.wait_ge(s_t.h, t_done_fw[p])
            s_fw[p].inc(SY.dma_start(fw2[p][:, 0:DBLK * P].rearrange("p (kk o) -> p kk o", kk=DBLK),
                                     wgq_d[ic * D:(ic + 1) * D, :].rearrange(
                                         "(p kk) o -> p kk o", p=P)), 16)
            s_fw[p].inc(SY.dma_start(fu2[p][:, 0:DBLK * P].rearrange("p (kk o) -> p kk o", kk=DBLK),
                                     wuq_d[ic * D:(ic + 1) * D, :].rearrange(
                                         "(p kk) o -> p kk o", p=P)), 16)
            ld_fw[p] = s_fw[p].n

        gu_load(0)
        for ic in range(IBLK):
            p = ic % 2
            q = ic % 4
            if ic + 1 < IBLK:
                gu_load(ic + 1)
            TE.wait_ge(s_fw[p].h, ld_fw[p])
            pmg = pm0 if p == 0 else pm2
            pmu = pm1 if p == 0 else pm3
            for tc in range(0, MINE, 512):
                w = min(512, MINE - tc)
                pm_wait(0 if p == 0 else 2)
                pm_wait(1 if p == 0 else 3)
                for kk in range(DBLK):
                    mm = TE.matmul(pmg[:, tc % 512:tc % 512 + w], fw2[p][:, kk * P:(kk + 1) * P],
                                   z2qT[:, kk * MINE + tc:kk * MINE + tc + w],
                                   start=(kk == 0), stop=(kk == DBLK - 1))
                    if kk == DBLK - 1:
                        s_t.inc(mm)
                for kk in range(DBLK):
                    mm = TE.matmul(pmu[:, tc % 512:tc % 512 + w], fu2[p][:, kk * P:(kk + 1) * P],
                                   z2qT[:, kk * MINE + tc:kk * MINE + tc + w],
                                   start=(kk == 0), stop=(kk == DBLK - 1))
                    if kk == DBLK - 1:
                        s_t.inc(mm)
                t_done_fw[p] = s_t.n
                S.wait_ge(s_t.h, t_done_fw[p])
                S.wait_ge(s_v.h, v_done_gp[q])
                s_s.inc(S.activation(gt2[q][:, tc:tc + w], pmg[:, tc % 512:tc % 512 + w],
                                     AF.Identity, scale=consts[:, 12:13],
                                     bias=consts[:, 9:10]))
                s_s.inc(S.activation(pt2[q][:, tc:tc + w], pmu[:, tc % 512:tc % 512 + w],
                                     AF.Identity, scale=consts[:, 4:5], bias=consts[:, 9:10]))
                s_done_gu[q] = s_s.n
                pm_g[0 if p == 0 else 2]["s"] = s_s.n
                pm_g[1 if p == 0 else 3]["s"] = s_s.n
            yield
            if q == 3:
                S.wait_ge(s_v.h, q4v[0])
                S.wait_ge(s_s.h, s_done_gu[q])
                s_s.inc(S.activation(sqrep4[:], gt4[:], AF.Tanh))
                V.wait_ge(s_s.h, s_s.n)
                vf(V.scalar_tensor_tensor(gt4[:], sqrep4[:], 1.0, gt4[:],
                                          ALU.add, ALU.mult))
                q4v[0] = s_v.n
                vf(V.tensor_tensor(qs[:, (ic - 3) * MINE:(ic + 1) * MINE], gt4[:],
                                   pt4[:], ALU.mult))
                for j in range(4):
                    v_done_gp[j] = s_v.n
                if ic == 3:
                    V.tensor_copy(mrow4[:], qs[:, 0:4 * MINE])
                    s_v.inc(V.tensor_copy(mrow24[:], qs[:, 0:4 * MINE]))
                else:
                    V.tensor_tensor(mrow4[:], mrow4[:],
                                    qs[:, (ic - 3) * MINE:(ic + 1) * MINE], ALU.max)
                    s_v.inc(V.tensor_tensor(mrow24[:], mrow24[:],
                                            qs[:, (ic - 3) * MINE:(ic + 1) * MINE],
                                            ALU.min))
                vf()
            yield
        vf(V.scalar_tensor_tensor(mrow4[:], mrow24[:], -1.0, mrow4[:],
                                  ALU.mult, ALU.max))
        vf(V.tensor_tensor(mrow4[:, 0:2 * MINE], mrow4[:, 0:2 * MINE],
                           mrow4[:, 2 * MINE:4 * MINE], ALU.max))
        vf(V.tensor_tensor(mrow[:, 0:MINE], mrow4[:, 0:MINE],
                           mrow4[:, MINE:2 * MINE], ALU.max))
        TE.wait_ge(s_v.h, s_v.n)
        for k in range(NMT):
            s_t.inc(TE.transpose(ptr[:, 0:P], mrow[:, k * P:(k + 1) * P], ident[:]))
            V.wait_ge(s_t.h, s_t.n)
            V.tensor_reduce(sqv[:, k:k + 1], ptr[:, 0:P], AX.X, ALU.max,
                            apply_absolute_value=True)
            vf()
            s_v.inc(V.tensor_copy(statw2[:, 15:16], sqv[:, k:k + 1]))
            TE.wait_ge(s_v.h, s_v.n)
            yield
        V.tensor_scalar(sqvr[:, 0:NMT], sqv[:, 0:NMT], 1e-5, 1.0 / 127.0,
                        ALU.max, ALU.mult)
        vf()
        V.reciprocal(sqv[:, 0:NMT], sqvr[:, 0:NMT])
        vf()
        V.wait_ge(s_t.h, xg["sqT_t"])
        V.tensor_copy(sqstg[:, 0:NMT], sqv[:, 0:NMT])
        vf()
        s_v.inc(V.tensor_copy(sqstg[:, NMT:2 * NMT], sqvr[:, 0:NMT]))
        TE.wait_ge(s_v.h, s_v.n)
        TE.wait_ge(s_v.h, ptrw_g[0])
        for j in range(2 * NMT):
            mm = TE.transpose(ptrw[0][0:1, j * P:(j + 1) * P], sqstg[:, j:j + 1], ident[:])
            if j == 2 * NMT - 1:
                s_t.inc(mm)
        xg["sqT_t"] = s_t.n
        S.wait_ge(s_t.h, s_t.n)
        s_s.inc(S.activation(sqTrow[0:1, 0:2 * NMT * P], ptrw[0][0:1, 0:2 * NMT * P],
                             AF.Identity, scale=consts[0:1, 7:8], bias=consts[0:1, 9:10]))
        V.wait_ge(s_s.h, s_s.n)
        s_v.inc(V.engine_nop())
        ptrw_g[0] = s_v.n
        TE.wait_ge(s_s.h, s_s.n)
        TE.wait_ge(s_s.h, xg["pmq_s"])
        for k in range(NMT):
            s_t.inc(TE.matmul(pmq[:, k * P:(k + 1) * P], onesrow[0:1, 0:P],
                              sqTrow[0:1, k * P:(k + 1) * P], start=True, stop=True))
            s_t.inc(TE.matmul(pmq[:, MINE + k * P:MINE + (k + 1) * P], onesrow[0:1, 0:P],
                              sqTrow[0:1, (NMT + k) * P:(NMT + k + 1) * P],
                              start=True, stop=True))
        S.wait_ge(s_t.h, s_t.n)
        S.wait_ge(s_v.h, xg["qs_v"])
        S.wait_ge(s_v.h, q4v[0])
        RB = 4 if 4 * MINE <= NMT * D else 2
        for rep in range(RB):
            s_s.inc(S.activation(sqrep4[:, rep * MINE:(rep + 1) * MINE],
                                 pmq[:, 0:MINE], AF.Identity,
                                 scale=consts[:, 7:8], bias=consts[:, 9:10]))
            s_s.inc(S.activation(sqrrep4[:, rep * MINE:(rep + 1) * MINE],
                                 pmq[:, MINE:2 * MINE], AF.Identity,
                                 scale=consts[:, 7:8], bias=consts[:, 9:10]))
        xg["pmq_s"] = s_s.n
        yield
        V.wait_ge(s_s.h, xg["pmq_s"])
        RW = RB * MINE
        for jj in range(IBLK // RB):
            sl = qs[:, jj * RW:(jj + 1) * RW]
            V.tensor_tensor(fTM[:, 0:RW], sl, sqrep4[:, 0:RW], ALU.mult)
            vf()
            V.tensor_scalar(fTM[:, 0:RW], fTM[:, 0:RW], MAGIC, -MAGIC,
                            ALU.add, ALU.add)
            vf()
            V.tensor_copy(qsc4[:, 0:RW], fTM[:, 0:RW])
            vf()
            s_v.inc(V.tensor_tensor(sl, qsc4[:, 0:RW], sqrrep4[:, 0:RW], ALU.mult))
            yield
        xg["qs_v"] = s_v.n
        TE.wait_ge(s_v.h, xg["qs_v"])
        t_done_wd = [0, 0]
        s_done_dn = [0, 0]
        v_done_fc = [0, 0]
        ld_wd = [0, 0]

        def dn_load(oc):
            p = oc % 2
            SY.wait_ge(s_t.h, t_done_wd[p])
            s_dw[p].inc(SY.dma_start(wdsl2[p][:, 0:IBLK * P].rearrange("p (kk o) -> p kk o", kk=IBLK),
                                     wdq_d[oc * INNER:(oc + 1) * INNER, :].rearrange(
                                         "(p kk) o -> p kk o", p=P)), 16)
            ld_wd[p] = s_dw[p].n

        dn_load(0)
        for oc in range(DBLK):
            p = oc % 2
            fq = oc % 4
            if oc + 1 < DBLK:
                dn_load(oc + 1)
            TE.wait_ge(s_dw[p].h, ld_wd[p])
            pmx = pm0 if p == 0 else pm1
            for tc in range(0, MINE, 512):
                w = min(512, MINE - tc)
                pm_wait(p)
                for kk in range(IBLK):
                    mm = TE.matmul(pmx[:, 0:w], wdsl2[p][:, kk * P:(kk + 1) * P],
                                   qs[:, kk * MINE + tc:kk * MINE + tc + w],
                                   start=(kk == 0), stop=(kk == IBLK - 1))
                    if kk == IBLK - 1:
                        s_t.inc(mm)
                t_done_wd[p] = s_t.n
                S.wait_ge(s_t.h, t_done_wd[p])
                S.wait_ge(s_v.h, gp_v[fq])
                S.wait_ge(s_t.h, fcm_t[fq])
                s_s.inc(S.activation(fcm2[fq][:, tc:tc + w], pmx[:, 0:w], AF.Identity,
                                     scale=consts[:, 5:6], bias=consts[:, 9:10]))
                s_done_dn[p] = s_s.n
                pm_g[p]["s"] = s_s.n
            yield
            TE.wait_ge(s_s.h, s_done_dn[p])
            TE.wait_ge(s_v.h, ptrw_g[p])
            ptw = ptrw[p]
            for k in range(NMT):
                mm = TE.transpose(ptw[:, k * P:(k + 1) * P], fcm2[fq][:, k * P:(k + 1) * P],
                                  ident[:])
                if k == NMT - 1:
                    s_t.inc(mm)
            t_done_wd[p] = s_t.n
            fcm_t[fq] = s_t.n
            V.wait_ge(s_t.h, t_done_wd[p])
            fTM_v = fTM.rearrange("p (k d) -> p k d", k=NMT)
            s_v.inc(V.tensor_copy(fTM_v[:, :, oc * P:(oc + 1) * P],
                                  ptw[:, 0:NMT * P].rearrange("p (k q) -> p k q", k=NMT)))
            ptrw_g[p] = s_v.n
            yield
        vf()
        for k in range(NMT):
            s_v.inc(ln_stats(ln2s[:, 2 * k:2 * k + 2], fTM[:, k * D:(k + 1) * D]))
        S.wait_ge(s_v.h, s_v.n)
        l4v = ln2s.ap().rearrange("p (k two) -> p two k", two=2)
        s_s.inc(rsqrt_cols(l4v[:, 1, :], sxv[:, 0:NMT], statw2[:, 0:NMT]))
        V.wait_ge(s_s.h, s_s.n)
        vf(V.tensor_tensor(statw2[:, 0:NMT], l4v[:, 0, :], sxv[:, 0:NMT], ALU.mult))
        s_v.inc(V.tensor_scalar(statw2[:, 0:NMT], statw2[:, 0:NMT], -1.0, None, ALU.mult))
        S.wait_ge(s_v.h, s_v.n)
        for k in range(NMT):
            mm = S.activation(zq[:, k * D:(k + 1) * D], fTM[:, k * D:(k + 1) * D],
                              AF.Identity, scale=sxv[:, k:k + 1], bias=statw2[:, k:k + 1])
            if k == NMT - 1:
                s_s.inc(mm)
        xg["ln4_s"] = s_s.n
        V.wait_ge(s_s.h, s_s.n)
        st_done_o = list(xg["o_st"])
        osc2 = [oscr, fscr]
        for k in range(NMT):
            p = k % 2
            V.wait_ge(s_st.h, st_done_o[p])
            s_v.inc(V.tensor_tensor(osc2[p][:, 0:D], um[:, k * D:(k + 1) * D],
                                    zq[:, k * D:(k + 1) * D], ALU.add))
            SY.wait_ge(s_v.h, s_v.n)
            s_st.inc(SY.dma_start(out_h[sidx * MINE + k * P:sidx * MINE + (k + 1) * P, :],
                                  osc2[p][:, 0:D]), 16)
            st_done_o[p] = s_st.n
            yield
        xg["o_st"] = list(st_done_o)
        xg["um_rd_v"] = s_v.n


    def gen_MIX(sidx):
        for _ in gen_A(sidx):
            yield
        for _ in gen_B(sidx):
            yield
        for _ in gen_C(sidx):
            yield

    def gen_DFull(sidx):
        for _ in gen_C2(sidx):
            yield
        for _ in gen_D_head(sidx):
            yield
        for _ in gen_D(sidx):
            yield

    def drive(mix, dside, ab_units=29):
        di = 0
        m = 0
        while True:
            rm, rd = (2, 2) if m < ab_units else (2, 3)
            for _ in range(rm):
                try:
                    next(mix)
                    m += 1
                except StopIteration:
                    return dside[di:]
            for _ in range(rd):
                if di < len(dside):
                    try:
                        next(dside[di])
                    except StopIteration:
                        di += 1

    dside = []
    for s in range(NSUP):
        if s >= 1:
            dside.append(gen_DFull(s - 1))
        dside = drive(gen_MIX(s), dside)
    dside.append(gen_DFull(NSUP - 1))
    for g in dside:
        for _ in g:
            pass
    barrier()

    SY.wait_ge(s_st.h, s_st.n)
    for cm in reversed(ctxs):
        cm.__exit__(None, None, None)
    return nc


def _prep_inputs(inputs, cfg):
    D, N = cfg["D"], cfg["N"]
    DBLK = D // P
    f = np.float32

    def arr(k):
        return np.asarray(inputs[k], f)

    gam = np.ascontiguousarray(arr("L0_attn_gamma").reshape(DBLK, P).T)
    bet = np.ascontiguousarray(arr("L0_attn_beta").reshape(DBLK, P).T)
    a_pp = np.ascontiguousarray(
        arr("L0_attn_A_log").reshape(DBLK, P, N).transpose(1, 0, 2).reshape(P, DBLK * N))
    return {
        "wdtT": np.ascontiguousarray(arr("L0_attn_W_dt").T),
        "wbT": np.ascontiguousarray(arr("L0_attn_W_B").T),
        "wcT": np.ascontiguousarray(arr("L0_attn_W_C").T),
        "gam_pp": gam, "bet_pp": bet, "a_pp": a_pp,
        "bdt_row": np.ascontiguousarray(arr("L0_attn_b_dt").reshape(1, D)),
        "dparam_row": np.ascontiguousarray(arr("L0_attn_D_param").reshape(1, D)),
        "wgT": np.ascontiguousarray(
            arr("L0_ffn_w_gate").T.reshape(D // P, P, -1, P)
            .transpose(2, 1, 0, 3).reshape(-1, P)),
        "wuT": np.ascontiguousarray(
            arr("L0_ffn_w_up").T.reshape(D // P, P, -1, P)
            .transpose(2, 1, 0, 3).reshape(-1, P)),
        "wdT": np.ascontiguousarray(
            arr("L0_ffn_w_down").T.reshape(-1, P, D // P, P)
            .transpose(2, 1, 0, 3).reshape(-1, P)),
    }


def kernel(**inputs):
    x = np.asarray(inputs["L0_attn_x"], np.float32)
    B, T, D = x.shape
    cfg = {"T": T, "D": D, "N": int(inputs["L0_attn_A_log"].shape[1]),
           "INNER": int(inputs["L0_ffn_w_gate"].shape[0]), "TCH": 512}
    nc = build_program(cfg)
    base = _prep_inputs(inputs, cfg)
    in_maps = []
    for c in range(8):
        b, j = c // 2, c % 2
        m = dict(base)
        m["x"] = np.ascontiguousarray(x[b])
        xb = x[b].reshape(T // P, P, D)
        m["xm"] = np.ascontiguousarray(xb[j::2].reshape(T // 2, D))
        m["parf"] = np.full((1, 1), j, np.float32)
        in_maps.append(m)
    res = run_bass_kernel_spmd(nc, in_maps, core_ids=list(range(8)))
    out = np.empty((B, T, D), np.float32)
    TCH = cfg["TCH"]
    MINE = TCH // 2
    NMT = MINE // P
    for c in range(8):
        b, j = c // 2, c % 2
        o = np.asarray(res.results[c]["out"])
        for s in range(T // TCH):
            for k in range(NMT):
                trow = s * TCH + (2 * k + j) * P
                out[b, trow:trow + P, :] = o[s * MINE + k * P:s * MINE + (k + 1) * P, :]
    return out
